# version 12
# baseline (speedup 1.0000x reference)
import numpy as np
import ml_dtypes
import concourse.bass as bass
import concourse.mybir as mybir
from concourse.bass_utils import run_bass_kernel_spmd
from contextlib import ExitStack

F32 = mybir.dt.float32
BF16 = mybir.dt.bfloat16
I32 = mybir.dt.int32
AF = mybir.ActivationFunctionType
AL = mybir.AluOpType
AX = mybir.AxisListType
bfdt = ml_dtypes.bfloat16

D = 4096
NT = 8
KC = 32
NCORE = 8
SM_SCALE = 1.0 / float(np.sqrt(128.0))

A1_ROWS = 64 * 3 * 512
A2_ROWS = 8 * 128

QT_GROUPS = [list(range(4 * i, 4 * i + 4)) for i in range(8)]

import os as _os
CUT = float(_os.environ.get("MK_CUT", "9"))
STOP_AFTER = ""

ENGINES = ("gpsimd", "sync", "vector", "scalar", "tensor")


class Ev:
    def __init__(self, st, nc, name):
        self.h = st.enter_context(nc.semaphore(name))
        self.n = 0


class Prog:
    def __init__(self, nc, st):
        self.nc = nc
        self.st = st
        self.streams = {e: [] for e in ENGINES}
        self.evs = {}

    def ev(self, name):
        if name not in self.evs:
            self.evs[name] = Ev(self.st, self.nc, name)
        return self.evs[name]

    def wait(self, eng, ev, val):
        if val > 0:
            self.streams[eng].append(("w", ev, val))

    def op(self, eng, fn, sig=None, amt=1):
        if sig is not None:
            sig.n += amt
        self.streams[eng].append(("o", fn, sig, amt))

    def raw(self, eng, fn):
        self.streams[eng].append(("o", fn, None, 0))

    def opw(self, eng, fn, sig=None):
        ev = sig if sig is not None else self.ev("self_" + eng)
        self.op(eng, fn, ev)
        self.wait(eng, ev, ev.n)

    FENCE_EXCLUDE = {"Ag1", "AgT0", "AgT1", "AgT2", "Ag2", "Gp10", "Gp11",
                     "Go", "Rld"}

    def fence(self, engines):
        for eng in engines:
            for name, ev in list(self.evs.items()):
                if ev.n > 0 and name not in self.FENCE_EXCLUDE:
                    self.streams[eng].append(("w", ev, ev.n))

    def emit(self, block):
        for ename in ENGINES:
            stream = self.streams[ename]
            if not stream:
                continue
            deco = getattr(block, ename)

            def make(stream):
                def body(eng):
                    for item in stream:
                        if item[0] == "w":
                            eng.wait_ge(item[1].h, item[2])
                        else:
                            ins = item[1](eng)
                            if item[2] is not None:
                                ins.then_inc(item[2].h, item[3])
                return body
            deco(make(stream))


def build():
    nc = bass.Bass("TRN2", target_bir_lowering=False)

    x_in = nc.dram_tensor("x_in", [1024, D], F32, kind="ExternalInput")
    wqT = nc.dram_tensor("wqT", [D, D], F32, kind="ExternalInput")
    wkT = nc.dram_tensor("wkT", [D, D], F32, kind="ExternalInput")
    wvT = nc.dram_tensor("wvT", [D, D], F32, kind="ExternalInput")
    woT = nc.dram_tensor("woT", [D, D], F32, kind="ExternalInput")
    qw_s = nc.dram_tensor("qw_s", [512, D], I32, kind="ExternalInput")
    qz_in = nc.dram_tensor("qz_in", [32, 512], I32, kind="ExternalInput")
    sc_in = nc.dram_tensor("sc_in", [32, D], F32, kind="ExternalInput")
    gi_in = nc.dram_tensor("gi_in", [32, 512], I32, kind="ExternalInput")
    b_bc = {n: nc.dram_tensor(n + "_bc", [128, D], F32, kind="ExternalInput")
            for n in ("bq", "bk", "bv", "bo", "bf")}
    mask01 = nc.dram_tensor("mask01", [128, 128], F32, kind="ExternalInput")
    ident_in = nc.dram_tensor("ident_in", [128, 128], F32, kind="ExternalInput")
    shifts_in = nc.dram_tensor("shifts_in", [128, 2048], I32, kind="ExternalInput")
    iota_in = nc.dram_tensor("iota_in", [32, 1], F32, kind="ExternalInput")
    y_out = nc.dram_tensor("y_out", [1024, D], F32, kind="ExternalOutput")

    wdq_in = nc.dram_tensor("wdq_in", [512, D], BF16)
    wdq_all = nc.dram_tensor("wdq_all", [D, D], BF16)
    aT_in = [nc.dram_tensor(f"a1_in{t}", [64 * 512, 128], BF16) for t in range(3)]
    aT_all = [nc.dram_tensor(f"a1_all{t}", [8 * 64 * 512, 128], BF16) for t in range(3)]
    a2_inA = nc.dram_tensor("a2_inA", [4096, 512], BF16)
    a2_inB = nc.dram_tensor("a2_inB", [4096, 512], BF16)
    a2_allA = nc.dram_tensor("a2_allA", [8 * 4096, 512], BF16)
    a2_allB = nc.dram_tensor("a2_allB", [8 * 4096, 512], BF16)
    rs_st = nc.dram_tensor("rs_st", [8 * 512, 1024], BF16)
    p1buf = nc.dram_tensor("p1buf", [8, 8 * 1536, 128], BF16)

    st = ExitStack()
    en = st.enter_context

    x_sb = en(nc.sbuf_tensor("x_sb", [128, NT * D], F32))
    mask_sb = en(nc.sbuf_tensor("mask_sb", [128, 128], BF16))
    id_sb = en(nc.sbuf_tensor("id_sb", [128, 128], BF16))
    sh_sb = None
    io_sb = en(nc.sbuf_tensor("io_sb", [32, 1], F32))
    stt_sb = en(nc.sbuf_tensor("stt_sb", [128, 16], F32))
    cst_scope = ExitStack()
    cst_f = cst_scope.enter_context(nc.sbuf_tensor("cst_f", [128, 128], F32))

    P = Prog(nc, st)
    E = P.ev

    def x_ap(tt, c0, n):
        return x_sb[:, tt * D + c0: tt * D + c0 + n]

    xT_holder = {}

    def xT_ap(k, t0, n):
        return xT_holder["t"][:, k * 1024 + t0: k * 1024 + t0 + n]

    s_x, s_c, s_cc = E("s_x"), E("s_c"), E("s_cc")
    for tt in range(NT):
        P.op("gpsimd", lambda g, tt=tt: g.dma_start(
            out=x_sb[:, tt * D:(tt + 1) * D],
            in_=x_in[tt * 128:(tt + 1) * 128, :]), s_x, 16)
    P.op("gpsimd", lambda g: g.dma_start(out=io_sb[:, :], in_=iota_in[:, :]), s_c, 16)
    P.op("gpsimd", lambda g: g.dma_start(out=cst_f[:, :], in_=mask01[:, :]), s_c, 16)
    P.wait("vector", s_c, s_c.n)
    P.op("vector", lambda v: v.tensor_copy(mask_sb[:, :], cst_f[:, :]), s_cc)
    s_ci = E("s_ci")
    P.wait("gpsimd", s_cc, s_cc.n)
    P.op("gpsimd", lambda g: g.dma_start(out=cst_f[:, :], in_=ident_in[:, :]), s_ci, 16)
    P.wait("vector", s_ci, s_ci.n)
    P.op("vector", lambda v: v.tensor_copy(id_sb[:, :], cst_f[:, :]), s_cc)

    cst_scope.close()

    dq = ExitStack()
    sh_sb = dq.enter_context(nc.sbuf_tensor("sh_sb", [128, 2048], I32))
    qwb = dq.enter_context(nc.sbuf_tensor("qwb", [128, 2048], I32))
    nibf = dq.enter_context(nc.sbuf_tensor("nibf", [128, 2048], F32))
    gi_i = dq.enter_context(nc.sbuf_tensor("gi_i", [32, 512], I32))
    gi_f = dq.enter_context(nc.sbuf_tensor("gi_f", [32, 512], F32))
    gt_b = dq.enter_context(nc.sbuf_tensor("gt_b", [32, 512], BF16))
    qz_i = dq.enter_context(nc.sbuf_tensor("qz_i", [32, 512], I32))
    z_i = dq.enter_context(nc.sbuf_tensor("z_i", [32, 2048], I32))
    z_b = dq.enter_context(nc.sbuf_tensor("z_b", [32, D], BF16))
    sc_f = dq.enter_context(nc.sbuf_tensor("sc_f", [32, 2048], F32))
    sc_b = dq.enter_context(nc.sbuf_tensor("sc_b", [32, D], BF16))
    wsub = dq.enter_context(nc.sbuf_tensor("wsub", [128, 2 * 2048], BF16))
    ps_s = dq.enter_context(nc.psum_tensor("ps_s", [128, 2048], F32))
    ps_z = dq.enter_context(nc.psum_tensor("ps_z", [128, 2048], F32))

    Lqz, Lsc, Lqw, Lgi, Vdq, Mdq, Ag1 = (E(n) for n in
        ("Lqz", "Lsc", "Lqw", "Lgi", "Vdq", "Mdq", "Ag1"))
    Wst = [E("Wst0"), E("Wst1")]

    P.wait("gpsimd", s_cc, s_cc.n)
    P.op("gpsimd", lambda g: g.dma_start(out=qz_i[:, :], in_=qz_in[:, :]), Lqz, 16)
    P.op("gpsimd", lambda g: g.dma_start(out=sh_sb[:, :], in_=shifts_in[:, :]), s_c, 16)
    P.op("gpsimd", lambda g: g.dma_start(out=gi_i[:, :], in_=gi_in[:, :]), Lgi, 16)

    P.wait("vector", Lgi, Lgi.n)
    P.wait("vector", Lqz, Lqz.n)
    P.wait("vector", s_c, s_c.n)
    P.opw("vector", lambda v: v.tensor_scalar(gi_f[:, :], gi_i[:, :], 0.0, 31.0,
                                              AL.max, AL.min), Vdq)
    P.op("vector", lambda v: v.tensor_scalar(gt_b[:, :], gi_f[:, :], io_sb[:, 0:1],
                                             None, AL.is_equal), Vdq)
    for half in range(2):
        if half > 0:
            P.wait("vector", Vdq, Vdq.n)
        for j in range(8):
            P.op("vector", lambda v, j=j, half=half: v.tensor_scalar(
                z_i[:, j:j + 2041:8],
                qz_i[:, half * 256:(half + 1) * 256], 4 * j, 15,
                AL.logical_shift_right, AL.bitwise_and), Vdq)
        P.wait("vector", Vdq, Vdq.n)
        P.op("vector", lambda v, half=half: v.tensor_scalar(
            z_b[:, half * 2048:(half + 1) * 2048], z_i[:, :], 1, None, AL.add), Vdq)
        if half > 0:
            P.wait("gpsimd", Vdq, Vdq.n)
        P.op("gpsimd", lambda g, half=half: g.dma_start(
            out=sc_f[:, :], in_=sc_in[:, half * 2048:(half + 1) * 2048]), Lsc, 16)
        P.wait("vector", Lsc, Lsc.n)
        P.op("vector", lambda v, half=half: v.tensor_copy(
            sc_b[:, half * 2048:(half + 1) * 2048], sc_f[:, :]), Vdq)
    vprep = Vdq.n

    nib_done = {}
    comb_done = {}
    for t in range(4):
        for half in range(2):
            i = t * 2 + half
            par = i % 2
            if i > 0:
                P.wait("gpsimd", Vdq, nib_done[i - 1])
            P.op("gpsimd", lambda g, t=t, half=half: g.dma_start(
                out=qwb[:, :],
                in_=qw_s[128 * t:128 * (t + 1), half * 2048:(half + 1) * 2048]),
                Lqw, 16)
            P.wait("vector", Lqw, Lqw.n)
            if t == 0 and half == 0:
                P.wait("vector", s_c, s_c.n)
            P.opw("vector", lambda v: v.tensor_tensor(
                qwb[:, :], qwb[:, :], sh_sb[:, :], AL.logical_shift_right))
            P.opw("vector", lambda v: v.tensor_scalar(
                qwb[:, :], qwb[:, :], 15, None, AL.bitwise_and))
            P.opw("vector", lambda v: v.tensor_copy(nibf[:, :], qwb[:, :]), Vdq)
            nib_done[i] = Vdq.n
            P.wait("tensor", Vdq, vprep)
            if i > 0:
                P.wait("tensor", Vdq, comb_done[i - 1])
            P.wait("tensor", Vdq, nib_done[i])
            for oc in range(4):
                P.op("tensor", lambda tn, t=t, half=half, oc=oc: tn.matmul(
                    ps_s[:, oc * 512:(oc + 1) * 512], gt_b[:, t * 128:(t + 1) * 128],
                    sc_b[:, half * 2048 + oc * 512:half * 2048 + (oc + 1) * 512],
                    start=True, stop=True))
            for oc in range(4):
                P.op("tensor", lambda tn, t=t, half=half, oc=oc: tn.matmul(
                    ps_z[:, oc * 512:(oc + 1) * 512], gt_b[:, t * 128:(t + 1) * 128],
                    z_b[:, half * 2048 + oc * 512:half * 2048 + (oc + 1) * 512],
                    start=True, stop=True), Mdq if oc == 3 else None)
            P.wait("vector", Mdq, Mdq.n)
            if i >= 2:
                P.wait("vector", Wst[par], Wst[par].n)
            P.opw("vector", lambda v, par=par: v.tensor_sub(
                wsub[:, par * 2048:(par + 1) * 2048], nibf[:, :], ps_z[:, :]))
            P.op("vector", lambda v, par=par: v.tensor_mul(
                wsub[:, par * 2048:(par + 1) * 2048],
                wsub[:, par * 2048:(par + 1) * 2048], ps_s[:, :]), Vdq)
            comb_done[i] = Vdq.n
            P.wait("scalar", Vdq, comb_done[i])
            P.op("scalar", lambda s, t=t, half=half, par=par: s.dma_start(
                out=wdq_in[t * 128:(t + 1) * 128, half * 2048:(half + 1) * 2048],
                in_=wsub[:, par * 2048:(par + 1) * 2048]), Wst[par], 16)

    P.wait("gpsimd", Wst[0], Wst[0].n)
    P.wait("gpsimd", Wst[1], Wst[1].n)
    P.op("gpsimd", lambda g: g.collective_compute(
        "AllGather", AL.bypass, replica_groups=[list(range(NCORE))],
        ins=[wdq_in[:, :]], outs=[wdq_all[:, :]]), Ag1)
    dq.close()
    xT_holder["t"] = en(nc.sbuf_tensor("xT_sb", [128, KC * 1024], BF16))

    Vln, Sln, Tln, Eln = E("Vln"), E("Sln"), E("Tln"), E("Eln")

    ln_ctr = [0]

    def emit_ln():
        P.fence(["vector", "scalar", "tensor"])
        u = ln_ctr[0]
        ln_ctr[0] += 1
        ln = ExitStack()
        xlnb = ln.enter_context(nc.sbuf_tensor(f"xlnb{u}", [128, D], BF16))
        bnst = ln.enter_context(nc.sbuf_tensor(f"bnst{u}", [128, 2 * 64], F32))
        pt = [ln.enter_context(nc.psum_tensor(f"ptln{u}_{i}", [128, 128], BF16))
              for i in range(6)]
        t_base = Tln.n
        e_base = Eln.n
        for tt in range(NT):
            sp = tt % 2
            c0 = sp * 8
            for i in range(7):
                P.op("vector", lambda v, tt=tt, i=i, sp=sp: v.bn_stats(
                    bnst[:, sp * 64 + i * 6:sp * 64 + (i + 1) * 6],
                    x_ap(tt, i * 512, 512)))
            P.opw("vector", lambda v, tt=tt, sp=sp: v.bn_stats(
                bnst[:, sp * 64 + 42:sp * 64 + 48], x_ap(tt, 7 * 512, 512)))
            P.opw("vector", lambda v, sp=sp, c0=c0: v.bn_aggr(
                stt_sb[:, c0:c0 + 2], bnst[:, sp * 64:sp * 64 + 48]))
            P.op("vector", lambda v, c0=c0: v.tensor_scalar(
                stt_sb[:, c0 + 1:c0 + 2], stt_sb[:, c0 + 1:c0 + 2], 1e-5, None,
                AL.add), Vln)
            P.wait("scalar", Vln, Vln.n)
            P.op("scalar", lambda s, c0=c0: s.activation(
                stt_sb[:, c0 + 2:c0 + 3], stt_sb[:, c0 + 1:c0 + 2], AF.Sqrt,
                0.0, 1.0), Sln)
            P.wait("vector", Sln, Sln.n)
            P.opw("vector", lambda v, c0=c0: v.reciprocal(
                stt_sb[:, c0 + 3:c0 + 4], stt_sb[:, c0 + 2:c0 + 3]))
            P.op("vector", lambda v, c0=c0: v.scalar_tensor_tensor(
                stt_sb[:, c0 + 4:c0 + 5], stt_sb[:, c0:c0 + 1], -1.0,
                stt_sb[:, c0 + 3:c0 + 4], AL.mult, AL.mult), Vln)
            P.wait("scalar", Vln, Vln.n)
            if tt >= 1:
                P.wait("scalar", Tln, t_base + tt * KC)
            P.op("scalar", lambda s, tt=tt, c0=c0: s.activation(
                xlnb[:, :], x_ap(tt, 0, D), AF.Identity,
                stt_sb[:, c0 + 4:c0 + 5], stt_sb[:, c0 + 3:c0 + 4]), Sln)
            P.wait("tensor", Sln, Sln.n)
            for k in range(KC):
                li = tt * KC + k
                if li >= 6:
                    P.wait("tensor", Eln, e_base + li - 5)
                pp = li % 6
                P.op("tensor", lambda tn, k=k, pp=pp: tn.transpose(
                    pt[pp][:, :], xlnb[:, k * 128:(k + 1) * 128],
                    id_sb[:, :]), Tln)
                P.wait("vector", Tln, Tln.n)
                P.op("vector", lambda v, k=k, tt=tt, pp=pp: v.tensor_copy(
                    xT_ap(k, tt * 128, 128), pt[pp][:, :]), Eln)
        ln.close()

    Pd = [E("Pd0"), E("Pd1")]
    Pc, Pmm, Pev = E("Pc"), E("Pmm"), E("Pev")
    Bld = [E("Bld0"), E("Bld1")]
    Sst = [E("Sst0"), E("Sst1")]

    prj_ctr = [0]

    def emit_proj(wsrc, w_is_f32, cast_eng, bias_src, sink, scale, stat_ap, sink_t=None,
                  extra_entry=()):
        P.fence(["sync", "gpsimd", "tensor", "vector", "scalar"])
        u = prj_ctr[0]
        prj_ctr[0] += 1
        a = ExitStack()
        if w_is_f32:
            wf = a.enter_context(nc.sbuf_tensor(f"wf{u}", [128, 2 * 512], F32))
        wb = a.enter_context(nc.sbuf_tensor(f"wb{u}", [128, 2 * 512], BF16))
        bias = a.enter_context(nc.sbuf_tensor(f"bias{u}", [128, 2 * 512], F32))
        tmp = a.enter_context(nc.sbuf_tensor(f"tmp{u}", [128, 512], F32))
        vst = a.enter_context(nc.sbuf_tensor(f"vst{u}", [128, 2 * 512], BF16))
        ps = [a.enter_context(nc.psum_tensor(f"pp{u}_{i}", [128, 512], F32))
              for i in range(NT)]
        m_base = Pmm.n
        e_base = Pev.n
        b_base = [Bld[0].n, Bld[1].n]
        for (eng, ev, val) in extra_entry:
            P.wait(eng, ev, val)
        for og in range(8):
            bpar = og % 2
            if og >= 2:
                P.wait("gpsimd", Pev, e_base + (og - 1) * NT)
            P.op("gpsimd", lambda g, og=og, bpar=bpar: g.dma_start(
                out=bias[:, bpar * 512:(bpar + 1) * 512],
                in_=bias_src[:, og * 512:(og + 1) * 512]), Bld[bpar], 16)
            for k in range(KC):
                n = og * KC + k
                par = n % 2
                if n >= 2:
                    P.wait("sync", Pc if w_is_f32 else Pmm,
                           (Pc.n - 1) if w_is_f32 else (m_base + n - 1))
                P.op("sync", lambda q, og=og, k=k, par=par, w=wsrc: q.dma_start(
                    out=(wf if w_is_f32 else wb)[:, par * 512:(par + 1) * 512],
                    in_=w[k * 128:(k + 1) * 128, og * 512:(og + 1) * 512]), Pd[par], 16)
                if w_is_f32:
                    P.wait(cast_eng, Pd[par], Pd[par].n)
                    if n >= 2:
                        P.wait(cast_eng, Pmm, m_base + n - 1)
                    if cast_eng == "scalar":
                        P.op(cast_eng, lambda e, par=par: e.copy(
                            wb[:, par * 512:(par + 1) * 512],
                            wf[:, par * 512:(par + 1) * 512]), Pc)
                    else:
                        P.op(cast_eng, lambda e, par=par: e.tensor_copy(
                            wb[:, par * 512:(par + 1) * 512],
                            wf[:, par * 512:(par + 1) * 512]), Pc)
                    P.wait("tensor", Pc, Pc.n)
                else:
                    P.wait("tensor", Pd[par], Pd[par].n)
                for tt in range(NT):
                    if og >= 1 and k == 0:
                        P.wait("tensor", Pev, e_base + (og - 1) * NT + tt + 1)
                    P.op("tensor", lambda tn, tt=tt, k=k, par=par: tn.matmul(
                        ps[tt][:, :], stat_ap(k, tt * 128, 128),
                        wb[:, par * 512:(par + 1) * 512],
                        start=(k == 0), stop=(k == KC - 1)),
                        Pmm if tt == NT - 1 else None)
            P.wait("vector", Pmm, m_base + (og + 1) * KC)
            P.wait("vector", Bld[bpar], b_base[bpar] + (og // 2 + 1) * 16)
            for tt in range(NT):
                mtile = og * NT + tt
                if sink == "resid":
                    P.opw("vector", lambda v, tt=tt, bpar=bpar: v.tensor_add(
                        tmp[:, :], ps[tt][:, :], bias[:, bpar * 512:(bpar + 1) * 512]))
                    P.opw("vector", lambda v, tt=tt, og=og, scale=scale:
                          v.scalar_tensor_tensor(
                              x_ap(tt, og * 512, 512), tmp[:, :], scale,
                              x_ap(tt, og * 512, 512), AL.mult, AL.add), Pev)
                else:
                    par2 = mtile % 2
                    if mtile >= 2:
                        P.wait("vector", Sst[par2], Sst[par2].n)
                    P.op("vector", lambda v, tt=tt, bpar=bpar, par2=par2:
                         v.scalar_tensor_tensor(
                             vst[:, par2 * 512:(par2 + 1) * 512], ps[tt][:, :], 1.0,
                             bias[:, bpar * 512:(bpar + 1) * 512], AL.bypass, AL.add),
                         Pev)
                    P.wait("scalar", Pev, Pev.n)
                    b = tt // 4
                    for cb in range(4):
                        h = og * 4 + cb
                        gg = b * 32 + h
                        r0 = gg * 512 + (tt % 4) * 128
                        P.op("scalar", lambda s, r0=r0, par2=par2, cb=cb,
                             st_=sink_t: s.dma_start(
                            out=aT_in[st_][r0:r0 + 128, :],
                            in_=vst[:, par2 * 512 + cb * 128:par2 * 512 + (cb + 1) * 128]),
                            Sst[par2], 16)
        a.close()
        return dict(mm=Pmm.n, ev=Pev.n)

    P.wait("vector", s_x, s_x.n)
    emit_ln()
    ffn1 = emit_proj(wdq_all, False, None, b_bc["bf"], "resid", 0.5, xT_ap,
                     extra_entry=[("sync", Ag1, 1)])

    if STOP_AFTER == "ffn1":
        _finish(P, x_ap, y_out, ffn1["ev"])
        P.emit(en(nc.Block()))
        return nc

    emit_ln()
    AgT = [E(f"AgT{t}") for t in range(3)]

    def emit_a1_ag(t):
        P.wait("gpsimd", Sst[0], Sst[0].n)
        P.wait("gpsimd", Sst[1], Sst[1].n)
        P.op("gpsimd", lambda g, t=t: g.collective_compute(
            "AllGather", AL.bypass, replica_groups=[list(range(NCORE))],
            ins=[aT_in[t][:, :]], outs=[aT_all[t][:, :]]), AgT[t])

    emit_proj(wkT, True, "vector", b_bc["bk"], "a2a", 1.0, xT_ap, 0)
    emit_a1_ag(0)
    emit_proj(wqT, True, "vector", b_bc["bq"], "a2a", 1.0, xT_ap, 1)
    emit_a1_ag(1)
    prj_v = emit_proj(wvT, True, "scalar", b_bc["bv"], "a2a", 1.0, xT_ap, 2)
    emit_a1_ag(2)
    if STOP_AFTER == "qkv":
        _finish(P, x_ap, y_out, prj_v["ev"])
        P.emit(en(nc.Block()))
        return nc

    P.fence(ENGINES)
    at = ExitStack()
    xTt = xT_holder["t"]
    HB = 16384

    def K_ap(par, c0, n):
        return xTt[:, par * HB + c0: par * HB + c0 + n]

    def Q_ap(par, c0, n):
        return xTt[:, par * HB + 4096 + c0: par * HB + 4096 + c0 + n]

    def V_ap(par, c0, n):
        return xTt[:, par * HB + 8192 + c0: par * HB + 8192 + c0 + n]

    attT = at.enter_context(nc.sbuf_tensor("attT", [128, 3 * 512], BF16))
    rstb = at.enter_context(nc.sbuf_tensor("rstb", [128, 4096], BF16))
    ps_sc = [at.enter_context(nc.psum_tensor(f"ps_sc{i}", [128, 512], F32))
             for i in range(4)]
    ps_R = [at.enter_context(nc.psum_tensor(f"ps_R{i}", [128, 512], F32))
            for i in range(4)]

    Vini = E("Vini")
    Lks = [E("Lks0"), E("Lks1")]
    Lqs = [E("Lqs0"), E("Lqs1")]
    Rtd = [E("Rtd0"), E("Rtd1")]
    Gp1 = [E("Gp10"), E("Gp11")]
    Amm, Aex, Amk, Acv, Arv, Trv, Erv = (
        E(n) for n in ("Amm", "Aex", "Amk", "Acv", "Arv", "Trv", "Erv"))
    A2st = [E("A2st0"), E("A2st1")]
    Ag2, Rld = E("Ag2"), E("Rld")

    P.op("vector", lambda v: v.memset(V_ap(0, 0, 4128), 1.0), Vini)
    P.op("vector", lambda v: v.memset(V_ap(1, 0, 4128), 1.0), Vini)

    aTv = [aT_all[t].rearrange("(i r) c -> i r c", i=8) for t in range(3)]
    p1v = p1buf.rearrange("p (i r) c -> p i r c", i=8)

    def emit_all_gathers(g):
        for arm in nc.switch_core_id(engines=[g], n=128):
            c = arm.logical % NCORE
            for p in range(8):
                gg = 8 * c + p
                for t in range(3):
                    g.dma_start(out=p1v[p, :, t * 512:(t + 1) * 512, :],
                                in_=aTv[t][:, gg * 512:(gg + 1) * 512, :]
                                ).then_inc(Gp1[0].h, 16)

    P.wait("sync", AgT[0], 1)
    P.wait("sync", AgT[1], 1)
    P.wait("sync", AgT[2], 1)
    P.raw("sync", emit_all_gathers)
    Gp1[0].n += 24 * 16

    mm_end = {}
    lks_after = {}
    lqs_after = {}

    OLDLOAD = bool(_os.environ.get("MK_OLDLOAD"))

    def issue_pair_loads(p):
        par = p % 2
        for q in ("sync", "scalar"):
            P.wait(q, Gp1[0], Gp1[0].n)
            P.wait(q, Vini, 2)
            if p >= 2:
                P.wait(q, Acv, mm_end[p - 2])
        for a_ in range(32):
            r0 = (a_ // 4) * 1536 + (a_ % 4) * 128
            P.op("sync", lambda s, r0=r0, a_=a_, par=par, p=p: s.dma_start(
                out=K_ap(par, a_ * 128, 128), in_=p1buf[p, r0:r0 + 128, :],
                transpose=True), Lks[par], 16)
        for a_ in range(32):
            r0 = (a_ // 4) * 1536 + 512 + (a_ % 4) * 128
            P.op("sync", lambda s, r0=r0, a_=a_, par=par, p=p: s.dma_start(
                out=Q_ap(par, a_ * 128, 128), in_=p1buf[p, r0:r0 + 128, :],
                transpose=True), Lks[par], 16)
        for a_ in range(32):
            r0 = (a_ // 4) * 1536 + 1024 + (a_ % 4) * 128
            P.op("scalar", lambda s, r0=r0, a_=a_, p=p, par=par: s.dma_start(
                out=V_ap(par, a_ * 129, 128), in_=p1buf[p, r0:r0 + 128, :]),
                Lqs[par], 16)
        lks_after[p] = Lks[par].n
        lqs_after[p] = Lqs[par].n

    NOPRE = bool(_os.environ.get("MK_NOPRE"))
    g_ch = [0]
    issue_pair_loads(0)
    if not NOPRE:
        issue_pair_loads(1)

    for p in range(8):
        par = p % 2
        if NOPRE and p > 0:
            issue_pair_loads(p)
        P.wait("tensor", Lks[par], lks_after[p])
        P.wait("tensor", Lqs[par], lqs_after[p])
        for G in QT_GROUPS:
            q0, q1 = G[0], G[-1]
            rv_pass = Arv.n
            chunks = []
            for kt in range(0, q1 + 1):
                vis0 = max(kt, q0)
                nch = (q1 - vis0 + 1 + 3) // 4
                for c in range(nch):
                    cq0 = vis0 + 4 * c
                    ncols = min(4, q1 - cq0 + 1) * 128
                    chunks.append((kt, c, cq0, ncols, q0 <= kt <= q1 and c == 0))

            def emit_scores(desc):
                kt, c, cq0, ncols, has_diag = desc
                ch = g_ch[0]
                g_ch[0] += 1
                spar = ch % 4
                apar_t = ch % 3
                if ch >= 4:
                    P.wait("tensor", Aex, ch - 3)
                if kt == 0 and c == 0:
                    P.wait("tensor", Arv, rv_pass)
                P.op("tensor", lambda tn, kt=kt, cq0=cq0, ncols=ncols, spar=spar,
                     par=par: tn.matmul(
                         ps_sc[spar][:, 0:ncols], K_ap(par, kt * 128, 128),
                         Q_ap(par, cq0 * 128, ncols),
                         start=True, stop=True), Amm)
                P.wait("scalar", Amm, Amm.n)
                if ch >= 3:
                    P.wait("scalar", Acv, ch - 2)
                P.op("scalar", lambda s, spar=spar, apar_t=apar_t, ncols=ncols:
                     s.activation(attT[:, apar_t * 512:apar_t * 512 + ncols],
                                  ps_sc[spar][:, 0:ncols], AF.Exp, 0.0, SM_SCALE),
                     Aex)
                if has_diag:
                    P.wait("vector", Aex, Aex.n)
                    P.op("vector", lambda v, apar_t=apar_t: v.tensor_mul(
                        attT[:, apar_t * 512:apar_t * 512 + 128],
                        attT[:, apar_t * 512:apar_t * 512 + 128], mask_sb[:, :]), Amk)
                return (ch, Aex.n, Amk.n)

            def emit_attv(desc, toks):
                kt, c, cq0, ncols, has_diag = desc
                ch, aex_n, amk_n = toks
                apar_t = ch % 3
                if has_diag:
                    P.wait("tensor", Amk, amk_n)
                else:
                    P.wait("tensor", Aex, aex_n)
                nq = ncols // 128
                for qi in range(nq):
                    qt = cq0 + qi
                    slot = qt - q0
                    P.op("tensor", lambda tn, qt=qt, qi=qi, kt=kt, apar_t=apar_t,
                         slot=slot, par=par: tn.matmul(
                             ps_R[slot][:, 0:129],
                             attT[:, apar_t * 512 + qi * 128:apar_t * 512 + (qi + 1) * 128],
                             V_ap(par, kt * 129, 129),
                             start=(kt == 0), stop=(kt == qt)),
                         Acv if qi == nq - 1 else None)
                if has_diag:
                    qt = kt
                    slot = qt - q0
                    P.wait("vector", Acv, Acv.n)
                    P.wait("vector", Arv, Arv.n)
                    if qt == 0 and p > 0:
                        P.wait("vector", Rtd[(p - 1) % 2], Rtd[(p - 1) % 2].n)
                    P.opw("vector", lambda v, slot=slot: v.reciprocal(
                        stt_sb[:, 8:9], ps_R[slot][:, 128:129]))
                    P.op("vector", lambda v, slot=slot, qt=qt: v.tensor_scalar(
                        rstb[:, qt * 128:(qt + 1) * 128], ps_R[slot][:, 0:128],
                        stt_sb[:, 8:9], None, AL.mult), Arv)

            SHIFT = 2
            toks = {}
            for i, desc in enumerate(chunks):
                toks[i] = emit_scores(desc)
                if i >= SHIFT:
                    emit_attv(chunks[i - SHIFT], toks[i - SHIFT])
            for i in range(max(0, len(chunks) - SHIFT), len(chunks)):
                emit_attv(chunks[i], toks[i])
        mm_end[p] = Acv.n
        if not NOPRE and p + 2 < 8:
            issue_pair_loads(p + 2)
        apar = p % 2
        P.wait("scalar", Arv, Arv.n)
        a2t = a2_inA if p < 4 else a2_inB
        pc = (p % 4) * 128
        for qt in range(32):
            P.op("scalar", lambda s, qt=qt, a2t=a2t, pc=pc: s.dma_start(
                out=a2t[qt * 128:(qt + 1) * 128, pc:pc + 128],
                in_=rstb[:, qt * 128:(qt + 1) * 128]), Rtd[apar], 16)
        if p == 3:
            P.wait("gpsimd", Rtd[0], Rtd[0].n)
            P.wait("gpsimd", Rtd[1], Rtd[1].n)
            P.op("gpsimd", lambda g: g.collective_compute(
                "AllGather", AL.bypass, replica_groups=[list(range(NCORE))],
                ins=[a2_inA[:, :]], outs=[a2_allA[:, :]]), Ag2)
        if STOP_AFTER == "pair0":
            break
    at.close()
    if STOP_AFTER in ("attn", "pair0"):
        _finish(P, x_ap, y_out, ffn1["ev"])
        P.emit(en(nc.Block()))
        return nc

    P.wait("gpsimd", Rtd[0], Rtd[0].n)
    P.wait("gpsimd", Rtd[1], Rtd[1].n)
    P.op("gpsimd", lambda g: g.collective_compute(
        "AllGather", AL.bypass, replica_groups=[list(range(NCORE))],
        ins=[a2_inB[:, :]], outs=[a2_allB[:, :]]), Ag2)
    P.wait("gpsimd", Ag2, Ag2.n)
    P.wait("gpsimd", Pmm, prj_v["mm"])
    Go = E("Go")

    def emit_o_gather(g):
        for arm in nc.switch_core_id(engines=[g], n=128):
            c = arm.logical % NCORE
            for i in range(8):
                g.dma_start(out=rs_st[i * 512:(i + 1) * 512, 0:512],
                            in_=a2_allA[i * 4096 + 512 * c:i * 4096 + 512 * (c + 1), :]
                            ).then_inc(Go.h, 16)
                g.dma_start(out=rs_st[i * 512:(i + 1) * 512, 512:1024],
                            in_=a2_allB[i * 4096 + 512 * c:i * 4096 + 512 * (c + 1), :]
                            ).then_inc(Go.h, 16)
    P.wait("sync", Ag2, 2)
    P.raw("sync", emit_o_gather)
    Go.n += 16 * 16
    rb = ExitStack()
    rb_st = rb.enter_context(nc.sbuf_tensor("rb_st", [128, 4 * 128], BF16))
    rb_pt = [rb.enter_context(nc.psum_tensor(f"rb_pt{i}", [128, 128], BF16))
             for i in range(4)]
    Rli = [E(f"Rli{i}") for i in range(4)]
    Rtt, Rle = E("Rtt"), E("Rle")
    P.wait("sync", Go, Go.n)
    P.fence(["sync", "tensor", "vector"])
    n2 = [0]
    for b in range(2):
        for h in range(32):
            gg = b * 32 + h
            i_src, pp_ = gg // 8, gg % 8
            c0 = (pp_ % 4) * 128 + (0 if pp_ < 4 else 512)
            for j in range(4):
                n = n2[0]
                n2[0] += 1
                sl = n % 4
                if n >= 4:
                    P.wait("sync", Rtt, n - 3)
                P.op("sync", lambda s, i_src=i_src, j=j, c0=c0, sl=sl: s.dma_start(
                    out=rb_st[:, sl * 128:(sl + 1) * 128],
                    in_=rs_st[i_src * 512 + j * 128:i_src * 512 + (j + 1) * 128,
                              c0:c0 + 128]), Rli[sl], 16)
                P.wait("tensor", Rli[sl], Rli[sl].n)
                if n >= 4:
                    P.wait("tensor", Rle, n - 3)
                P.op("tensor", lambda tn, sl=sl: tn.transpose(
                    rb_pt[sl][:, :], rb_st[:, sl * 128:(sl + 1) * 128],
                    id_sb[:, :]), Rtt)
                P.wait("vector", Rtt, Rtt.n)
                P.op("vector", lambda v, h=h, b=b, j=j, sl=sl: v.tensor_copy(
                    xT_holder["t"][:, h * 1024 + b * 512 + j * 128:
                                   h * 1024 + b * 512 + (j + 1) * 128],
                    rb_pt[sl][:, :]), Rle)
    rb.close()

    oprj = emit_proj(woT, True, "vector", b_bc["bo"], "resid", 1.0, xT_ap,
                     extra_entry=[("tensor", Rle, Rle.n), ("vector", Rle, Rle.n),
                                  ("sync", Rle, Rle.n)])

    if STOP_AFTER == "oproj":
        _finish(P, x_ap, y_out, oprj["ev"])
        P.emit(en(nc.Block()))
        return nc

    emit_ln()
    ffn2 = emit_proj(wdq_all, False, None, b_bc["bf"], "resid", 0.5, xT_ap)
    _finish(P, x_ap, y_out, ffn2["ev"])
    P.emit(en(nc.Block()))
    return nc


def _finish(P, x_ap, y_out, ev_val):
    Sout = P.ev("Sout")
    Pev = P.ev("Pev")
    P.wait("scalar", Pev, ev_val)
    for tt in range(NT):
        P.op("scalar", lambda s, tt=tt: s.dma_start(
            out=y_out[tt * 128:(tt + 1) * 128, :], in_=x_ap(tt, 0, D)), Sout, 16)
    P.wait("scalar", Sout, Sout.n)


def kernel(L0_attn_x, L0_attn_Wq, L0_attn_bq, L0_attn_Wk, L0_attn_bk, L0_attn_Wv,
           L0_attn_bv, L0_attn_Wo, L0_attn_bo, L0_ffn_qweight, L0_ffn_qzeros,
           L0_ffn_scales, L0_ffn_g_idx, L0_ffn_bias):
    x = np.asarray(L0_attn_x, dtype=np.float32)
    cm = np.ascontiguousarray
    wq = cm(np.asarray(L0_attn_Wq, np.float32).T)
    wk = cm(np.asarray(L0_attn_Wk, np.float32).T)
    wv = cm(np.asarray(L0_attn_Wv, np.float32).T)
    wo = cm(np.asarray(L0_attn_Wo, np.float32).T)
    qw = np.asarray(L0_ffn_qweight, np.int32)
    qz = np.asarray(L0_ffn_qzeros, np.int32)
    sc = np.asarray(L0_ffn_scales, np.float32)
    gi = np.asarray(L0_ffn_g_idx, np.int32)
    bias_bc = {}
    for n, b in (("bq", L0_attn_bq), ("bk", L0_attn_bk), ("bv", L0_attn_bv),
                 ("bo", L0_attn_bo), ("bf", L0_ffn_bias)):
        bias_bc[n] = cm(np.broadcast_to(np.asarray(b, np.float32)[None, :], (128, D)))
    r = np.arange(128)
    mask01 = (r[:, None] <= r[None, :]).astype(np.float32)
    ident = np.eye(128, dtype=np.float32)
    shifts = cm(np.broadcast_to((4 * (r % 8)).astype(np.int32)[:, None], (128, 2048)))
    iota = np.arange(32, dtype=np.float32).reshape(32, 1)

    nc = build()
    in_maps = []
    for c in range(NCORE):
        xs = np.concatenate([x[0, 512 * c:512 * (c + 1), :],
                             x[1, 512 * c:512 * (c + 1), :]], axis=0)
        m = dict(x_in=cm(xs), wqT=wq, wkT=wk, wvT=wv, woT=wo,
                 qw_s=cm(np.repeat(qw[64 * c:64 * (c + 1), :], 8, axis=0)),
                 qz_in=qz, sc_in=sc,
                 gi_in=cm(np.broadcast_to(gi[512 * c:512 * (c + 1)][None, :],
                                          (32, 512))),
                 mask01=mask01, ident_in=ident, shifts_in=shifts, iota_in=iota)
        for n in bias_bc:
            m[n + "_bc"] = bias_bc[n]
        in_maps.append(m)
    res = run_bass_kernel_spmd(nc, in_maps, core_ids=list(range(NCORE)))
    out = np.empty((2, 4096, D), dtype=np.float32)
    for c in range(NCORE):
        yc = res.results[c]["y_out"]
        out[0, 512 * c:512 * (c + 1), :] = yc[0:512, :]
        out[1, 512 * c:512 * (c + 1), :] = yc[512:1024, :]
    return out


if __name__ == "__main__":
    import reference as R
    ins = R.setup_inputs(0)
    out = kernel(**{k: np.asarray(v) for k, v in ins.items()})
    ref = np.asarray(R.reference(**ins))
    num = float(((out - ref) ** 2).mean())
    den = float((ref ** 2).mean())
    print("resid var ratio:", num / den)


# version 13
# speedup vs baseline: 1.0002x; 1.0002x over previous
import numpy as np
import ml_dtypes
import concourse.bass as bass
import concourse.mybir as mybir
from concourse.bass_utils import run_bass_kernel_spmd
from contextlib import ExitStack

F32 = mybir.dt.float32
BF16 = mybir.dt.bfloat16
I32 = mybir.dt.int32
AF = mybir.ActivationFunctionType
AL = mybir.AluOpType
AX = mybir.AxisListType
bfdt = ml_dtypes.bfloat16

D = 4096
NT = 8
KC = 32
NCORE = 8
SM_SCALE = 1.0 / float(np.sqrt(128.0))

A1_ROWS = 64 * 3 * 512
A2_ROWS = 8 * 128

QT_GROUPS = [list(range(4 * i, 4 * i + 4)) for i in range(8)]

import os as _os
CUT = float(_os.environ.get("MK_CUT", "9"))
STOP_AFTER = ""

ENGINES = ("gpsimd", "sync", "vector", "scalar", "tensor")


class Ev:
    def __init__(self, st, nc, name):
        self.h = st.enter_context(nc.semaphore(name))
        self.n = 0


class Prog:
    def __init__(self, nc, st):
        self.nc = nc
        self.st = st
        self.streams = {e: [] for e in ENGINES}
        self.evs = {}

    def ev(self, name):
        if name not in self.evs:
            self.evs[name] = Ev(self.st, self.nc, name)
        return self.evs[name]

    def wait(self, eng, ev, val):
        if val > 0:
            self.streams[eng].append(("w", ev, val))

    def op(self, eng, fn, sig=None, amt=1):
        if sig is not None:
            sig.n += amt
        self.streams[eng].append(("o", fn, sig, amt))

    def raw(self, eng, fn):
        self.streams[eng].append(("o", fn, None, 0))

    def opw(self, eng, fn, sig=None):
        ev = sig if sig is not None else self.ev("self_" + eng)
        self.op(eng, fn, ev)
        self.wait(eng, ev, ev.n)

    FENCE_EXCLUDE = {"Ag1", "AgT0", "AgT1", "AgT2", "Ag2", "Gp10", "Gp11",
                     "Go", "Rld"}

    def fence(self, engines):
        for eng in engines:
            for name, ev in list(self.evs.items()):
                if ev.n > 0 and name not in self.FENCE_EXCLUDE:
                    self.streams[eng].append(("w", ev, ev.n))

    def emit(self, block):
        for ename in ENGINES:
            stream = self.streams[ename]
            if not stream:
                continue
            deco = getattr(block, ename)

            def make(stream):
                def body(eng):
                    for item in stream:
                        if item[0] == "w":
                            eng.wait_ge(item[1].h, item[2])
                        else:
                            ins = item[1](eng)
                            if item[2] is not None:
                                ins.then_inc(item[2].h, item[3])
                return body
            deco(make(stream))


def build():
    nc = bass.Bass("TRN2", target_bir_lowering=False)

    x_in = nc.dram_tensor("x_in", [1024, D], F32, kind="ExternalInput")
    wqT = nc.dram_tensor("wqT", [D, D], F32, kind="ExternalInput")
    wkT = nc.dram_tensor("wkT", [D, D], F32, kind="ExternalInput")
    wvT = nc.dram_tensor("wvT", [D, D], F32, kind="ExternalInput")
    woT = nc.dram_tensor("woT", [D, D], F32, kind="ExternalInput")
    qw_s = nc.dram_tensor("qw_s", [512, D], I32, kind="ExternalInput")
    qz_in = nc.dram_tensor("qz_in", [32, 512], I32, kind="ExternalInput")
    sc_in = nc.dram_tensor("sc_in", [32, D], F32, kind="ExternalInput")
    gi_in = nc.dram_tensor("gi_in", [32, 512], I32, kind="ExternalInput")
    b_bc = {n: nc.dram_tensor(n + "_bc", [128, D], F32, kind="ExternalInput")
            for n in ("bq", "bk", "bv", "bo", "bf")}
    mask01 = nc.dram_tensor("mask01", [128, 128], F32, kind="ExternalInput")
    ident_in = nc.dram_tensor("ident_in", [128, 128], F32, kind="ExternalInput")
    shifts_in = nc.dram_tensor("shifts_in", [128, 2048], I32, kind="ExternalInput")
    iota_in = nc.dram_tensor("iota_in", [32, 1], F32, kind="ExternalInput")
    y_out = nc.dram_tensor("y_out", [1024, D], F32, kind="ExternalOutput")

    wdq_in = nc.dram_tensor("wdq_in", [512, D], BF16)
    wdq_all = nc.dram_tensor("wdq_all", [D, D], BF16)
    aT_in = [nc.dram_tensor(f"a1_in{t}", [64 * 512, 128], BF16) for t in range(3)]
    aT_all = [nc.dram_tensor(f"a1_all{t}", [8 * 64 * 512, 128], BF16) for t in range(3)]
    a2_inA = nc.dram_tensor("a2_inA", [4096, 512], BF16)
    a2_inB = nc.dram_tensor("a2_inB", [4096, 512], BF16)
    a2_allA = nc.dram_tensor("a2_allA", [8 * 4096, 512], BF16)
    a2_allB = nc.dram_tensor("a2_allB", [8 * 4096, 512], BF16)
    rs_st = nc.dram_tensor("rs_st", [8 * 512, 1024], BF16)
    pgT = [nc.dram_tensor(f"pg{t}", [8 * 4096, 128], BF16) for t in range(3)]

    st = ExitStack()
    en = st.enter_context

    x_sb = en(nc.sbuf_tensor("x_sb", [128, NT * D], F32))
    mask_sb = en(nc.sbuf_tensor("mask_sb", [128, 128], BF16))
    id_sb = en(nc.sbuf_tensor("id_sb", [128, 128], BF16))
    sh_sb = None
    io_sb = en(nc.sbuf_tensor("io_sb", [32, 1], F32))
    stt_sb = en(nc.sbuf_tensor("stt_sb", [128, 16], F32))
    cst_scope = ExitStack()
    cst_f = cst_scope.enter_context(nc.sbuf_tensor("cst_f", [128, 128], F32))

    P = Prog(nc, st)
    E = P.ev

    def x_ap(tt, c0, n):
        return x_sb[:, tt * D + c0: tt * D + c0 + n]

    xT_holder = {}

    def xT_ap(k, t0, n):
        return xT_holder["t"][:, k * 1024 + t0: k * 1024 + t0 + n]

    s_x, s_c, s_cc = E("s_x"), E("s_c"), E("s_cc")
    for tt in range(NT):
        P.op("gpsimd", lambda g, tt=tt: g.dma_start(
            out=x_sb[:, tt * D:(tt + 1) * D],
            in_=x_in[tt * 128:(tt + 1) * 128, :]), s_x, 16)
    P.op("gpsimd", lambda g: g.dma_start(out=io_sb[:, :], in_=iota_in[:, :]), s_c, 16)
    P.op("gpsimd", lambda g: g.dma_start(out=cst_f[:, :], in_=mask01[:, :]), s_c, 16)
    P.wait("vector", s_c, s_c.n)
    P.op("vector", lambda v: v.tensor_copy(mask_sb[:, :], cst_f[:, :]), s_cc)
    s_ci = E("s_ci")
    P.wait("gpsimd", s_cc, s_cc.n)
    P.op("gpsimd", lambda g: g.dma_start(out=cst_f[:, :], in_=ident_in[:, :]), s_ci, 16)
    P.wait("vector", s_ci, s_ci.n)
    P.op("vector", lambda v: v.tensor_copy(id_sb[:, :], cst_f[:, :]), s_cc)

    cst_scope.close()

    dq = ExitStack()
    sh_sb = dq.enter_context(nc.sbuf_tensor("sh_sb", [128, 2048], I32))
    qwb = dq.enter_context(nc.sbuf_tensor("qwb", [128, 2048], I32))
    nibf = dq.enter_context(nc.sbuf_tensor("nibf", [128, 2048], F32))
    gi_i = dq.enter_context(nc.sbuf_tensor("gi_i", [32, 512], I32))
    gi_f = dq.enter_context(nc.sbuf_tensor("gi_f", [32, 512], F32))
    gt_b = dq.enter_context(nc.sbuf_tensor("gt_b", [32, 512], BF16))
    qz_i = dq.enter_context(nc.sbuf_tensor("qz_i", [32, 512], I32))
    z_i = dq.enter_context(nc.sbuf_tensor("z_i", [32, 2048], I32))
    z_b = dq.enter_context(nc.sbuf_tensor("z_b", [32, D], BF16))
    sc_f = dq.enter_context(nc.sbuf_tensor("sc_f", [32, 2048], F32))
    sc_b = dq.enter_context(nc.sbuf_tensor("sc_b", [32, D], BF16))
    wsub = dq.enter_context(nc.sbuf_tensor("wsub", [128, 2 * 2048], BF16))
    ps_s = dq.enter_context(nc.psum_tensor("ps_s", [128, 2048], F32))
    ps_z = dq.enter_context(nc.psum_tensor("ps_z", [128, 2048], F32))

    Lqz, Lsc, Lqw, Lgi, Vdq, Mdq, Ag1 = (E(n) for n in
        ("Lqz", "Lsc", "Lqw", "Lgi", "Vdq", "Mdq", "Ag1"))
    Wst = [E("Wst0"), E("Wst1")]

    P.wait("gpsimd", s_cc, s_cc.n)
    P.op("gpsimd", lambda g: g.dma_start(out=qz_i[:, :], in_=qz_in[:, :]), Lqz, 16)
    P.op("gpsimd", lambda g: g.dma_start(out=sh_sb[:, :], in_=shifts_in[:, :]), s_c, 16)
    P.op("gpsimd", lambda g: g.dma_start(out=gi_i[:, :], in_=gi_in[:, :]), Lgi, 16)

    P.wait("vector", Lgi, Lgi.n)
    P.wait("vector", Lqz, Lqz.n)
    P.wait("vector", s_c, s_c.n)
    P.opw("vector", lambda v: v.tensor_scalar(gi_f[:, :], gi_i[:, :], 0.0, 31.0,
                                              AL.max, AL.min), Vdq)
    P.op("vector", lambda v: v.tensor_scalar(gt_b[:, :], gi_f[:, :], io_sb[:, 0:1],
                                             None, AL.is_equal), Vdq)
    for half in range(2):
        if half > 0:
            P.wait("vector", Vdq, Vdq.n)
        for j in range(8):
            P.op("vector", lambda v, j=j, half=half: v.tensor_scalar(
                z_i[:, j:j + 2041:8],
                qz_i[:, half * 256:(half + 1) * 256], 4 * j, 15,
                AL.logical_shift_right, AL.bitwise_and), Vdq)
        P.wait("vector", Vdq, Vdq.n)
        P.op("vector", lambda v, half=half: v.tensor_scalar(
            z_b[:, half * 2048:(half + 1) * 2048], z_i[:, :], 1, None, AL.add), Vdq)
        if half > 0:
            P.wait("gpsimd", Vdq, Vdq.n)
        P.op("gpsimd", lambda g, half=half: g.dma_start(
            out=sc_f[:, :], in_=sc_in[:, half * 2048:(half + 1) * 2048]), Lsc, 16)
        P.wait("vector", Lsc, Lsc.n)
        P.op("vector", lambda v, half=half: v.tensor_copy(
            sc_b[:, half * 2048:(half + 1) * 2048], sc_f[:, :]), Vdq)
    vprep = Vdq.n

    nib_done = {}
    comb_done = {}
    for t in range(4):
        for half in range(2):
            i = t * 2 + half
            par = i % 2
            if i > 0:
                P.wait("gpsimd", Vdq, nib_done[i - 1])
            P.op("gpsimd", lambda g, t=t, half=half: g.dma_start(
                out=qwb[:, :],
                in_=qw_s[128 * t:128 * (t + 1), half * 2048:(half + 1) * 2048]),
                Lqw, 16)
            P.wait("vector", Lqw, Lqw.n)
            if t == 0 and half == 0:
                P.wait("vector", s_c, s_c.n)
            P.opw("vector", lambda v: v.tensor_tensor(
                qwb[:, :], qwb[:, :], sh_sb[:, :], AL.logical_shift_right))
            P.opw("vector", lambda v: v.tensor_scalar(
                qwb[:, :], qwb[:, :], 15, None, AL.bitwise_and))
            P.opw("vector", lambda v: v.tensor_copy(nibf[:, :], qwb[:, :]), Vdq)
            nib_done[i] = Vdq.n
            P.wait("tensor", Vdq, vprep)
            if i > 0:
                P.wait("tensor", Vdq, comb_done[i - 1])
            P.wait("tensor", Vdq, nib_done[i])
            for oc in range(4):
                P.op("tensor", lambda tn, t=t, half=half, oc=oc: tn.matmul(
                    ps_s[:, oc * 512:(oc + 1) * 512], gt_b[:, t * 128:(t + 1) * 128],
                    sc_b[:, half * 2048 + oc * 512:half * 2048 + (oc + 1) * 512],
                    start=True, stop=True))
            for oc in range(4):
                P.op("tensor", lambda tn, t=t, half=half, oc=oc: tn.matmul(
                    ps_z[:, oc * 512:(oc + 1) * 512], gt_b[:, t * 128:(t + 1) * 128],
                    z_b[:, half * 2048 + oc * 512:half * 2048 + (oc + 1) * 512],
                    start=True, stop=True), Mdq if oc == 3 else None)
            P.wait("vector", Mdq, Mdq.n)
            if i >= 2:
                P.wait("vector", Wst[par], Wst[par].n)
            P.opw("vector", lambda v, par=par: v.tensor_sub(
                wsub[:, par * 2048:(par + 1) * 2048], nibf[:, :], ps_z[:, :]))
            P.op("vector", lambda v, par=par: v.tensor_mul(
                wsub[:, par * 2048:(par + 1) * 2048],
                wsub[:, par * 2048:(par + 1) * 2048], ps_s[:, :]), Vdq)
            comb_done[i] = Vdq.n
            P.wait("scalar", Vdq, comb_done[i])
            P.op("scalar", lambda s, t=t, half=half, par=par: s.dma_start(
                out=wdq_in[t * 128:(t + 1) * 128, half * 2048:(half + 1) * 2048],
                in_=wsub[:, par * 2048:(par + 1) * 2048]), Wst[par], 16)

    P.wait("gpsimd", Wst[0], Wst[0].n)
    P.wait("gpsimd", Wst[1], Wst[1].n)
    P.op("gpsimd", lambda g: g.collective_compute(
        "AllGather", AL.bypass, replica_groups=[list(range(NCORE))],
        ins=[wdq_in[:, :]], outs=[wdq_all[:, :]]), Ag1)
    dq.close()
    xT_holder["t"] = en(nc.sbuf_tensor("xT_sb", [128, KC * 1024], BF16))

    Vln, Sln, Tln, Eln = E("Vln"), E("Sln"), E("Tln"), E("Eln")

    ln_ctr = [0]

    def emit_ln():
        P.fence(["vector", "scalar", "tensor"])
        u = ln_ctr[0]
        ln_ctr[0] += 1
        ln = ExitStack()
        xlnb = ln.enter_context(nc.sbuf_tensor(f"xlnb{u}", [128, D], BF16))
        bnst = ln.enter_context(nc.sbuf_tensor(f"bnst{u}", [128, 2 * 64], F32))
        pt = [ln.enter_context(nc.psum_tensor(f"ptln{u}_{i}", [128, 128], BF16))
              for i in range(6)]
        t_base = Tln.n
        e_base = Eln.n
        for tt in range(NT):
            sp = tt % 2
            c0 = sp * 8
            for i in range(7):
                P.op("vector", lambda v, tt=tt, i=i, sp=sp: v.bn_stats(
                    bnst[:, sp * 64 + i * 6:sp * 64 + (i + 1) * 6],
                    x_ap(tt, i * 512, 512)))
            P.opw("vector", lambda v, tt=tt, sp=sp: v.bn_stats(
                bnst[:, sp * 64 + 42:sp * 64 + 48], x_ap(tt, 7 * 512, 512)))
            P.opw("vector", lambda v, sp=sp, c0=c0: v.bn_aggr(
                stt_sb[:, c0:c0 + 2], bnst[:, sp * 64:sp * 64 + 48]))
            P.op("vector", lambda v, c0=c0: v.tensor_scalar(
                stt_sb[:, c0 + 1:c0 + 2], stt_sb[:, c0 + 1:c0 + 2], 1e-5, None,
                AL.add), Vln)
            P.wait("scalar", Vln, Vln.n)
            P.op("scalar", lambda s, c0=c0: s.activation(
                stt_sb[:, c0 + 2:c0 + 3], stt_sb[:, c0 + 1:c0 + 2], AF.Sqrt,
                0.0, 1.0), Sln)
            P.wait("vector", Sln, Sln.n)
            P.opw("vector", lambda v, c0=c0: v.reciprocal(
                stt_sb[:, c0 + 3:c0 + 4], stt_sb[:, c0 + 2:c0 + 3]))
            P.op("vector", lambda v, c0=c0: v.scalar_tensor_tensor(
                stt_sb[:, c0 + 4:c0 + 5], stt_sb[:, c0:c0 + 1], -1.0,
                stt_sb[:, c0 + 3:c0 + 4], AL.mult, AL.mult), Vln)
            P.wait("scalar", Vln, Vln.n)
            if tt >= 1:
                P.wait("scalar", Tln, t_base + tt * KC)
            P.op("scalar", lambda s, tt=tt, c0=c0: s.activation(
                xlnb[:, :], x_ap(tt, 0, D), AF.Identity,
                stt_sb[:, c0 + 4:c0 + 5], stt_sb[:, c0 + 3:c0 + 4]), Sln)
            P.wait("tensor", Sln, Sln.n)
            for k in range(KC):
                li = tt * KC + k
                if li >= 6:
                    P.wait("tensor", Eln, e_base + li - 5)
                pp = li % 6
                P.op("tensor", lambda tn, k=k, pp=pp: tn.transpose(
                    pt[pp][:, :], xlnb[:, k * 128:(k + 1) * 128],
                    id_sb[:, :]), Tln)
                P.wait("vector", Tln, Tln.n)
                P.op("vector", lambda v, k=k, tt=tt, pp=pp: v.tensor_copy(
                    xT_ap(k, tt * 128, 128), pt[pp][:, :]), Eln)
        ln.close()

    Pd = [E("Pd0"), E("Pd1")]
    Pc, Pmm, Pev = E("Pc"), E("Pmm"), E("Pev")
    Bld = [E("Bld0"), E("Bld1")]
    Sst = [E("Sst0"), E("Sst1")]

    prj_ctr = [0]

    def emit_proj(wsrc, w_is_f32, cast_eng, bias_src, sink, scale, stat_ap, sink_t=None,
                  extra_entry=()):
        P.fence(["sync", "gpsimd", "tensor", "vector", "scalar"])
        u = prj_ctr[0]
        prj_ctr[0] += 1
        a = ExitStack()
        if w_is_f32:
            wf = a.enter_context(nc.sbuf_tensor(f"wf{u}", [128, 2 * 512], F32))
        wb = a.enter_context(nc.sbuf_tensor(f"wb{u}", [128, 2 * 512], BF16))
        bias = a.enter_context(nc.sbuf_tensor(f"bias{u}", [128, 2 * 512], F32))
        tmp = a.enter_context(nc.sbuf_tensor(f"tmp{u}", [128, 512], F32))
        vst = a.enter_context(nc.sbuf_tensor(f"vst{u}", [128, 2 * 512], BF16))
        ps = [a.enter_context(nc.psum_tensor(f"pp{u}_{i}", [128, 512], F32))
              for i in range(NT)]
        m_base = Pmm.n
        e_base = Pev.n
        b_base = [Bld[0].n, Bld[1].n]
        for (eng, ev, val) in extra_entry:
            P.wait(eng, ev, val)
        for og in range(8):
            bpar = og % 2
            if og >= 2:
                P.wait("gpsimd", Pev, e_base + (og - 1) * NT)
            P.op("gpsimd", lambda g, og=og, bpar=bpar: g.dma_start(
                out=bias[:, bpar * 512:(bpar + 1) * 512],
                in_=bias_src[:, og * 512:(og + 1) * 512]), Bld[bpar], 16)
            for k in range(KC):
                n = og * KC + k
                par = n % 2
                if n >= 2:
                    P.wait("sync", Pc if w_is_f32 else Pmm,
                           (Pc.n - 1) if w_is_f32 else (m_base + n - 1))
                P.op("sync", lambda q, og=og, k=k, par=par, w=wsrc: q.dma_start(
                    out=(wf if w_is_f32 else wb)[:, par * 512:(par + 1) * 512],
                    in_=w[k * 128:(k + 1) * 128, og * 512:(og + 1) * 512]), Pd[par], 16)
                if w_is_f32:
                    P.wait(cast_eng, Pd[par], Pd[par].n)
                    if n >= 2:
                        P.wait(cast_eng, Pmm, m_base + n - 1)
                    if cast_eng == "scalar":
                        P.op(cast_eng, lambda e, par=par: e.copy(
                            wb[:, par * 512:(par + 1) * 512],
                            wf[:, par * 512:(par + 1) * 512]), Pc)
                    else:
                        P.op(cast_eng, lambda e, par=par: e.tensor_copy(
                            wb[:, par * 512:(par + 1) * 512],
                            wf[:, par * 512:(par + 1) * 512]), Pc)
                    P.wait("tensor", Pc, Pc.n)
                else:
                    P.wait("tensor", Pd[par], Pd[par].n)
                for tt in range(NT):
                    if og >= 1 and k == 0:
                        P.wait("tensor", Pev, e_base + (og - 1) * NT + tt + 1)
                    P.op("tensor", lambda tn, tt=tt, k=k, par=par: tn.matmul(
                        ps[tt][:, :], stat_ap(k, tt * 128, 128),
                        wb[:, par * 512:(par + 1) * 512],
                        start=(k == 0), stop=(k == KC - 1)),
                        Pmm if tt == NT - 1 else None)
            P.wait("vector", Pmm, m_base + (og + 1) * KC)
            P.wait("vector", Bld[bpar], b_base[bpar] + (og // 2 + 1) * 16)
            for tt in range(NT):
                mtile = og * NT + tt
                if sink == "resid":
                    P.opw("vector", lambda v, tt=tt, bpar=bpar: v.tensor_add(
                        tmp[:, :], ps[tt][:, :], bias[:, bpar * 512:(bpar + 1) * 512]))
                    P.opw("vector", lambda v, tt=tt, og=og, scale=scale:
                          v.scalar_tensor_tensor(
                              x_ap(tt, og * 512, 512), tmp[:, :], scale,
                              x_ap(tt, og * 512, 512), AL.mult, AL.add), Pev)
                else:
                    par2 = mtile % 2
                    if mtile >= 2:
                        P.wait("vector", Sst[par2], Sst[par2].n)
                    P.op("vector", lambda v, tt=tt, bpar=bpar, par2=par2:
                         v.scalar_tensor_tensor(
                             vst[:, par2 * 512:(par2 + 1) * 512], ps[tt][:, :], 1.0,
                             bias[:, bpar * 512:(bpar + 1) * 512], AL.bypass, AL.add),
                         Pev)
                    P.wait("scalar", Pev, Pev.n)
                    b = tt // 4
                    for cb in range(4):
                        h = og * 4 + cb
                        gg = b * 32 + h
                        r0 = gg * 512 + (tt % 4) * 128
                        P.op("scalar", lambda s, r0=r0, par2=par2, cb=cb,
                             st_=sink_t: s.dma_start(
                            out=aT_in[st_][r0:r0 + 128, :],
                            in_=vst[:, par2 * 512 + cb * 128:par2 * 512 + (cb + 1) * 128]),
                            Sst[par2], 16)
        a.close()
        return dict(mm=Pmm.n, ev=Pev.n)

    P.wait("vector", s_x, s_x.n)
    emit_ln()
    ffn1 = emit_proj(wdq_all, False, None, b_bc["bf"], "resid", 0.5, xT_ap,
                     extra_entry=[("sync", Ag1, 1)])

    if STOP_AFTER == "ffn1":
        _finish(P, x_ap, y_out, ffn1["ev"])
        P.emit(en(nc.Block()))
        return nc

    emit_ln()
    AgT = [E(f"AgT{t}") for t in range(3)]

    def emit_a1_ag(t):
        P.wait("gpsimd", Sst[0], Sst[0].n)
        P.wait("gpsimd", Sst[1], Sst[1].n)
        P.op("gpsimd", lambda g, t=t: g.collective_compute(
            "AllGather", AL.bypass, replica_groups=[list(range(NCORE))],
            ins=[aT_in[t][:, :]], outs=[aT_all[t][:, :]]), AgT[t])

    emit_proj(wkT, True, "vector", b_bc["bk"], "a2a", 1.0, xT_ap, 0)
    emit_a1_ag(0)
    emit_proj(wqT, True, "vector", b_bc["bq"], "a2a", 1.0, xT_ap, 1)
    emit_a1_ag(1)
    prj_v = emit_proj(wvT, True, "scalar", b_bc["bv"], "a2a", 1.0, xT_ap, 2)
    emit_a1_ag(2)
    if STOP_AFTER == "qkv":
        _finish(P, x_ap, y_out, prj_v["ev"])
        P.emit(en(nc.Block()))
        return nc

    Gp1x = E("Gp10")
    aTv3 = [aT_all[t].rearrange("(i r) c -> i r c", i=8) for t in range(3)]

    def emit_all_gathers2(g):
        for arm in nc.switch_core_id(engines=[g], n=128):
            c = arm.logical % NCORE
            for t in range(3):
                g.wait_ge(AgT[t].h, 1)
                g.dma_start(out=pgT[t][:, :],
                            in_=aTv3[t][:, 8 * c * 512:(8 * c + 8) * 512, :]
                            ).then_inc(Gp1x.h, 16)
    P.raw("sync", emit_all_gathers2)
    Gp1x.n += 3 * 16

    P.fence(["gpsimd", "vector", "scalar", "tensor"])
    at = ExitStack()
    xTt = xT_holder["t"]
    HB = 16384

    def K_ap(par, c0, n):
        return xTt[:, par * HB + c0: par * HB + c0 + n]

    def Q_ap(par, c0, n):
        return xTt[:, par * HB + 4096 + c0: par * HB + 4096 + c0 + n]

    def V_ap(par, c0, n):
        return xTt[:, par * HB + 8192 + c0: par * HB + 8192 + c0 + n]

    attT = at.enter_context(nc.sbuf_tensor("attT", [128, 3 * 512], BF16))
    rstb = at.enter_context(nc.sbuf_tensor("rstb", [128, 4096], BF16))
    ps_sc = [at.enter_context(nc.psum_tensor(f"ps_sc{i}", [128, 512], F32))
             for i in range(4)]
    ps_R = [at.enter_context(nc.psum_tensor(f"ps_R{i}", [128, 512], F32))
            for i in range(4)]

    Vini = E("Vini")
    Lks = [E("Lks0"), E("Lks1")]
    Lqs = [E("Lqs0"), E("Lqs1")]
    Rtd = [E("Rtd0"), E("Rtd1")]
    Gp1 = [E("Gp10"), E("Gp11")]
    Amm, Aex, Amk, Acv, Arv, Trv, Erv = (
        E(n) for n in ("Amm", "Aex", "Amk", "Acv", "Arv", "Trv", "Erv"))
    A2st = [E("A2st0"), E("A2st1")]
    Ag2, Rld = E("Ag2"), E("Rld")

    P.op("vector", lambda v: v.memset(V_ap(0, 0, 4128), 1.0), Vini)
    P.op("vector", lambda v: v.memset(V_ap(1, 0, 4128), 1.0), Vini)
    P.wait("sync", Pmm, Pmm.n)
    P.wait("sync", Eln, Eln.n)
    P.wait("sync", Pev, Pev.n)



    mm_end = {}
    lks_after = {}
    lqs_after = {}

    OLDLOAD = bool(_os.environ.get("MK_OLDLOAD"))

    def issue_pair_loads(p):
        par = p % 2
        for q in ("sync", "scalar"):
            P.wait(q, Gp1x, Gp1x.n)
            P.wait(q, Vini, 2)
            if p >= 2:
                P.wait(q, Acv, mm_end[p - 2])
        for i in range(8):
            r0 = i * 4096 + p * 512
            P.op("sync", lambda s, r0=r0, i=i, par=par: s.dma_start(
                out=K_ap(par, i * 512, 512), in_=pgT[0][r0:r0 + 512, :],
                transpose=True), Lks[par], 16)
            P.op("sync", lambda s, r0=r0, i=i, par=par: s.dma_start(
                out=Q_ap(par, i * 512, 512), in_=pgT[1][r0:r0 + 512, :],
                transpose=True), Lks[par], 16)
        for a_ in range(32):
            r0 = (a_ // 4) * 4096 + p * 512 + (a_ % 4) * 128
            P.op("scalar", lambda s, r0=r0, a_=a_, par=par: s.dma_start(
                out=V_ap(par, a_ * 129, 128), in_=pgT[2][r0:r0 + 128, :]),
                Lqs[par], 16)
        lks_after[p] = Lks[par].n
        lqs_after[p] = Lqs[par].n

    NOPRE = bool(_os.environ.get("MK_NOPRE"))
    g_ch = [0]
    issue_pair_loads(0)
    if not NOPRE:
        issue_pair_loads(1)

    for p in range(8):
        par = p % 2
        if NOPRE and p > 0:
            issue_pair_loads(p)
        P.wait("tensor", Lks[par], lks_after[p])
        P.wait("tensor", Lqs[par], lqs_after[p])
        for G in QT_GROUPS:
            q0, q1 = G[0], G[-1]
            rv_pass = Arv.n
            chunks = []
            for kt in range(0, q1 + 1):
                vis0 = max(kt, q0)
                nch = (q1 - vis0 + 1 + 3) // 4
                for c in range(nch):
                    cq0 = vis0 + 4 * c
                    ncols = min(4, q1 - cq0 + 1) * 128
                    chunks.append((kt, c, cq0, ncols, q0 <= kt <= q1 and c == 0))

            def emit_scores(desc):
                kt, c, cq0, ncols, has_diag = desc
                ch = g_ch[0]
                g_ch[0] += 1
                spar = ch % 4
                apar_t = ch % 3
                if ch >= 4:
                    P.wait("tensor", Aex, ch - 3)
                if kt == 0 and c == 0:
                    P.wait("tensor", Arv, rv_pass)
                P.op("tensor", lambda tn, kt=kt, cq0=cq0, ncols=ncols, spar=spar,
                     par=par: tn.matmul(
                         ps_sc[spar][:, 0:ncols], K_ap(par, kt * 128, 128),
                         Q_ap(par, cq0 * 128, ncols),
                         start=True, stop=True), Amm)
                P.wait("scalar", Amm, Amm.n)
                if ch >= 3:
                    P.wait("scalar", Acv, ch - 2)
                P.op("scalar", lambda s, spar=spar, apar_t=apar_t, ncols=ncols:
                     s.activation(attT[:, apar_t * 512:apar_t * 512 + ncols],
                                  ps_sc[spar][:, 0:ncols], AF.Exp, 0.0, SM_SCALE),
                     Aex)
                if has_diag:
                    P.wait("vector", Aex, Aex.n)
                    P.op("vector", lambda v, apar_t=apar_t: v.tensor_mul(
                        attT[:, apar_t * 512:apar_t * 512 + 128],
                        attT[:, apar_t * 512:apar_t * 512 + 128], mask_sb[:, :]), Amk)
                return (ch, Aex.n, Amk.n)

            def emit_attv(desc, toks):
                kt, c, cq0, ncols, has_diag = desc
                ch, aex_n, amk_n = toks
                apar_t = ch % 3
                if has_diag:
                    P.wait("tensor", Amk, amk_n)
                else:
                    P.wait("tensor", Aex, aex_n)
                nq = ncols // 128
                for qi in range(nq):
                    qt = cq0 + qi
                    slot = qt - q0
                    P.op("tensor", lambda tn, qt=qt, qi=qi, kt=kt, apar_t=apar_t,
                         slot=slot, par=par: tn.matmul(
                             ps_R[slot][:, 0:129],
                             attT[:, apar_t * 512 + qi * 128:apar_t * 512 + (qi + 1) * 128],
                             V_ap(par, kt * 129, 129),
                             start=(kt == 0), stop=(kt == qt)),
                         Acv if qi == nq - 1 else None)
                if has_diag:
                    qt = kt
                    slot = qt - q0
                    P.wait("vector", Acv, Acv.n)
                    P.wait("vector", Arv, Arv.n)
                    if qt == 0 and p > 0:
                        P.wait("vector", Rtd[(p - 1) % 2], Rtd[(p - 1) % 2].n)
                    P.opw("vector", lambda v, slot=slot: v.reciprocal(
                        stt_sb[:, 8:9], ps_R[slot][:, 128:129]))
                    P.op("vector", lambda v, slot=slot, qt=qt: v.tensor_scalar(
                        rstb[:, qt * 128:(qt + 1) * 128], ps_R[slot][:, 0:128],
                        stt_sb[:, 8:9], None, AL.mult), Arv)

            SHIFT = 2
            toks = {}
            for i, desc in enumerate(chunks):
                toks[i] = emit_scores(desc)
                if i >= SHIFT:
                    emit_attv(chunks[i - SHIFT], toks[i - SHIFT])
            for i in range(max(0, len(chunks) - SHIFT), len(chunks)):
                emit_attv(chunks[i], toks[i])
        mm_end[p] = Acv.n
        if not NOPRE and p + 2 < 8:
            issue_pair_loads(p + 2)
        apar = p % 2
        P.wait("scalar", Arv, Arv.n)
        a2t = a2_inA if p < 4 else a2_inB
        pc = (p % 4) * 128
        for qt in range(32):
            P.op("scalar", lambda s, qt=qt, a2t=a2t, pc=pc: s.dma_start(
                out=a2t[qt * 128:(qt + 1) * 128, pc:pc + 128],
                in_=rstb[:, qt * 128:(qt + 1) * 128]), Rtd[apar], 16)
        if p == 3:
            P.wait("gpsimd", Rtd[0], Rtd[0].n)
            P.wait("gpsimd", Rtd[1], Rtd[1].n)
            P.op("gpsimd", lambda g: g.collective_compute(
                "AllGather", AL.bypass, replica_groups=[list(range(NCORE))],
                ins=[a2_inA[:, :]], outs=[a2_allA[:, :]]), Ag2)
        if STOP_AFTER == "pair0":
            break
    at.close()
    if STOP_AFTER in ("attn", "pair0"):
        _finish(P, x_ap, y_out, ffn1["ev"])
        P.emit(en(nc.Block()))
        return nc

    P.wait("gpsimd", Rtd[0], Rtd[0].n)
    P.wait("gpsimd", Rtd[1], Rtd[1].n)
    P.op("gpsimd", lambda g: g.collective_compute(
        "AllGather", AL.bypass, replica_groups=[list(range(NCORE))],
        ins=[a2_inB[:, :]], outs=[a2_allB[:, :]]), Ag2)
    P.wait("gpsimd", Ag2, Ag2.n)
    P.wait("gpsimd", Pmm, prj_v["mm"])
    Go = E("Go")

    a2vA = a2_allA.rearrange("(i q) c -> i q c", i=8)
    a2vB = a2_allB.rearrange("(i q) c -> i q c", i=8)
    rs_v = rs_st.rearrange("(i q) c -> i q c", i=8)

    def emit_o_gather(g):
        for arm in nc.switch_core_id(engines=[g], n=128):
            c = arm.logical % NCORE
            g.wait_ge(Ag2.h, 2)
            g.dma_start(out=rs_v[:, :, 0:512],
                        in_=a2vA[:, 512 * c:512 * (c + 1), :]).then_inc(Go.h, 16)
            g.dma_start(out=rs_v[:, :, 512:1024],
                        in_=a2vB[:, 512 * c:512 * (c + 1), :]).then_inc(Go.h, 16)
    P.raw("sync", emit_o_gather)
    Go.n += 2 * 16
    rb = ExitStack()
    rb_st = rb.enter_context(nc.sbuf_tensor("rb_st", [128, 4 * 128], BF16))
    rb_pt = [rb.enter_context(nc.psum_tensor(f"rb_pt{i}", [128, 128], BF16))
             for i in range(4)]
    Rli = [E(f"Rli{i}") for i in range(4)]
    Rtt, Rle = E("Rtt"), E("Rle")
    P.wait("sync", Go, Go.n)
    P.fence(["sync", "tensor", "vector"])
    n2 = [0]
    for b in range(2):
        for h in range(32):
            gg = b * 32 + h
            i_src, pp_ = gg // 8, gg % 8
            c0 = (pp_ % 4) * 128 + (0 if pp_ < 4 else 512)
            for j in range(4):
                n = n2[0]
                n2[0] += 1
                sl = n % 4
                if n >= 4:
                    P.wait("sync", Rtt, n - 3)
                P.op("sync", lambda s, i_src=i_src, j=j, c0=c0, sl=sl: s.dma_start(
                    out=rb_st[:, sl * 128:(sl + 1) * 128],
                    in_=rs_st[i_src * 512 + j * 128:i_src * 512 + (j + 1) * 128,
                              c0:c0 + 128]), Rli[sl], 16)
                P.wait("tensor", Rli[sl], Rli[sl].n)
                if n >= 4:
                    P.wait("tensor", Rle, n - 3)
                P.op("tensor", lambda tn, sl=sl: tn.transpose(
                    rb_pt[sl][:, :], rb_st[:, sl * 128:(sl + 1) * 128],
                    id_sb[:, :]), Rtt)
                P.wait("vector", Rtt, Rtt.n)
                P.op("vector", lambda v, h=h, b=b, j=j, sl=sl: v.tensor_copy(
                    xT_holder["t"][:, h * 1024 + b * 512 + j * 128:
                                   h * 1024 + b * 512 + (j + 1) * 128],
                    rb_pt[sl][:, :]), Rle)
    rb.close()

    oprj = emit_proj(woT, True, "vector", b_bc["bo"], "resid", 1.0, xT_ap,
                     extra_entry=[("tensor", Rle, Rle.n), ("vector", Rle, Rle.n),
                                  ("sync", Rle, Rle.n)])

    if STOP_AFTER == "oproj":
        _finish(P, x_ap, y_out, oprj["ev"])
        P.emit(en(nc.Block()))
        return nc

    emit_ln()
    ffn2 = emit_proj(wdq_all, False, None, b_bc["bf"], "resid", 0.5, xT_ap)
    _finish(P, x_ap, y_out, ffn2["ev"])
    P.emit(en(nc.Block()))
    return nc


def _finish(P, x_ap, y_out, ev_val):
    Sout = P.ev("Sout")
    Pev = P.ev("Pev")
    P.wait("scalar", Pev, ev_val)
    for tt in range(NT):
        P.op("scalar", lambda s, tt=tt: s.dma_start(
            out=y_out[tt * 128:(tt + 1) * 128, :], in_=x_ap(tt, 0, D)), Sout, 16)
    P.wait("scalar", Sout, Sout.n)


def kernel(L0_attn_x, L0_attn_Wq, L0_attn_bq, L0_attn_Wk, L0_attn_bk, L0_attn_Wv,
           L0_attn_bv, L0_attn_Wo, L0_attn_bo, L0_ffn_qweight, L0_ffn_qzeros,
           L0_ffn_scales, L0_ffn_g_idx, L0_ffn_bias):
    x = np.asarray(L0_attn_x, dtype=np.float32)
    cm = np.ascontiguousarray
    wq = cm(np.asarray(L0_attn_Wq, np.float32).T)
    wk = cm(np.asarray(L0_attn_Wk, np.float32).T)
    wv = cm(np.asarray(L0_attn_Wv, np.float32).T)
    wo = cm(np.asarray(L0_attn_Wo, np.float32).T)
    qw = np.asarray(L0_ffn_qweight, np.int32)
    qz = np.asarray(L0_ffn_qzeros, np.int32)
    sc = np.asarray(L0_ffn_scales, np.float32)
    gi = np.asarray(L0_ffn_g_idx, np.int32)
    bias_bc = {}
    for n, b in (("bq", L0_attn_bq), ("bk", L0_attn_bk), ("bv", L0_attn_bv),
                 ("bo", L0_attn_bo), ("bf", L0_ffn_bias)):
        bias_bc[n] = cm(np.broadcast_to(np.asarray(b, np.float32)[None, :], (128, D)))
    r = np.arange(128)
    mask01 = (r[:, None] <= r[None, :]).astype(np.float32)
    ident = np.eye(128, dtype=np.float32)
    shifts = cm(np.broadcast_to((4 * (r % 8)).astype(np.int32)[:, None], (128, 2048)))
    iota = np.arange(32, dtype=np.float32).reshape(32, 1)

    nc = build()
    in_maps = []
    for c in range(NCORE):
        xs = np.concatenate([x[0, 512 * c:512 * (c + 1), :],
                             x[1, 512 * c:512 * (c + 1), :]], axis=0)
        m = dict(x_in=cm(xs), wqT=wq, wkT=wk, wvT=wv, woT=wo,
                 qw_s=cm(np.repeat(qw[64 * c:64 * (c + 1), :], 8, axis=0)),
                 qz_in=qz, sc_in=sc,
                 gi_in=cm(np.broadcast_to(gi[512 * c:512 * (c + 1)][None, :],
                                          (32, 512))),
                 mask01=mask01, ident_in=ident, shifts_in=shifts, iota_in=iota)
        for n in bias_bc:
            m[n + "_bc"] = bias_bc[n]
        in_maps.append(m)
    res = run_bass_kernel_spmd(nc, in_maps, core_ids=list(range(NCORE)))
    out = np.empty((2, 4096, D), dtype=np.float32)
    for c in range(NCORE):
        yc = res.results[c]["y_out"]
        out[0, 512 * c:512 * (c + 1), :] = yc[0:512, :]
        out[1, 512 * c:512 * (c + 1), :] = yc[512:1024, :]
    return out


if __name__ == "__main__":
    import reference as R
    ins = R.setup_inputs(0)
    out = kernel(**{k: np.asarray(v) for k, v in ins.items()})
    ref = np.asarray(R.reference(**ins))
    num = float(((out - ref) ** 2).mean())
    den = float((ref ** 2).mean())
    print("resid var ratio:", num / den)


# version 14
# speedup vs baseline: 1.0110x; 1.0107x over previous
import numpy as np
import ml_dtypes
import concourse.bass as bass
import concourse.mybir as mybir
from concourse.bass_utils import run_bass_kernel_spmd
from contextlib import ExitStack

F32 = mybir.dt.float32
BF16 = mybir.dt.bfloat16
I32 = mybir.dt.int32
AF = mybir.ActivationFunctionType
AL = mybir.AluOpType
AX = mybir.AxisListType
bfdt = ml_dtypes.bfloat16

D = 4096
NT = 8
KC = 32
NCORE = 8
SM_SCALE = 1.0 / float(np.sqrt(128.0))

A1_ROWS = 64 * 3 * 512
A2_ROWS = 8 * 128

QT_GROUPS = [list(range(4 * i, 4 * i + 4)) for i in range(8)]

import os as _os
CUT = float(_os.environ.get("MK_CUT", "9"))
STOP_AFTER = ""

ENGINES = ("gpsimd", "sync", "vector", "scalar", "tensor")


class Ev:
    def __init__(self, st, nc, name):
        self.h = st.enter_context(nc.semaphore(name))
        self.n = 0


class Prog:
    def __init__(self, nc, st):
        self.nc = nc
        self.st = st
        self.streams = {e: [] for e in ENGINES}
        self.evs = {}

    def ev(self, name):
        if name not in self.evs:
            self.evs[name] = Ev(self.st, self.nc, name)
        return self.evs[name]

    def wait(self, eng, ev, val):
        if val > 0:
            self.streams[eng].append(("w", ev, val))

    def op(self, eng, fn, sig=None, amt=1):
        if sig is not None:
            sig.n += amt
        self.streams[eng].append(("o", fn, sig, amt))

    def raw(self, eng, fn):
        self.streams[eng].append(("o", fn, None, 0))

    def opw(self, eng, fn, sig=None):
        ev = sig if sig is not None else self.ev("self_" + eng)
        self.op(eng, fn, ev)
        self.wait(eng, ev, ev.n)

    FENCE_EXCLUDE = {"Ag1", "AgT0", "AgT1", "AgT2", "Ag2", "Gp10", "Gp11",
                     "Go", "Rld"}

    def fence(self, engines):
        for eng in engines:
            for name, ev in list(self.evs.items()):
                if ev.n > 0 and name not in self.FENCE_EXCLUDE:
                    self.streams[eng].append(("w", ev, ev.n))

    def emit(self, block):
        for ename in ENGINES:
            stream = self.streams[ename]
            if not stream:
                continue
            deco = getattr(block, ename)

            def make(stream):
                def body(eng):
                    for item in stream:
                        if item[0] == "w":
                            eng.wait_ge(item[1].h, item[2])
                        else:
                            ins = item[1](eng)
                            if item[2] is not None:
                                ins.then_inc(item[2].h, item[3])
                return body
            deco(make(stream))


def build():
    nc = bass.Bass("TRN2", target_bir_lowering=False)

    x_in = nc.dram_tensor("x_in", [1024, D], F32, kind="ExternalInput")
    wqT = nc.dram_tensor("wqT", [D, D], F32, kind="ExternalInput")
    wkT = nc.dram_tensor("wkT", [D, D], F32, kind="ExternalInput")
    wvT = nc.dram_tensor("wvT", [D, D], F32, kind="ExternalInput")
    woT = nc.dram_tensor("woT", [D, D], F32, kind="ExternalInput")
    qw_s = nc.dram_tensor("qw_s", [512, D], I32, kind="ExternalInput")
    qz_in = nc.dram_tensor("qz_in", [32, 512], I32, kind="ExternalInput")
    sc_in = nc.dram_tensor("sc_in", [32, D], F32, kind="ExternalInput")
    gi_in = nc.dram_tensor("gi_in", [32, 512], I32, kind="ExternalInput")
    b_bc = {n: nc.dram_tensor(n + "_bc", [128, D], F32, kind="ExternalInput")
            for n in ("bq", "bk", "bv", "bo", "bf")}
    mask01 = nc.dram_tensor("mask01", [128, 128], F32, kind="ExternalInput")
    ident_in = nc.dram_tensor("ident_in", [128, 128], F32, kind="ExternalInput")
    shifts_in = nc.dram_tensor("shifts_in", [128, 2048], I32, kind="ExternalInput")
    iota_in = nc.dram_tensor("iota_in", [32, 1], F32, kind="ExternalInput")
    y_out = nc.dram_tensor("y_out", [1024, D], F32, kind="ExternalOutput")

    wdq_in = nc.dram_tensor("wdq_in", [512, D], BF16)
    wdq_all = nc.dram_tensor("wdq_all", [D, D], BF16)
    aT_in = [nc.dram_tensor(f"a1_in{t}", [64 * 512, 128], BF16) for t in range(3)]
    aT_all = [nc.dram_tensor(f"a1_all{t}", [8 * 64 * 512, 128], BF16) for t in range(3)]
    a2_inA = nc.dram_tensor("a2_inA", [4096, 512], BF16)
    a2_inB = nc.dram_tensor("a2_inB", [4096, 512], BF16)
    a2_allA = nc.dram_tensor("a2_allA", [8 * 4096, 512], BF16)
    a2_allB = nc.dram_tensor("a2_allB", [8 * 4096, 512], BF16)
    rs_st = nc.dram_tensor("rs_st", [8 * 512, 1024], BF16)
    p1buf = nc.dram_tensor("p1buf", [8, 8 * 1536, 128], BF16)

    st = ExitStack()
    en = st.enter_context

    x_sb = en(nc.sbuf_tensor("x_sb", [128, NT * D], F32))
    mask_sb = en(nc.sbuf_tensor("mask_sb", [128, 128], BF16))
    id_sb = en(nc.sbuf_tensor("id_sb", [128, 128], BF16))
    sh_sb = None
    io_sb = en(nc.sbuf_tensor("io_sb", [32, 1], F32))
    stt_sb = en(nc.sbuf_tensor("stt_sb", [128, 16], F32))
    cst_scope = ExitStack()
    cst_f = cst_scope.enter_context(nc.sbuf_tensor("cst_f", [128, 128], F32))

    P = Prog(nc, st)
    E = P.ev

    def x_ap(tt, c0, n):
        return x_sb[:, tt * D + c0: tt * D + c0 + n]

    xT_holder = {}

    def xT_ap(k, t0, n):
        return xT_holder["t"][:, k * 1024 + t0: k * 1024 + t0 + n]

    s_x, s_c, s_cc = E("s_x"), E("s_c"), E("s_cc")
    for tt in range(NT):
        P.op("gpsimd", lambda g, tt=tt: g.dma_start(
            out=x_sb[:, tt * D:(tt + 1) * D],
            in_=x_in[tt * 128:(tt + 1) * 128, :]), s_x, 16)
    P.op("gpsimd", lambda g: g.dma_start(out=io_sb[:, :], in_=iota_in[:, :]), s_c, 16)
    P.op("gpsimd", lambda g: g.dma_start(out=cst_f[:, :], in_=mask01[:, :]), s_c, 16)
    P.wait("vector", s_c, s_c.n)
    P.op("vector", lambda v: v.tensor_copy(mask_sb[:, :], cst_f[:, :]), s_cc)
    s_ci = E("s_ci")
    P.wait("gpsimd", s_cc, s_cc.n)
    P.op("gpsimd", lambda g: g.dma_start(out=cst_f[:, :], in_=ident_in[:, :]), s_ci, 16)
    P.wait("vector", s_ci, s_ci.n)
    P.op("vector", lambda v: v.tensor_copy(id_sb[:, :], cst_f[:, :]), s_cc)

    cst_scope.close()

    dq = ExitStack()
    sh_sb = dq.enter_context(nc.sbuf_tensor("sh_sb", [128, 2048], I32))
    qwb = dq.enter_context(nc.sbuf_tensor("qwb", [128, 2048], I32))
    nibf = dq.enter_context(nc.sbuf_tensor("nibf", [128, 2048], F32))
    gi_i = dq.enter_context(nc.sbuf_tensor("gi_i", [32, 512], I32))
    gi_f = dq.enter_context(nc.sbuf_tensor("gi_f", [32, 512], F32))
    gt_b = dq.enter_context(nc.sbuf_tensor("gt_b", [32, 512], BF16))
    qz_i = dq.enter_context(nc.sbuf_tensor("qz_i", [32, 512], I32))
    z_i = dq.enter_context(nc.sbuf_tensor("z_i", [32, 2048], I32))
    z_b = dq.enter_context(nc.sbuf_tensor("z_b", [32, D], BF16))
    sc_f = dq.enter_context(nc.sbuf_tensor("sc_f", [32, 2048], F32))
    sc_b = dq.enter_context(nc.sbuf_tensor("sc_b", [32, D], BF16))
    wsub = dq.enter_context(nc.sbuf_tensor("wsub", [128, 2 * 2048], BF16))
    ps_s = dq.enter_context(nc.psum_tensor("ps_s", [128, 2048], F32))
    ps_z = dq.enter_context(nc.psum_tensor("ps_z", [128, 2048], F32))

    Lqz, Lsc, Lqw, Lgi, Vdq, Mdq, Ag1 = (E(n) for n in
        ("Lqz", "Lsc", "Lqw", "Lgi", "Vdq", "Mdq", "Ag1"))
    Wst = [E("Wst0"), E("Wst1")]

    P.wait("gpsimd", s_cc, s_cc.n)
    P.op("gpsimd", lambda g: g.dma_start(out=qz_i[:, :], in_=qz_in[:, :]), Lqz, 16)
    P.op("gpsimd", lambda g: g.dma_start(out=sh_sb[:, :], in_=shifts_in[:, :]), s_c, 16)
    P.op("gpsimd", lambda g: g.dma_start(out=gi_i[:, :], in_=gi_in[:, :]), Lgi, 16)

    P.wait("vector", Lgi, Lgi.n)
    P.wait("vector", Lqz, Lqz.n)
    P.wait("vector", s_c, s_c.n)
    P.opw("vector", lambda v: v.tensor_scalar(gi_f[:, :], gi_i[:, :], 0.0, 31.0,
                                              AL.max, AL.min), Vdq)
    P.op("vector", lambda v: v.tensor_scalar(gt_b[:, :], gi_f[:, :], io_sb[:, 0:1],
                                             None, AL.is_equal), Vdq)
    for half in range(2):
        if half > 0:
            P.wait("vector", Vdq, Vdq.n)
        for j in range(8):
            P.op("vector", lambda v, j=j, half=half: v.tensor_scalar(
                z_i[:, j:j + 2041:8],
                qz_i[:, half * 256:(half + 1) * 256], 4 * j, 15,
                AL.logical_shift_right, AL.bitwise_and), Vdq)
        P.wait("vector", Vdq, Vdq.n)
        P.op("vector", lambda v, half=half: v.tensor_scalar(
            z_b[:, half * 2048:(half + 1) * 2048], z_i[:, :], 1, None, AL.add), Vdq)
        if half > 0:
            P.wait("gpsimd", Vdq, Vdq.n)
        P.op("gpsimd", lambda g, half=half: g.dma_start(
            out=sc_f[:, :], in_=sc_in[:, half * 2048:(half + 1) * 2048]), Lsc, 16)
        P.wait("vector", Lsc, Lsc.n)
        P.op("vector", lambda v, half=half: v.tensor_copy(
            sc_b[:, half * 2048:(half + 1) * 2048], sc_f[:, :]), Vdq)
    vprep = Vdq.n

    nib_done = {}
    comb_done = {}
    for t in range(4):
        for half in range(2):
            i = t * 2 + half
            par = i % 2
            if i > 0:
                P.wait("gpsimd", Vdq, nib_done[i - 1])
            P.op("gpsimd", lambda g, t=t, half=half: g.dma_start(
                out=qwb[:, :],
                in_=qw_s[128 * t:128 * (t + 1), half * 2048:(half + 1) * 2048]),
                Lqw, 16)
            P.wait("vector", Lqw, Lqw.n)
            if t == 0 and half == 0:
                P.wait("vector", s_c, s_c.n)
            P.opw("vector", lambda v: v.tensor_tensor(
                qwb[:, :], qwb[:, :], sh_sb[:, :], AL.logical_shift_right))
            P.opw("vector", lambda v: v.tensor_scalar(
                qwb[:, :], qwb[:, :], 15, None, AL.bitwise_and))
            P.opw("vector", lambda v: v.tensor_copy(nibf[:, :], qwb[:, :]), Vdq)
            nib_done[i] = Vdq.n
            P.wait("tensor", Vdq, vprep)
            if i > 0:
                P.wait("tensor", Vdq, comb_done[i - 1])
            P.wait("tensor", Vdq, nib_done[i])
            for oc in range(4):
                P.op("tensor", lambda tn, t=t, half=half, oc=oc: tn.matmul(
                    ps_s[:, oc * 512:(oc + 1) * 512], gt_b[:, t * 128:(t + 1) * 128],
                    sc_b[:, half * 2048 + oc * 512:half * 2048 + (oc + 1) * 512],
                    start=True, stop=True))
            for oc in range(4):
                P.op("tensor", lambda tn, t=t, half=half, oc=oc: tn.matmul(
                    ps_z[:, oc * 512:(oc + 1) * 512], gt_b[:, t * 128:(t + 1) * 128],
                    z_b[:, half * 2048 + oc * 512:half * 2048 + (oc + 1) * 512],
                    start=True, stop=True), Mdq if oc == 3 else None)
            P.wait("vector", Mdq, Mdq.n)
            if i >= 2:
                P.wait("vector", Wst[par], Wst[par].n)
            P.opw("vector", lambda v, par=par: v.tensor_sub(
                wsub[:, par * 2048:(par + 1) * 2048], nibf[:, :], ps_z[:, :]))
            P.op("vector", lambda v, par=par: v.tensor_mul(
                wsub[:, par * 2048:(par + 1) * 2048],
                wsub[:, par * 2048:(par + 1) * 2048], ps_s[:, :]), Vdq)
            comb_done[i] = Vdq.n
            P.wait("scalar", Vdq, comb_done[i])
            P.op("scalar", lambda s, t=t, half=half, par=par: s.dma_start(
                out=wdq_in[t * 128:(t + 1) * 128, half * 2048:(half + 1) * 2048],
                in_=wsub[:, par * 2048:(par + 1) * 2048]), Wst[par], 16)

    P.wait("gpsimd", Wst[0], Wst[0].n)
    P.wait("gpsimd", Wst[1], Wst[1].n)
    P.op("gpsimd", lambda g: g.collective_compute(
        "AllGather", AL.bypass, replica_groups=[list(range(NCORE))],
        ins=[wdq_in[:, :]], outs=[wdq_all[:, :]]), Ag1)
    dq.close()
    xT_holder["t"] = en(nc.sbuf_tensor("xT_sb", [128, KC * 1024], BF16))

    Vln, Sln, Tln, Eln = E("Vln"), E("Sln"), E("Tln"), E("Eln")

    ln_ctr = [0]

    def emit_ln():
        P.fence(["vector", "scalar", "tensor"])
        u = ln_ctr[0]
        ln_ctr[0] += 1
        ln = ExitStack()
        xlnb = ln.enter_context(nc.sbuf_tensor(f"xlnb{u}", [128, D], BF16))
        bnst = ln.enter_context(nc.sbuf_tensor(f"bnst{u}", [128, 2 * 64], F32))
        pt = [ln.enter_context(nc.psum_tensor(f"ptln{u}_{i}", [128, 128], BF16))
              for i in range(6)]
        t_base = Tln.n
        e_base = Eln.n
        for tt in range(NT):
            sp = tt % 2
            c0 = sp * 8
            for i in range(7):
                P.op("vector", lambda v, tt=tt, i=i, sp=sp: v.bn_stats(
                    bnst[:, sp * 64 + i * 6:sp * 64 + (i + 1) * 6],
                    x_ap(tt, i * 512, 512)))
            P.opw("vector", lambda v, tt=tt, sp=sp: v.bn_stats(
                bnst[:, sp * 64 + 42:sp * 64 + 48], x_ap(tt, 7 * 512, 512)))
            P.opw("vector", lambda v, sp=sp, c0=c0: v.bn_aggr(
                stt_sb[:, c0:c0 + 2], bnst[:, sp * 64:sp * 64 + 48]))
            P.op("vector", lambda v, c0=c0: v.tensor_scalar(
                stt_sb[:, c0 + 1:c0 + 2], stt_sb[:, c0 + 1:c0 + 2], 1e-5, None,
                AL.add), Vln)
            P.wait("scalar", Vln, Vln.n)
            P.op("scalar", lambda s, c0=c0: s.activation(
                stt_sb[:, c0 + 2:c0 + 3], stt_sb[:, c0 + 1:c0 + 2], AF.Sqrt,
                0.0, 1.0), Sln)
            P.wait("vector", Sln, Sln.n)
            P.opw("vector", lambda v, c0=c0: v.reciprocal(
                stt_sb[:, c0 + 3:c0 + 4], stt_sb[:, c0 + 2:c0 + 3]))
            P.op("vector", lambda v, c0=c0: v.scalar_tensor_tensor(
                stt_sb[:, c0 + 4:c0 + 5], stt_sb[:, c0:c0 + 1], -1.0,
                stt_sb[:, c0 + 3:c0 + 4], AL.mult, AL.mult), Vln)
            P.wait("scalar", Vln, Vln.n)
            if tt >= 1:
                P.wait("scalar", Tln, t_base + tt * KC)
            P.op("scalar", lambda s, tt=tt, c0=c0: s.activation(
                xlnb[:, :], x_ap(tt, 0, D), AF.Identity,
                stt_sb[:, c0 + 4:c0 + 5], stt_sb[:, c0 + 3:c0 + 4]), Sln)
            P.wait("tensor", Sln, Sln.n)
            for k in range(KC):
                li = tt * KC + k
                if li >= 6:
                    P.wait("tensor", Eln, e_base + li - 5)
                pp = li % 6
                P.op("tensor", lambda tn, k=k, pp=pp: tn.transpose(
                    pt[pp][:, :], xlnb[:, k * 128:(k + 1) * 128],
                    id_sb[:, :]), Tln)
                P.wait("vector", Tln, Tln.n)
                P.op("vector", lambda v, k=k, tt=tt, pp=pp: v.tensor_copy(
                    xT_ap(k, tt * 128, 128), pt[pp][:, :]), Eln)
        ln.close()

    Pd = [E("Pd0"), E("Pd1")]
    Pc, Pmm, Pev = E("Pc"), E("Pmm"), E("Pev")
    Bld = [E("Bld0"), E("Bld1")]
    Sst = [E("Sst0"), E("Sst1")]

    prj_ctr = [0]

    def emit_proj(wsrc, w_is_f32, cast_eng, bias_src, sink, scale, stat_ap, sink_t=None,
                  extra_entry=()):
        P.fence(["sync", "gpsimd", "tensor", "vector", "scalar"])
        u = prj_ctr[0]
        prj_ctr[0] += 1
        a = ExitStack()
        if w_is_f32:
            wf = a.enter_context(nc.sbuf_tensor(f"wf{u}", [128, 2 * 512], F32))
        wb = a.enter_context(nc.sbuf_tensor(f"wb{u}", [128, 2 * 512], BF16))
        bias = a.enter_context(nc.sbuf_tensor(f"bias{u}", [128, 2 * 512], F32))
        tmp = a.enter_context(nc.sbuf_tensor(f"tmp{u}", [128, 512], F32))
        vst = a.enter_context(nc.sbuf_tensor(f"vst{u}", [128, 2 * 512], BF16))
        ps = [a.enter_context(nc.psum_tensor(f"pp{u}_{i}", [128, 512], F32))
              for i in range(NT)]
        m_base = Pmm.n
        e_base = Pev.n
        b_base = [Bld[0].n, Bld[1].n]
        for (eng, ev, val) in extra_entry:
            P.wait(eng, ev, val)
        for og in range(8):
            bpar = og % 2
            if og >= 2:
                P.wait("gpsimd", Pev, e_base + (og - 1) * NT)
            P.op("gpsimd", lambda g, og=og, bpar=bpar: g.dma_start(
                out=bias[:, bpar * 512:(bpar + 1) * 512],
                in_=bias_src[:, og * 512:(og + 1) * 512]), Bld[bpar], 16)
            for k in range(KC):
                n = og * KC + k
                par = n % 2
                if n >= 2:
                    P.wait("sync", Pc if w_is_f32 else Pmm,
                           (Pc.n - 1) if w_is_f32 else (m_base + n - 1))
                P.op("sync", lambda q, og=og, k=k, par=par, w=wsrc: q.dma_start(
                    out=(wf if w_is_f32 else wb)[:, par * 512:(par + 1) * 512],
                    in_=w[k * 128:(k + 1) * 128, og * 512:(og + 1) * 512]), Pd[par], 16)
                if w_is_f32:
                    P.wait(cast_eng, Pd[par], Pd[par].n)
                    if n >= 2:
                        P.wait(cast_eng, Pmm, m_base + n - 1)
                    if cast_eng == "scalar":
                        P.op(cast_eng, lambda e, par=par: e.copy(
                            wb[:, par * 512:(par + 1) * 512],
                            wf[:, par * 512:(par + 1) * 512]), Pc)
                    else:
                        P.op(cast_eng, lambda e, par=par: e.tensor_copy(
                            wb[:, par * 512:(par + 1) * 512],
                            wf[:, par * 512:(par + 1) * 512]), Pc)
                    P.wait("tensor", Pc, Pc.n)
                else:
                    P.wait("tensor", Pd[par], Pd[par].n)
                for tt in range(NT):
                    if og >= 1 and k == 0:
                        P.wait("tensor", Pev, e_base + (og - 1) * NT + tt + 1)
                    P.op("tensor", lambda tn, tt=tt, k=k, par=par: tn.matmul(
                        ps[tt][:, :], stat_ap(k, tt * 128, 128),
                        wb[:, par * 512:(par + 1) * 512],
                        start=(k == 0), stop=(k == KC - 1)),
                        Pmm if tt == NT - 1 else None)
            P.wait("vector", Bld[bpar], b_base[bpar] + (og // 2 + 1) * 16)
            if sink == "resid":
                for tt in range(NT):
                    P.opw("vector", lambda v, tt=tt, og=og, bpar=bpar, scale=scale:
                          v.scalar_tensor_tensor(
                              x_ap(tt, og * 512, 512),
                              bias[:, bpar * 512:(bpar + 1) * 512], scale,
                              x_ap(tt, og * 512, 512), AL.mult, AL.add))
            P.wait("vector", Pmm, m_base + (og + 1) * KC)
            for tt in range(NT):
                mtile = og * NT + tt
                if sink == "resid":
                    P.opw("vector", lambda v, tt=tt, og=og, scale=scale:
                          v.scalar_tensor_tensor(
                              x_ap(tt, og * 512, 512), ps[tt][:, :], scale,
                              x_ap(tt, og * 512, 512), AL.mult, AL.add), Pev)
                else:
                    par2 = mtile % 2
                    if mtile >= 2:
                        P.wait("vector", Sst[par2], Sst[par2].n)
                    P.op("vector", lambda v, tt=tt, bpar=bpar, par2=par2:
                         v.scalar_tensor_tensor(
                             vst[:, par2 * 512:(par2 + 1) * 512], ps[tt][:, :], 1.0,
                             bias[:, bpar * 512:(bpar + 1) * 512], AL.bypass, AL.add),
                         Pev)
                    P.wait("scalar", Pev, Pev.n)
                    b = tt // 4
                    for cb in range(4):
                        h = og * 4 + cb
                        gg = b * 32 + h
                        r0 = gg * 512 + (tt % 4) * 128
                        P.op("scalar", lambda s, r0=r0, par2=par2, cb=cb,
                             st_=sink_t: s.dma_start(
                            out=aT_in[st_][r0:r0 + 128, :],
                            in_=vst[:, par2 * 512 + cb * 128:par2 * 512 + (cb + 1) * 128]),
                            Sst[par2], 16)
        a.close()
        return dict(mm=Pmm.n, ev=Pev.n)

    P.wait("vector", s_x, s_x.n)
    emit_ln()
    ffn1 = emit_proj(wdq_all, False, None, b_bc["bf"], "resid", 0.5, xT_ap,
                     extra_entry=[("sync", Ag1, 1)])

    if STOP_AFTER == "ffn1":
        _finish(P, x_ap, y_out, ffn1["ev"])
        P.emit(en(nc.Block()))
        return nc

    emit_ln()
    AgT = [E(f"AgT{t}") for t in range(3)]

    def emit_a1_ag(t):
        P.wait("gpsimd", Sst[0], Sst[0].n)
        P.wait("gpsimd", Sst[1], Sst[1].n)
        P.op("gpsimd", lambda g, t=t: g.collective_compute(
            "AllGather", AL.bypass, replica_groups=[list(range(NCORE))],
            ins=[aT_in[t][:, :]], outs=[aT_all[t][:, :]]), AgT[t])

    emit_proj(wkT, True, "vector", b_bc["bk"], "a2a", 1.0, xT_ap, 0)
    emit_a1_ag(0)
    emit_proj(wqT, True, "vector", b_bc["bq"], "a2a", 1.0, xT_ap, 1)
    emit_a1_ag(1)
    prj_v = emit_proj(wvT, True, "scalar", b_bc["bv"], "a2a", 1.0, xT_ap, 2)
    emit_a1_ag(2)
    if STOP_AFTER == "qkv":
        _finish(P, x_ap, y_out, prj_v["ev"])
        P.emit(en(nc.Block()))
        return nc

    P.fence(ENGINES)
    at = ExitStack()
    xTt = xT_holder["t"]
    HB = 16384

    def K_ap(par, c0, n):
        return xTt[:, par * HB + c0: par * HB + c0 + n]

    def Q_ap(par, c0, n):
        return xTt[:, par * HB + 4096 + c0: par * HB + 4096 + c0 + n]

    def V_ap(par, c0, n):
        return xTt[:, par * HB + 8192 + c0: par * HB + 8192 + c0 + n]

    attT = at.enter_context(nc.sbuf_tensor("attT", [128, 3 * 512], BF16))
    rstb = at.enter_context(nc.sbuf_tensor("rstb", [128, 4096], BF16))
    ps_sc = [at.enter_context(nc.psum_tensor(f"ps_sc{i}", [128, 512], F32))
             for i in range(4)]
    ps_R = [at.enter_context(nc.psum_tensor(f"ps_R{i}", [128, 512], F32))
            for i in range(4)]

    Vini = E("Vini")
    Lks = [E("Lks0"), E("Lks1")]
    Lqs = [E("Lqs0"), E("Lqs1")]
    Rtd = [E("Rtd0"), E("Rtd1")]
    Gp1 = [E("Gp10"), E("Gp11")]
    Amm, Aex, Amk, Acv, Arv, Trv, Erv = (
        E(n) for n in ("Amm", "Aex", "Amk", "Acv", "Arv", "Trv", "Erv"))
    A2st = [E("A2st0"), E("A2st1")]
    Ag2, Rld = E("Ag2"), E("Rld")

    P.op("vector", lambda v: v.memset(V_ap(0, 0, 4128), 1.0), Vini)
    P.op("vector", lambda v: v.memset(V_ap(1, 0, 4128), 1.0), Vini)
    P.wait("gpsimd", AgT[0], 1)
    P.wait("gpsimd", AgT[1], 1)
    P.wait("gpsimd", AgT[2], 1)

    aTv = [aT_all[t].rearrange("(i r) c -> i r c", i=8) for t in range(3)]
    p1v = p1buf.rearrange("p (i r) c -> p i r c", i=8)

    def emit_pair_gather(g, p):
        for arm in nc.switch_core_id(engines=[g], n=128):
            c = arm.logical % NCORE
            gg = 8 * c + p
            for t in range(3):
                g.dma_start(out=p1v[p, :, t * 512:(t + 1) * 512, :],
                            in_=aTv[t][:, gg * 512:(gg + 1) * 512, :]
                            ).then_inc(Gp1[p % 2].h, 16)

    for p in range(8):
        if p >= 2:
            P.wait("gpsimd", Gp1[p % 2], Gp1[p % 2].n)
        P.raw("gpsimd", lambda g, p=p: emit_pair_gather(g, p))
        Gp1[p % 2].n += 3 * 16

    mm_end = {}
    lks_after = {}
    lqs_after = {}

    OLDLOAD = bool(_os.environ.get("MK_OLDLOAD"))

    def issue_pair_loads(p):
        par = p % 2
        for q in ("sync", "scalar"):
            P.wait(q, Gp1[par], (p // 2 + 1) * 48)
            P.wait(q, Vini, 2)
            if p >= 2:
                P.wait(q, Acv, mm_end[p - 2])
        for a_ in range(32):
            r0 = (a_ // 4) * 1536 + (a_ % 4) * 128
            P.op("sync", lambda s, r0=r0, a_=a_, par=par, p=p: s.dma_start(
                out=K_ap(par, a_ * 128, 128), in_=p1buf[p, r0:r0 + 128, :],
                transpose=True), Lks[par], 16)
        for a_ in range(32):
            r0 = (a_ // 4) * 1536 + 512 + (a_ % 4) * 128
            P.op("sync", lambda s, r0=r0, a_=a_, par=par, p=p: s.dma_start(
                out=Q_ap(par, a_ * 128, 128), in_=p1buf[p, r0:r0 + 128, :],
                transpose=True), Lks[par], 16)
        P.wait("gpsimd", Gp1[par], (p // 2 + 1) * 48)
        P.wait("gpsimd", Vini, 2)
        if p >= 2:
            P.wait("gpsimd", Acv, mm_end[p - 2])
        for a_ in range(32):
            r0 = (a_ // 4) * 1536 + 1024 + (a_ % 4) * 128
            P.op("gpsimd", lambda s, r0=r0, a_=a_, p=p, par=par: s.dma_start(
                out=V_ap(par, a_ * 129, 128), in_=p1buf[p, r0:r0 + 128, :]),
                Lqs[par], 16)
        lks_after[p] = Lks[par].n
        lqs_after[p] = Lqs[par].n

    NOPRE = bool(_os.environ.get("MK_NOPRE"))
    g_ch = [0]
    issue_pair_loads(0)
    if not NOPRE:
        issue_pair_loads(1)

    for p in range(8):
        par = p % 2
        if NOPRE and p > 0:
            issue_pair_loads(p)
        P.wait("tensor", Lks[par], lks_after[p])
        P.wait("tensor", Lqs[par], lqs_after[p])
        for G in QT_GROUPS:
            q0, q1 = G[0], G[-1]
            rv_pass = Arv.n
            chunks = []
            for kt in range(0, q1 + 1):
                vis0 = max(kt, q0)
                nch = (q1 - vis0 + 1 + 3) // 4
                for c in range(nch):
                    cq0 = vis0 + 4 * c
                    ncols = min(4, q1 - cq0 + 1) * 128
                    chunks.append((kt, c, cq0, ncols, q0 <= kt <= q1 and c == 0))

            def emit_scores(desc):
                kt, c, cq0, ncols, has_diag = desc
                ch = g_ch[0]
                g_ch[0] += 1
                spar = ch % 4
                apar_t = ch % 3
                if ch >= 4:
                    P.wait("tensor", Aex, ch - 3)
                if kt == 0 and c == 0:
                    P.wait("tensor", Arv, rv_pass)
                P.op("tensor", lambda tn, kt=kt, cq0=cq0, ncols=ncols, spar=spar,
                     par=par: tn.matmul(
                         ps_sc[spar][:, 0:ncols], K_ap(par, kt * 128, 128),
                         Q_ap(par, cq0 * 128, ncols),
                         start=True, stop=True), Amm)
                P.wait("scalar", Amm, Amm.n)
                if ch >= 3:
                    P.wait("scalar", Acv, ch - 2)
                P.op("scalar", lambda s, spar=spar, apar_t=apar_t, ncols=ncols:
                     s.activation(attT[:, apar_t * 512:apar_t * 512 + ncols],
                                  ps_sc[spar][:, 0:ncols], AF.Exp, 0.0, SM_SCALE),
                     Aex)
                if has_diag:
                    P.wait("vector", Aex, Aex.n)
                    P.op("vector", lambda v, apar_t=apar_t: v.tensor_mul(
                        attT[:, apar_t * 512:apar_t * 512 + 128],
                        attT[:, apar_t * 512:apar_t * 512 + 128], mask_sb[:, :]), Amk)
                return (ch, Aex.n, Amk.n)

            def emit_attv(desc, toks):
                kt, c, cq0, ncols, has_diag = desc
                ch, aex_n, amk_n = toks
                apar_t = ch % 3
                if has_diag:
                    P.wait("tensor", Amk, amk_n)
                else:
                    P.wait("tensor", Aex, aex_n)
                nq = ncols // 128
                for qi in range(nq):
                    qt = cq0 + qi
                    slot = qt - q0
                    P.op("tensor", lambda tn, qt=qt, qi=qi, kt=kt, apar_t=apar_t,
                         slot=slot, par=par: tn.matmul(
                             ps_R[slot][:, 0:129],
                             attT[:, apar_t * 512 + qi * 128:apar_t * 512 + (qi + 1) * 128],
                             V_ap(par, kt * 129, 129),
                             start=(kt == 0), stop=(kt == qt)),
                         Acv if qi == nq - 1 else None)
                if has_diag:
                    qt = kt
                    slot = qt - q0
                    P.wait("vector", Acv, Acv.n)
                    P.wait("vector", Arv, Arv.n)
                    if qt == 0 and p > 0:
                        P.wait("vector", Rtd[(p - 1) % 2], Rtd[(p - 1) % 2].n)
                    P.opw("vector", lambda v, slot=slot: v.reciprocal(
                        stt_sb[:, 8:9], ps_R[slot][:, 128:129]))
                    P.op("vector", lambda v, slot=slot, qt=qt: v.tensor_scalar(
                        rstb[:, qt * 128:(qt + 1) * 128], ps_R[slot][:, 0:128],
                        stt_sb[:, 8:9], None, AL.mult), Arv)

            SHIFT = 2
            toks = {}
            for i, desc in enumerate(chunks):
                toks[i] = emit_scores(desc)
                if i >= SHIFT:
                    emit_attv(chunks[i - SHIFT], toks[i - SHIFT])
            for i in range(max(0, len(chunks) - SHIFT), len(chunks)):
                emit_attv(chunks[i], toks[i])
        mm_end[p] = Acv.n
        if not NOPRE and p + 2 < 8:
            issue_pair_loads(p + 2)
        apar = p % 2
        P.wait("scalar", Arv, Arv.n)
        a2t = a2_inA if p < 4 else a2_inB
        pc = (p % 4) * 128
        for qt in range(32):
            P.op("scalar", lambda s, qt=qt, a2t=a2t, pc=pc: s.dma_start(
                out=a2t[qt * 128:(qt + 1) * 128, pc:pc + 128],
                in_=rstb[:, qt * 128:(qt + 1) * 128]), Rtd[apar], 16)
        if p == 3:
            P.wait("gpsimd", Rtd[0], Rtd[0].n)
            P.wait("gpsimd", Rtd[1], Rtd[1].n)
            P.op("gpsimd", lambda g: g.collective_compute(
                "AllGather", AL.bypass, replica_groups=[list(range(NCORE))],
                ins=[a2_inA[:, :]], outs=[a2_allA[:, :]]), Ag2)
        if STOP_AFTER == "pair0":
            break
    at.close()
    if STOP_AFTER in ("attn", "pair0"):
        _finish(P, x_ap, y_out, ffn1["ev"])
        P.emit(en(nc.Block()))
        return nc

    P.wait("gpsimd", Rtd[0], Rtd[0].n)
    P.wait("gpsimd", Rtd[1], Rtd[1].n)
    P.op("gpsimd", lambda g: g.collective_compute(
        "AllGather", AL.bypass, replica_groups=[list(range(NCORE))],
        ins=[a2_inB[:, :]], outs=[a2_allB[:, :]]), Ag2)
    P.wait("gpsimd", Ag2, Ag2.n)
    P.wait("gpsimd", Pmm, prj_v["mm"])
    Go = E("Go")

    def emit_o_gather(g):
        for arm in nc.switch_core_id(engines=[g], n=128):
            c = arm.logical % NCORE
            for i in range(8):
                g.dma_start(out=rs_st[i * 512:(i + 1) * 512, 0:512],
                            in_=a2_allA[i * 4096 + 512 * c:i * 4096 + 512 * (c + 1), :]
                            ).then_inc(Go.h, 16)
                g.dma_start(out=rs_st[i * 512:(i + 1) * 512, 512:1024],
                            in_=a2_allB[i * 4096 + 512 * c:i * 4096 + 512 * (c + 1), :]
                            ).then_inc(Go.h, 16)
    P.raw("gpsimd", emit_o_gather)
    Go.n += 16 * 16
    rb = ExitStack()
    rb_st = rb.enter_context(nc.sbuf_tensor("rb_st", [128, 4 * 128], BF16))
    rb_pt = [rb.enter_context(nc.psum_tensor(f"rb_pt{i}", [128, 128], BF16))
             for i in range(4)]
    Rli = [E(f"Rli{i}") for i in range(4)]
    Rtt, Rle = E("Rtt"), E("Rle")
    P.wait("sync", Go, Go.n)
    P.fence(["sync", "tensor", "vector"])
    n2 = [0]
    for b in range(2):
        for h in range(32):
            gg = b * 32 + h
            i_src, pp_ = gg // 8, gg % 8
            c0 = (pp_ % 4) * 128 + (0 if pp_ < 4 else 512)
            for j in range(4):
                n = n2[0]
                n2[0] += 1
                sl = n % 4
                if n >= 4:
                    P.wait("sync", Rtt, n - 3)
                P.op("sync", lambda s, i_src=i_src, j=j, c0=c0, sl=sl: s.dma_start(
                    out=rb_st[:, sl * 128:(sl + 1) * 128],
                    in_=rs_st[i_src * 512 + j * 128:i_src * 512 + (j + 1) * 128,
                              c0:c0 + 128]), Rli[sl], 16)
                P.wait("tensor", Rli[sl], Rli[sl].n)
                if n >= 4:
                    P.wait("tensor", Rle, n - 3)
                P.op("tensor", lambda tn, sl=sl: tn.transpose(
                    rb_pt[sl][:, :], rb_st[:, sl * 128:(sl + 1) * 128],
                    id_sb[:, :]), Rtt)
                P.wait("vector", Rtt, Rtt.n)
                P.op("vector", lambda v, h=h, b=b, j=j, sl=sl: v.tensor_copy(
                    xT_holder["t"][:, h * 1024 + b * 512 + j * 128:
                                   h * 1024 + b * 512 + (j + 1) * 128],
                    rb_pt[sl][:, :]), Rle)
    rb.close()

    oprj = emit_proj(woT, True, "vector", b_bc["bo"], "resid", 1.0, xT_ap,
                     extra_entry=[("tensor", Rle, Rle.n), ("vector", Rle, Rle.n),
                                  ("sync", Rle, Rle.n)])

    if STOP_AFTER == "oproj":
        _finish(P, x_ap, y_out, oprj["ev"])
        P.emit(en(nc.Block()))
        return nc

    emit_ln()
    ffn2 = emit_proj(wdq_all, False, None, b_bc["bf"], "resid", 0.5, xT_ap)
    _finish(P, x_ap, y_out, ffn2["ev"])
    P.emit(en(nc.Block()))
    return nc


def _finish(P, x_ap, y_out, ev_val):
    Sout = P.ev("Sout")
    Pev = P.ev("Pev")
    P.wait("scalar", Pev, ev_val)
    for tt in range(NT):
        P.op("scalar", lambda s, tt=tt: s.dma_start(
            out=y_out[tt * 128:(tt + 1) * 128, :], in_=x_ap(tt, 0, D)), Sout, 16)
    P.wait("scalar", Sout, Sout.n)


def kernel(L0_attn_x, L0_attn_Wq, L0_attn_bq, L0_attn_Wk, L0_attn_bk, L0_attn_Wv,
           L0_attn_bv, L0_attn_Wo, L0_attn_bo, L0_ffn_qweight, L0_ffn_qzeros,
           L0_ffn_scales, L0_ffn_g_idx, L0_ffn_bias):
    x = np.asarray(L0_attn_x, dtype=np.float32)
    cm = np.ascontiguousarray
    wq = cm(np.asarray(L0_attn_Wq, np.float32).T)
    wk = cm(np.asarray(L0_attn_Wk, np.float32).T)
    wv = cm(np.asarray(L0_attn_Wv, np.float32).T)
    wo = cm(np.asarray(L0_attn_Wo, np.float32).T)
    qw = np.asarray(L0_ffn_qweight, np.int32)
    qz = np.asarray(L0_ffn_qzeros, np.int32)
    sc = np.asarray(L0_ffn_scales, np.float32)
    gi = np.asarray(L0_ffn_g_idx, np.int32)
    bias_bc = {}
    for n, b in (("bq", L0_attn_bq), ("bk", L0_attn_bk), ("bv", L0_attn_bv),
                 ("bo", L0_attn_bo), ("bf", L0_ffn_bias)):
        bias_bc[n] = cm(np.broadcast_to(np.asarray(b, np.float32)[None, :], (128, D)))
    r = np.arange(128)
    mask01 = (r[:, None] <= r[None, :]).astype(np.float32)
    ident = np.eye(128, dtype=np.float32)
    shifts = cm(np.broadcast_to((4 * (r % 8)).astype(np.int32)[:, None], (128, 2048)))
    iota = np.arange(32, dtype=np.float32).reshape(32, 1)

    nc = build()
    in_maps = []
    for c in range(NCORE):
        xs = np.concatenate([x[0, 512 * c:512 * (c + 1), :],
                             x[1, 512 * c:512 * (c + 1), :]], axis=0)
        m = dict(x_in=cm(xs), wqT=wq, wkT=wk, wvT=wv, woT=wo,
                 qw_s=cm(np.repeat(qw[64 * c:64 * (c + 1), :], 8, axis=0)),
                 qz_in=qz, sc_in=sc,
                 gi_in=cm(np.broadcast_to(gi[512 * c:512 * (c + 1)][None, :],
                                          (32, 512))),
                 mask01=mask01, ident_in=ident, shifts_in=shifts, iota_in=iota)
        for n in bias_bc:
            m[n + "_bc"] = bias_bc[n]
        in_maps.append(m)
    res = run_bass_kernel_spmd(nc, in_maps, core_ids=list(range(NCORE)))
    out = np.empty((2, 4096, D), dtype=np.float32)
    for c in range(NCORE):
        yc = res.results[c]["y_out"]
        out[0, 512 * c:512 * (c + 1), :] = yc[0:512, :]
        out[1, 512 * c:512 * (c + 1), :] = yc[512:1024, :]
    return out


if __name__ == "__main__":
    import reference as R
    ins = R.setup_inputs(0)
    out = kernel(**{k: np.asarray(v) for k, v in ins.items()})
    ref = np.asarray(R.reference(**ins))
    num = float(((out - ref) ** 2).mean())
    den = float((ref ** 2).mean())
    print("resid var ratio:", num / den)


# version 15
# speedup vs baseline: 1.0524x; 1.0409x over previous
import numpy as np
import ml_dtypes
import concourse.bass as bass
import concourse.mybir as mybir
from concourse.bass_utils import run_bass_kernel_spmd
from contextlib import ExitStack

F32 = mybir.dt.float32
BF16 = mybir.dt.bfloat16
I32 = mybir.dt.int32
AF = mybir.ActivationFunctionType
AL = mybir.AluOpType
AX = mybir.AxisListType
bfdt = ml_dtypes.bfloat16

D = 4096
NT = 8
KC = 32
NCORE = 8
SM_SCALE = 1.0 / float(np.sqrt(128.0))

A1_ROWS = 64 * 3 * 512
A2_ROWS = 8 * 128

QT_GROUPS = [list(range(4 * i, 4 * i + 4)) for i in range(8)]

import os as _os
CUT = float(_os.environ.get("MK_CUT", "9"))
STOP_AFTER = ""

ENGINES = ("gpsimd", "sync", "vector", "scalar", "tensor")


class Ev:
    def __init__(self, st, nc, name):
        self.h = st.enter_context(nc.semaphore(name))
        self.n = 0


class Prog:
    def __init__(self, nc, st):
        self.nc = nc
        self.st = st
        self.streams = {e: [] for e in ENGINES}
        self.evs = {}

    def ev(self, name):
        if name not in self.evs:
            self.evs[name] = Ev(self.st, self.nc, name)
        return self.evs[name]

    def wait(self, eng, ev, val):
        if val > 0:
            self.streams[eng].append(("w", ev, val))

    def op(self, eng, fn, sig=None, amt=1):
        if sig is not None:
            sig.n += amt
        self.streams[eng].append(("o", fn, sig, amt))

    def raw(self, eng, fn):
        self.streams[eng].append(("o", fn, None, 0))

    def opw(self, eng, fn, sig=None):
        ev = sig if sig is not None else self.ev("self_" + eng)
        self.op(eng, fn, ev)
        self.wait(eng, ev, ev.n)

    FENCE_EXCLUDE = {"Ag1", "AgT0", "AgT1", "AgT2", "Ag2", "Gp10", "Gp11",
                     "Go", "Rld"}

    def fence(self, engines):
        for eng in engines:
            for name, ev in list(self.evs.items()):
                if ev.n > 0 and name not in self.FENCE_EXCLUDE:
                    self.streams[eng].append(("w", ev, ev.n))

    def emit(self, block):
        for ename in ENGINES:
            stream = self.streams[ename]
            if not stream:
                continue
            deco = getattr(block, ename)

            def make(stream):
                def body(eng):
                    for item in stream:
                        if item[0] == "w":
                            eng.wait_ge(item[1].h, item[2])
                        else:
                            ins = item[1](eng)
                            if item[2] is not None:
                                ins.then_inc(item[2].h, item[3])
                return body
            deco(make(stream))


def build():
    nc = bass.Bass("TRN2", target_bir_lowering=False)

    x_in = nc.dram_tensor("x_in", [1024, D], F32, kind="ExternalInput")
    wqT = nc.dram_tensor("wqT", [D, D], F32, kind="ExternalInput")
    wkT = nc.dram_tensor("wkT", [D, D], F32, kind="ExternalInput")
    wvT = nc.dram_tensor("wvT", [D, D], F32, kind="ExternalInput")
    woT = nc.dram_tensor("woT", [D, D], F32, kind="ExternalInput")
    qw_s = nc.dram_tensor("qw_s", [512, D], I32, kind="ExternalInput")
    qz_in = nc.dram_tensor("qz_in", [32, 512], I32, kind="ExternalInput")
    sc_in = nc.dram_tensor("sc_in", [32, D], F32, kind="ExternalInput")
    gi_in = nc.dram_tensor("gi_in", [32, 512], I32, kind="ExternalInput")
    b_bc = {n: nc.dram_tensor(n + "_bc", [128, D], F32, kind="ExternalInput")
            for n in ("bq", "bk", "bv", "bo", "bf")}
    mask01 = nc.dram_tensor("mask01", [128, 128], F32, kind="ExternalInput")
    ident_in = nc.dram_tensor("ident_in", [128, 128], F32, kind="ExternalInput")
    shifts_in = nc.dram_tensor("shifts_in", [128, 2048], I32, kind="ExternalInput")
    iota_in = nc.dram_tensor("iota_in", [32, 1], F32, kind="ExternalInput")
    y_out = nc.dram_tensor("y_out", [1024, D], F32, kind="ExternalOutput")

    wdq_in = nc.dram_tensor("wdq_in", [512, D], BF16)
    wdq_all = nc.dram_tensor("wdq_all", [D, D], BF16)
    aT_in = [nc.dram_tensor(f"a1_in{t}", [64 * 512, 128], BF16) for t in range(3)]
    aT_all = [nc.dram_tensor(f"a1_all{t}", [8 * 64 * 512, 128], BF16) for t in range(3)]
    a2_inA = nc.dram_tensor("a2_inA", [4096, 512], BF16)
    a2_inB = nc.dram_tensor("a2_inB", [4096, 512], BF16)
    a2_allA = nc.dram_tensor("a2_allA", [8 * 4096, 512], BF16)
    a2_allB = nc.dram_tensor("a2_allB", [8 * 4096, 512], BF16)
    rs_st = nc.dram_tensor("rs_st", [8 * 512, 1024], BF16)
    p1buf = nc.dram_tensor("p1buf", [8, 8 * 1536, 128], BF16)

    st = ExitStack()
    en = st.enter_context

    x_sb = en(nc.sbuf_tensor("x_sb", [128, NT * D], F32))
    mask_sb = en(nc.sbuf_tensor("mask_sb", [128, 128], BF16))
    id_sb = en(nc.sbuf_tensor("id_sb", [128, 128], BF16))
    sh_sb = None
    io_sb = en(nc.sbuf_tensor("io_sb", [32, 1], F32))
    stt_sb = en(nc.sbuf_tensor("stt_sb", [128, 16], F32))
    cst_scope = ExitStack()
    cst_f = cst_scope.enter_context(nc.sbuf_tensor("cst_f", [128, 128], F32))

    P = Prog(nc, st)
    E = P.ev

    def x_ap(tt, c0, n):
        return x_sb[:, tt * D + c0: tt * D + c0 + n]

    xT_holder = {}

    def xT_ap(k, t0, n):
        return xT_holder["t"][:, k * 1024 + t0: k * 1024 + t0 + n]

    s_x, s_c, s_cc = E("s_x"), E("s_c"), E("s_cc")
    for tt in range(NT):
        P.op("gpsimd", lambda g, tt=tt: g.dma_start(
            out=x_sb[:, tt * D:(tt + 1) * D],
            in_=x_in[tt * 128:(tt + 1) * 128, :]), s_x, 16)
    P.op("gpsimd", lambda g: g.dma_start(out=io_sb[:, :], in_=iota_in[:, :]), s_c, 16)
    P.op("gpsimd", lambda g: g.dma_start(out=cst_f[:, :], in_=mask01[:, :]), s_c, 16)
    P.wait("vector", s_c, s_c.n)
    P.op("vector", lambda v: v.tensor_copy(mask_sb[:, :], cst_f[:, :]), s_cc)
    s_ci = E("s_ci")
    P.wait("gpsimd", s_cc, s_cc.n)
    P.op("gpsimd", lambda g: g.dma_start(out=cst_f[:, :], in_=ident_in[:, :]), s_ci, 16)
    P.wait("vector", s_ci, s_ci.n)
    P.op("vector", lambda v: v.tensor_copy(id_sb[:, :], cst_f[:, :]), s_cc)

    cst_scope.close()

    dq = ExitStack()
    sh_sb = dq.enter_context(nc.sbuf_tensor("sh_sb", [128, 2048], I32))
    qwb = dq.enter_context(nc.sbuf_tensor("qwb", [128, 2048], I32))
    nibf = dq.enter_context(nc.sbuf_tensor("nibf", [128, 2048], F32))
    gi_i = dq.enter_context(nc.sbuf_tensor("gi_i", [32, 512], I32))
    gi_f = dq.enter_context(nc.sbuf_tensor("gi_f", [32, 512], F32))
    gt_b = dq.enter_context(nc.sbuf_tensor("gt_b", [32, 512], BF16))
    qz_i = dq.enter_context(nc.sbuf_tensor("qz_i", [32, 512], I32))
    z_i = dq.enter_context(nc.sbuf_tensor("z_i", [32, 2048], I32))
    z_b = dq.enter_context(nc.sbuf_tensor("z_b", [32, D], BF16))
    sc_f = dq.enter_context(nc.sbuf_tensor("sc_f", [32, 2048], F32))
    sc_b = dq.enter_context(nc.sbuf_tensor("sc_b", [32, D], BF16))
    wsub = dq.enter_context(nc.sbuf_tensor("wsub", [128, 2 * 2048], BF16))
    ps_s = dq.enter_context(nc.psum_tensor("ps_s", [128, 2048], F32))
    ps_z = dq.enter_context(nc.psum_tensor("ps_z", [128, 2048], F32))

    Lqz, Lsc, Lqw, Lgi, Vdq, Mdq, Ag1 = (E(n) for n in
        ("Lqz", "Lsc", "Lqw", "Lgi", "Vdq", "Mdq", "Ag1"))
    Wst = [E("Wst0"), E("Wst1")]

    P.wait("gpsimd", s_cc, s_cc.n)
    P.op("gpsimd", lambda g: g.dma_start(out=qz_i[:, :], in_=qz_in[:, :]), Lqz, 16)
    P.op("gpsimd", lambda g: g.dma_start(out=sh_sb[:, :], in_=shifts_in[:, :]), s_c, 16)
    P.op("gpsimd", lambda g: g.dma_start(out=gi_i[:, :], in_=gi_in[:, :]), Lgi, 16)

    P.wait("vector", Lgi, Lgi.n)
    P.wait("vector", Lqz, Lqz.n)
    P.wait("vector", s_c, s_c.n)
    P.opw("vector", lambda v: v.tensor_scalar(gi_f[:, :], gi_i[:, :], 0.0, 31.0,
                                              AL.max, AL.min), Vdq)
    P.op("vector", lambda v: v.tensor_scalar(gt_b[:, :], gi_f[:, :], io_sb[:, 0:1],
                                             None, AL.is_equal), Vdq)
    for half in range(2):
        if half > 0:
            P.wait("vector", Vdq, Vdq.n)
        for j in range(8):
            P.op("vector", lambda v, j=j, half=half: v.tensor_scalar(
                z_i[:, j:j + 2041:8],
                qz_i[:, half * 256:(half + 1) * 256], 4 * j, 15,
                AL.logical_shift_right, AL.bitwise_and), Vdq)
        P.wait("vector", Vdq, Vdq.n)
        P.op("vector", lambda v, half=half: v.tensor_scalar(
            z_b[:, half * 2048:(half + 1) * 2048], z_i[:, :], 1, None, AL.add), Vdq)
        if half > 0:
            P.wait("gpsimd", Vdq, Vdq.n)
        P.op("gpsimd", lambda g, half=half: g.dma_start(
            out=sc_f[:, :], in_=sc_in[:, half * 2048:(half + 1) * 2048]), Lsc, 16)
        P.wait("vector", Lsc, Lsc.n)
        P.op("vector", lambda v, half=half: v.tensor_copy(
            sc_b[:, half * 2048:(half + 1) * 2048], sc_f[:, :]), Vdq)
    vprep = Vdq.n

    nib_done = {}
    comb_done = {}
    for t in range(4):
        for half in range(2):
            i = t * 2 + half
            par = i % 2
            if i > 0:
                P.wait("gpsimd", Vdq, nib_done[i - 1])
            P.op("gpsimd", lambda g, t=t, half=half: g.dma_start(
                out=qwb[:, :],
                in_=qw_s[128 * t:128 * (t + 1), half * 2048:(half + 1) * 2048]),
                Lqw, 16)
            P.wait("vector", Lqw, Lqw.n)
            if t == 0 and half == 0:
                P.wait("vector", s_c, s_c.n)
            P.opw("vector", lambda v: v.tensor_tensor(
                qwb[:, :], qwb[:, :], sh_sb[:, :], AL.logical_shift_right))
            P.opw("vector", lambda v: v.tensor_scalar(
                qwb[:, :], qwb[:, :], 15, None, AL.bitwise_and))
            P.opw("vector", lambda v: v.tensor_copy(nibf[:, :], qwb[:, :]), Vdq)
            nib_done[i] = Vdq.n
            P.wait("tensor", Vdq, vprep)
            if i > 0:
                P.wait("tensor", Vdq, comb_done[i - 1])
            P.wait("tensor", Vdq, nib_done[i])
            for oc in range(4):
                P.op("tensor", lambda tn, t=t, half=half, oc=oc: tn.matmul(
                    ps_s[:, oc * 512:(oc + 1) * 512], gt_b[:, t * 128:(t + 1) * 128],
                    sc_b[:, half * 2048 + oc * 512:half * 2048 + (oc + 1) * 512],
                    start=True, stop=True))
            for oc in range(4):
                P.op("tensor", lambda tn, t=t, half=half, oc=oc: tn.matmul(
                    ps_z[:, oc * 512:(oc + 1) * 512], gt_b[:, t * 128:(t + 1) * 128],
                    z_b[:, half * 2048 + oc * 512:half * 2048 + (oc + 1) * 512],
                    start=True, stop=True), Mdq if oc == 3 else None)
            P.wait("vector", Mdq, Mdq.n)
            if i >= 2:
                P.wait("vector", Wst[par], Wst[par].n)
            P.opw("vector", lambda v, par=par: v.tensor_sub(
                wsub[:, par * 2048:(par + 1) * 2048], nibf[:, :], ps_z[:, :]))
            P.op("vector", lambda v, par=par: v.tensor_mul(
                wsub[:, par * 2048:(par + 1) * 2048],
                wsub[:, par * 2048:(par + 1) * 2048], ps_s[:, :]), Vdq)
            comb_done[i] = Vdq.n
            P.wait("scalar", Vdq, comb_done[i])
            P.op("scalar", lambda s, t=t, half=half, par=par: s.dma_start(
                out=wdq_in[t * 128:(t + 1) * 128, half * 2048:(half + 1) * 2048],
                in_=wsub[:, par * 2048:(par + 1) * 2048]), Wst[par], 16)

    P.wait("gpsimd", Wst[0], Wst[0].n)
    P.wait("gpsimd", Wst[1], Wst[1].n)
    P.op("gpsimd", lambda g: g.collective_compute(
        "AllGather", AL.bypass, replica_groups=[list(range(NCORE))],
        ins=[wdq_in[:, :]], outs=[wdq_all[:, :]]), Ag1)
    dq.close()
    xT_holder["t"] = en(nc.sbuf_tensor("xT_sb", [128, KC * 1024], BF16))

    Vln, Sln, Tln, Eln = E("Vln"), E("Sln"), E("Tln"), E("Eln")

    ln_ctr = [0]

    def emit_ln():
        P.fence(["vector", "scalar", "tensor"])
        u = ln_ctr[0]
        ln_ctr[0] += 1
        ln = ExitStack()
        xlnb = ln.enter_context(nc.sbuf_tensor(f"xlnb{u}", [128, D], BF16))
        bnst = ln.enter_context(nc.sbuf_tensor(f"bnst{u}", [128, 2 * 64], F32))
        pt = [ln.enter_context(nc.psum_tensor(f"ptln{u}_{i}", [128, 128], BF16))
              for i in range(6)]
        t_base = Tln.n
        e_base = Eln.n
        for tt in range(NT):
            sp = tt % 2
            c0 = sp * 8
            for i in range(7):
                P.op("vector", lambda v, tt=tt, i=i, sp=sp: v.bn_stats(
                    bnst[:, sp * 64 + i * 6:sp * 64 + (i + 1) * 6],
                    x_ap(tt, i * 512, 512)))
            P.opw("vector", lambda v, tt=tt, sp=sp: v.bn_stats(
                bnst[:, sp * 64 + 42:sp * 64 + 48], x_ap(tt, 7 * 512, 512)))
            P.opw("vector", lambda v, sp=sp, c0=c0: v.bn_aggr(
                stt_sb[:, c0:c0 + 2], bnst[:, sp * 64:sp * 64 + 48]))
            P.op("vector", lambda v, c0=c0: v.tensor_scalar(
                stt_sb[:, c0 + 1:c0 + 2], stt_sb[:, c0 + 1:c0 + 2], 1e-5, None,
                AL.add), Vln)
            P.wait("scalar", Vln, Vln.n)
            P.op("scalar", lambda s, c0=c0: s.activation(
                stt_sb[:, c0 + 2:c0 + 3], stt_sb[:, c0 + 1:c0 + 2], AF.Sqrt,
                0.0, 1.0), Sln)
            P.wait("vector", Sln, Sln.n)
            P.opw("vector", lambda v, c0=c0: v.reciprocal(
                stt_sb[:, c0 + 3:c0 + 4], stt_sb[:, c0 + 2:c0 + 3]))
            P.op("vector", lambda v, c0=c0: v.scalar_tensor_tensor(
                stt_sb[:, c0 + 4:c0 + 5], stt_sb[:, c0:c0 + 1], -1.0,
                stt_sb[:, c0 + 3:c0 + 4], AL.mult, AL.mult), Vln)
            P.wait("scalar", Vln, Vln.n)
            if tt >= 1:
                P.wait("scalar", Tln, t_base + tt * KC)
            P.op("scalar", lambda s, tt=tt, c0=c0: s.activation(
                xlnb[:, :], x_ap(tt, 0, D), AF.Identity,
                stt_sb[:, c0 + 4:c0 + 5], stt_sb[:, c0 + 3:c0 + 4]), Sln)
            P.wait("tensor", Sln, Sln.n)
            for k in range(KC):
                li = tt * KC + k
                if li >= 6:
                    P.wait("tensor", Eln, e_base + li - 5)
                pp = li % 6
                P.op("tensor", lambda tn, k=k, pp=pp: tn.transpose(
                    pt[pp][:, :], xlnb[:, k * 128:(k + 1) * 128],
                    id_sb[:, :]), Tln)
                P.wait("vector", Tln, Tln.n)
                P.op("vector", lambda v, k=k, tt=tt, pp=pp: v.tensor_copy(
                    xT_ap(k, tt * 128, 128), pt[pp][:, :]), Eln)
        ln.close()

    Pd = [E("Pd0"), E("Pd1"), E("Pd2"), E("Pd3")]
    Pc, Pmm, Pev = E("Pc"), E("Pmm"), E("Pev")
    Bld = [E("Bld0"), E("Bld1")]
    Sst = [E("Sst0"), E("Sst1")]

    prj_ctr = [0]

    def emit_proj(wsrc, w_is_f32, cast_eng, bias_src, sink, scale, stat_ap, sink_t=None,
                  extra_entry=()):
        P.fence(["sync", "gpsimd", "tensor", "vector", "scalar"])
        u = prj_ctr[0]
        prj_ctr[0] += 1
        a = ExitStack()
        if w_is_f32:
            wf = a.enter_context(nc.sbuf_tensor(f"wf{u}", [128, 2 * 512], F32))
        wb = a.enter_context(nc.sbuf_tensor(f"wb{u}", [128, 4 * 512], BF16))
        bias = a.enter_context(nc.sbuf_tensor(f"bias{u}", [128, 2 * 512], F32))
        vst = a.enter_context(nc.sbuf_tensor(f"vst{u}", [128, 2 * 512], BF16))
        ps = [a.enter_context(nc.psum_tensor(f"pp{u}_{i}", [128, 512], F32))
              for i in range(NT)]
        m_base = Pmm.n
        e_base = Pev.n
        b_base = [Bld[0].n, Bld[1].n]
        for (eng, ev, val) in extra_entry:
            P.wait(eng, ev, val)
        for og in range(8):
            bpar = og % 2
            if og >= 2:
                P.wait("gpsimd", Pev, e_base + (og - 1) * NT)
            P.op("gpsimd", lambda g, og=og, bpar=bpar: g.dma_start(
                out=bias[:, bpar * 512:(bpar + 1) * 512],
                in_=bias_src[:, og * 512:(og + 1) * 512]), Bld[bpar], 16)
            for k in range(KC):
                n = og * KC + k
                par = n % 4
                wpar = n % 2
                if w_is_f32:
                    if n >= 2:
                        P.wait("sync", Pc, Pc.n - 1)
                    P.op("sync", lambda q, og=og, k=k, wpar=wpar, w=wsrc: q.dma_start(
                        out=wf[:, wpar * 512:(wpar + 1) * 512],
                        in_=w[k * 128:(k + 1) * 128, og * 512:(og + 1) * 512]),
                        Pd[wpar], 16)
                    P.wait(cast_eng, Pd[wpar], Pd[wpar].n)
                    if n >= 4:
                        P.wait(cast_eng, Pmm, m_base + n - 3)
                    if cast_eng == "scalar":
                        P.op(cast_eng, lambda e, par=par, wpar=wpar: e.copy(
                            wb[:, par * 512:(par + 1) * 512],
                            wf[:, wpar * 512:(wpar + 1) * 512]), Pc)
                    else:
                        P.op(cast_eng, lambda e, par=par, wpar=wpar: e.tensor_copy(
                            wb[:, par * 512:(par + 1) * 512],
                            wf[:, wpar * 512:(wpar + 1) * 512]), Pc)
                    P.wait("tensor", Pc, Pc.n)
                else:
                    if n >= 4:
                        P.wait("sync", Pmm, m_base + n - 3)
                    P.op("sync", lambda q, og=og, k=k, par=par, w=wsrc: q.dma_start(
                        out=wb[:, par * 512:(par + 1) * 512],
                        in_=w[k * 128:(k + 1) * 128, og * 512:(og + 1) * 512]),
                        Pd[par], 16)
                    P.wait("tensor", Pd[par], Pd[par].n)
                for tt in range(NT):
                    if og >= 1 and k == 0:
                        P.wait("tensor", Pev, e_base + (og - 1) * NT + tt + 1)
                    P.op("tensor", lambda tn, tt=tt, k=k, par=par: tn.matmul(
                        ps[tt][:, :], stat_ap(k, tt * 128, 128),
                        wb[:, par * 512:(par + 1) * 512],
                        start=(k == 0), stop=(k == KC - 1)),
                        Pmm if tt == NT - 1 else None)
            P.wait("vector", Bld[bpar], b_base[bpar] + (og // 2 + 1) * 16)
            if sink == "resid":
                for tt in range(NT):
                    P.opw("vector", lambda v, tt=tt, og=og, bpar=bpar, scale=scale:
                          v.scalar_tensor_tensor(
                              x_ap(tt, og * 512, 512),
                              bias[:, bpar * 512:(bpar + 1) * 512], scale,
                              x_ap(tt, og * 512, 512), AL.mult, AL.add))
            P.wait("vector", Pmm, m_base + (og + 1) * KC)
            for tt in range(NT):
                mtile = og * NT + tt
                if sink == "resid":
                    P.opw("vector", lambda v, tt=tt, og=og, scale=scale:
                          v.scalar_tensor_tensor(
                              x_ap(tt, og * 512, 512), ps[tt][:, :], scale,
                              x_ap(tt, og * 512, 512), AL.mult, AL.add), Pev)
                else:
                    par2 = mtile % 2
                    if mtile >= 2:
                        P.wait("vector", Sst[par2], Sst[par2].n)
                    P.op("vector", lambda v, tt=tt, bpar=bpar, par2=par2:
                         v.scalar_tensor_tensor(
                             vst[:, par2 * 512:(par2 + 1) * 512], ps[tt][:, :], 1.0,
                             bias[:, bpar * 512:(bpar + 1) * 512], AL.bypass, AL.add),
                         Pev)
                    P.wait("scalar", Pev, Pev.n)
                    b = tt // 4
                    for cb in range(4):
                        h = og * 4 + cb
                        gg = b * 32 + h
                        r0 = gg * 512 + (tt % 4) * 128
                        P.op("scalar", lambda s, r0=r0, par2=par2, cb=cb,
                             st_=sink_t: s.dma_start(
                            out=aT_in[st_][r0:r0 + 128, :],
                            in_=vst[:, par2 * 512 + cb * 128:par2 * 512 + (cb + 1) * 128]),
                            Sst[par2], 16)
        a.close()
        return dict(mm=Pmm.n, ev=Pev.n)

    P.wait("vector", s_x, s_x.n)
    emit_ln()
    ffn1 = emit_proj(wdq_all, False, None, b_bc["bf"], "resid", 0.5, xT_ap,
                     extra_entry=[("sync", Ag1, 1)])

    if STOP_AFTER == "ffn1":
        _finish(P, x_ap, y_out, ffn1["ev"])
        P.emit(en(nc.Block()))
        return nc

    emit_ln()
    AgT = [E(f"AgT{t}") for t in range(3)]

    def emit_a1_ag(t):
        P.wait("gpsimd", Sst[0], Sst[0].n)
        P.wait("gpsimd", Sst[1], Sst[1].n)
        P.op("gpsimd", lambda g, t=t: g.collective_compute(
            "AllGather", AL.bypass, replica_groups=[list(range(NCORE))],
            ins=[aT_in[t][:, :]], outs=[aT_all[t][:, :]]), AgT[t])

    emit_proj(wkT, True, "vector", b_bc["bk"], "a2a", 1.0, xT_ap, 0)
    emit_a1_ag(0)
    emit_proj(wqT, True, "vector", b_bc["bq"], "a2a", 1.0, xT_ap, 1)
    emit_a1_ag(1)
    prj_v = emit_proj(wvT, True, "scalar", b_bc["bv"], "a2a", 1.0, xT_ap, 2)
    emit_a1_ag(2)
    if STOP_AFTER == "qkv":
        _finish(P, x_ap, y_out, prj_v["ev"])
        P.emit(en(nc.Block()))
        return nc

    P.fence(ENGINES)
    at = ExitStack()
    xTt = xT_holder["t"]
    HB = 16384

    def K_ap(par, c0, n):
        return xTt[:, par * HB + c0: par * HB + c0 + n]

    def Q_ap(par, c0, n):
        return xTt[:, par * HB + 4096 + c0: par * HB + 4096 + c0 + n]

    def V_ap(par, c0, n):
        return xTt[:, par * HB + 8192 + c0: par * HB + 8192 + c0 + n]

    attT = at.enter_context(nc.sbuf_tensor("attT", [128, 3 * 512], BF16))
    rstb = at.enter_context(nc.sbuf_tensor("rstb", [128, 4096], BF16))
    ps_sc = [at.enter_context(nc.psum_tensor(f"ps_sc{i}", [128, 512], F32))
             for i in range(4)]
    ps_R = [at.enter_context(nc.psum_tensor(f"ps_R{i}", [128, 512], F32))
            for i in range(4)]

    Vini = E("Vini")
    Lks = [E("Lks0"), E("Lks1")]
    Lqs = [E("Lqs0"), E("Lqs1")]
    Rtd = [E("Rtd0"), E("Rtd1")]
    Gp1 = [E("Gp10"), E("Gp11")]
    Amm, Aex, Amk, Acv, Arv, Trv, Erv = (
        E(n) for n in ("Amm", "Aex", "Amk", "Acv", "Arv", "Trv", "Erv"))
    A2st = [E("A2st0"), E("A2st1")]
    Ag2, Rld = E("Ag2"), E("Rld")

    P.op("vector", lambda v: v.memset(V_ap(0, 0, 4128), 1.0), Vini)
    P.op("vector", lambda v: v.memset(V_ap(1, 0, 4128), 1.0), Vini)
    P.wait("gpsimd", AgT[0], 1)
    P.wait("gpsimd", AgT[1], 1)
    P.wait("gpsimd", AgT[2], 1)

    aTv = [aT_all[t].rearrange("(i r) c -> i r c", i=8) for t in range(3)]
    p1v = p1buf.rearrange("p (i r) c -> p i r c", i=8)

    def emit_pair_gather(g, p):
        for arm in nc.switch_core_id(engines=[g], n=128):
            c = arm.logical % NCORE
            gg = 8 * c + p
            for t in range(3):
                g.dma_start(out=p1v[p, :, t * 512:(t + 1) * 512, :],
                            in_=aTv[t][:, gg * 512:(gg + 1) * 512, :]
                            ).then_inc(Gp1[p % 2].h, 16)

    for p in range(8):
        if p >= 2:
            P.wait("gpsimd", Gp1[p % 2], Gp1[p % 2].n)
        P.raw("gpsimd", lambda g, p=p: emit_pair_gather(g, p))
        Gp1[p % 2].n += 3 * 16

    mm_end = {}
    lks_after = {}
    lqs_after = {}

    OLDLOAD = bool(_os.environ.get("MK_OLDLOAD"))

    def issue_pair_loads(p):
        par = p % 2
        for q in ("sync", "scalar"):
            P.wait(q, Gp1[par], (p // 2 + 1) * 48)
            P.wait(q, Vini, 2)
            if p >= 2:
                P.wait(q, Acv, mm_end[p - 2])
        for a_ in range(32):
            r0 = (a_ // 4) * 1536 + (a_ % 4) * 128
            P.op("sync", lambda s, r0=r0, a_=a_, par=par, p=p: s.dma_start(
                out=K_ap(par, a_ * 128, 128), in_=p1buf[p, r0:r0 + 128, :],
                transpose=True), Lks[par], 16)
        for a_ in range(32):
            r0 = (a_ // 4) * 1536 + 512 + (a_ % 4) * 128
            P.op("sync", lambda s, r0=r0, a_=a_, par=par, p=p: s.dma_start(
                out=Q_ap(par, a_ * 128, 128), in_=p1buf[p, r0:r0 + 128, :],
                transpose=True), Lks[par], 16)
        P.wait("gpsimd", Gp1[par], (p // 2 + 1) * 48)
        P.wait("gpsimd", Vini, 2)
        if p >= 2:
            P.wait("gpsimd", Acv, mm_end[p - 2])
        for a_ in range(32):
            r0 = (a_ // 4) * 1536 + 1024 + (a_ % 4) * 128
            P.op("gpsimd", lambda s, r0=r0, a_=a_, p=p, par=par: s.dma_start(
                out=V_ap(par, a_ * 129, 128), in_=p1buf[p, r0:r0 + 128, :]),
                Lqs[par], 16)
        lks_after[p] = Lks[par].n
        lqs_after[p] = Lqs[par].n

    NOPRE = bool(_os.environ.get("MK_NOPRE"))
    g_ch = [0]
    issue_pair_loads(0)
    if not NOPRE:
        issue_pair_loads(1)

    for p in range(8):
        par = p % 2
        if NOPRE and p > 0:
            issue_pair_loads(p)
        P.wait("tensor", Lks[par], lks_after[p])
        P.wait("tensor", Lqs[par], lqs_after[p])
        for G in QT_GROUPS:
            q0, q1 = G[0], G[-1]
            rv_pass = Arv.n
            chunks = []
            for kt in range(0, q1 + 1):
                vis0 = max(kt, q0)
                nch = (q1 - vis0 + 1 + 3) // 4
                for c in range(nch):
                    cq0 = vis0 + 4 * c
                    ncols = min(4, q1 - cq0 + 1) * 128
                    chunks.append((kt, c, cq0, ncols, q0 <= kt <= q1 and c == 0))

            def emit_scores(desc):
                kt, c, cq0, ncols, has_diag = desc
                ch = g_ch[0]
                g_ch[0] += 1
                spar = ch % 4
                apar_t = ch % 3
                if ch >= 4:
                    P.wait("tensor", Aex, ch - 3)
                if kt == 0 and c == 0:
                    P.wait("tensor", Arv, rv_pass)
                P.op("tensor", lambda tn, kt=kt, cq0=cq0, ncols=ncols, spar=spar,
                     par=par: tn.matmul(
                         ps_sc[spar][:, 0:ncols], K_ap(par, kt * 128, 128),
                         Q_ap(par, cq0 * 128, ncols),
                         start=True, stop=True), Amm)
                P.wait("scalar", Amm, Amm.n)
                if ch >= 3:
                    P.wait("scalar", Acv, ch - 2)
                P.op("scalar", lambda s, spar=spar, apar_t=apar_t, ncols=ncols:
                     s.activation(attT[:, apar_t * 512:apar_t * 512 + ncols],
                                  ps_sc[spar][:, 0:ncols], AF.Exp, 0.0, SM_SCALE),
                     Aex)
                if has_diag:
                    P.wait("vector", Aex, Aex.n)
                    P.op("vector", lambda v, apar_t=apar_t: v.tensor_mul(
                        attT[:, apar_t * 512:apar_t * 512 + 128],
                        attT[:, apar_t * 512:apar_t * 512 + 128], mask_sb[:, :]), Amk)
                return (ch, Aex.n, Amk.n)

            def emit_attv(desc, toks):
                kt, c, cq0, ncols, has_diag = desc
                ch, aex_n, amk_n = toks
                apar_t = ch % 3
                if has_diag:
                    P.wait("tensor", Amk, amk_n)
                else:
                    P.wait("tensor", Aex, aex_n)
                nq = ncols // 128
                for qi in range(nq):
                    qt = cq0 + qi
                    slot = qt - q0
                    P.op("tensor", lambda tn, qt=qt, qi=qi, kt=kt, apar_t=apar_t,
                         slot=slot, par=par: tn.matmul(
                             ps_R[slot][:, 0:129],
                             attT[:, apar_t * 512 + qi * 128:apar_t * 512 + (qi + 1) * 128],
                             V_ap(par, kt * 129, 129),
                             start=(kt == 0), stop=(kt == qt)),
                         Acv if qi == nq - 1 else None)
                if has_diag:
                    qt = kt
                    slot = qt - q0
                    P.wait("vector", Acv, Acv.n)
                    P.wait("vector", Arv, Arv.n)
                    if qt == 0 and p > 0:
                        P.wait("vector", Rtd[(p - 1) % 2], Rtd[(p - 1) % 2].n)
                    P.opw("vector", lambda v, slot=slot: v.reciprocal(
                        stt_sb[:, 8:9], ps_R[slot][:, 128:129]))
                    P.op("vector", lambda v, slot=slot, qt=qt: v.tensor_scalar(
                        rstb[:, qt * 128:(qt + 1) * 128], ps_R[slot][:, 0:128],
                        stt_sb[:, 8:9], None, AL.mult), Arv)

            SHIFT = 2
            toks = {}
            for i, desc in enumerate(chunks):
                toks[i] = emit_scores(desc)
                if i >= SHIFT:
                    emit_attv(chunks[i - SHIFT], toks[i - SHIFT])
            for i in range(max(0, len(chunks) - SHIFT), len(chunks)):
                emit_attv(chunks[i], toks[i])
        mm_end[p] = Acv.n
        if not NOPRE and p + 2 < 8:
            issue_pair_loads(p + 2)
        apar = p % 2
        P.wait("scalar", Arv, Arv.n)
        a2t = a2_inA if p < 4 else a2_inB
        pc = (p % 4) * 128
        for qt in range(32):
            P.op("scalar", lambda s, qt=qt, a2t=a2t, pc=pc: s.dma_start(
                out=a2t[qt * 128:(qt + 1) * 128, pc:pc + 128],
                in_=rstb[:, qt * 128:(qt + 1) * 128]), Rtd[apar], 16)
        if p == 3:
            P.wait("gpsimd", Rtd[0], Rtd[0].n)
            P.wait("gpsimd", Rtd[1], Rtd[1].n)
            P.op("gpsimd", lambda g: g.collective_compute(
                "AllGather", AL.bypass, replica_groups=[list(range(NCORE))],
                ins=[a2_inA[:, :]], outs=[a2_allA[:, :]]), Ag2)
        if STOP_AFTER == "pair0":
            break
    at.close()
    if STOP_AFTER in ("attn", "pair0"):
        _finish(P, x_ap, y_out, ffn1["ev"])
        P.emit(en(nc.Block()))
        return nc

    P.wait("gpsimd", Rtd[0], Rtd[0].n)
    P.wait("gpsimd", Rtd[1], Rtd[1].n)
    P.op("gpsimd", lambda g: g.collective_compute(
        "AllGather", AL.bypass, replica_groups=[list(range(NCORE))],
        ins=[a2_inB[:, :]], outs=[a2_allB[:, :]]), Ag2)
    P.wait("gpsimd", Ag2, Ag2.n)
    P.wait("gpsimd", Pmm, prj_v["mm"])
    Go = E("Go")

    def emit_o_gather(g):
        for arm in nc.switch_core_id(engines=[g], n=128):
            c = arm.logical % NCORE
            for i in range(8):
                g.dma_start(out=rs_st[i * 512:(i + 1) * 512, 0:512],
                            in_=a2_allA[i * 4096 + 512 * c:i * 4096 + 512 * (c + 1), :]
                            ).then_inc(Go.h, 16)
                g.dma_start(out=rs_st[i * 512:(i + 1) * 512, 512:1024],
                            in_=a2_allB[i * 4096 + 512 * c:i * 4096 + 512 * (c + 1), :]
                            ).then_inc(Go.h, 16)
    P.raw("gpsimd", emit_o_gather)
    Go.n += 16 * 16
    rb = ExitStack()
    rb_st = rb.enter_context(nc.sbuf_tensor("rb_st", [128, 4 * 128], BF16))
    rb_pt = [rb.enter_context(nc.psum_tensor(f"rb_pt{i}", [128, 128], BF16))
             for i in range(4)]
    Rli = [E(f"Rli{i}") for i in range(4)]
    Rtt, Rle = E("Rtt"), E("Rle")
    P.wait("sync", Go, Go.n)
    P.fence(["sync", "tensor", "vector"])
    n2 = [0]
    for b in range(2):
        for h in range(32):
            gg = b * 32 + h
            i_src, pp_ = gg // 8, gg % 8
            c0 = (pp_ % 4) * 128 + (0 if pp_ < 4 else 512)
            for j in range(4):
                n = n2[0]
                n2[0] += 1
                sl = n % 4
                if n >= 4:
                    P.wait("sync", Rtt, n - 3)
                P.op("sync", lambda s, i_src=i_src, j=j, c0=c0, sl=sl: s.dma_start(
                    out=rb_st[:, sl * 128:(sl + 1) * 128],
                    in_=rs_st[i_src * 512 + j * 128:i_src * 512 + (j + 1) * 128,
                              c0:c0 + 128]), Rli[sl], 16)
                P.wait("tensor", Rli[sl], Rli[sl].n)
                if n >= 4:
                    P.wait("tensor", Rle, n - 3)
                P.op("tensor", lambda tn, sl=sl: tn.transpose(
                    rb_pt[sl][:, :], rb_st[:, sl * 128:(sl + 1) * 128],
                    id_sb[:, :]), Rtt)
                P.wait("vector", Rtt, Rtt.n)
                P.op("vector", lambda v, h=h, b=b, j=j, sl=sl: v.tensor_copy(
                    xT_holder["t"][:, h * 1024 + b * 512 + j * 128:
                                   h * 1024 + b * 512 + (j + 1) * 128],
                    rb_pt[sl][:, :]), Rle)
    rb.close()

    oprj = emit_proj(woT, True, "vector", b_bc["bo"], "resid", 1.0, xT_ap,
                     extra_entry=[("tensor", Rle, Rle.n), ("vector", Rle, Rle.n),
                                  ("sync", Rle, Rle.n)])

    if STOP_AFTER == "oproj":
        _finish(P, x_ap, y_out, oprj["ev"])
        P.emit(en(nc.Block()))
        return nc

    emit_ln()
    ffn2 = emit_proj(wdq_all, False, None, b_bc["bf"], "resid", 0.5, xT_ap)
    _finish(P, x_ap, y_out, ffn2["ev"])
    P.emit(en(nc.Block()))
    return nc


def _finish(P, x_ap, y_out, ev_val):
    Sout = P.ev("Sout")
    Pev = P.ev("Pev")
    P.wait("scalar", Pev, ev_val)
    for tt in range(NT):
        P.op("scalar", lambda s, tt=tt: s.dma_start(
            out=y_out[tt * 128:(tt + 1) * 128, :], in_=x_ap(tt, 0, D)), Sout, 16)
    P.wait("scalar", Sout, Sout.n)


def kernel(L0_attn_x, L0_attn_Wq, L0_attn_bq, L0_attn_Wk, L0_attn_bk, L0_attn_Wv,
           L0_attn_bv, L0_attn_Wo, L0_attn_bo, L0_ffn_qweight, L0_ffn_qzeros,
           L0_ffn_scales, L0_ffn_g_idx, L0_ffn_bias):
    x = np.asarray(L0_attn_x, dtype=np.float32)
    cm = np.ascontiguousarray
    wq = cm(np.asarray(L0_attn_Wq, np.float32).T)
    wk = cm(np.asarray(L0_attn_Wk, np.float32).T)
    wv = cm(np.asarray(L0_attn_Wv, np.float32).T)
    wo = cm(np.asarray(L0_attn_Wo, np.float32).T)
    qw = np.asarray(L0_ffn_qweight, np.int32)
    qz = np.asarray(L0_ffn_qzeros, np.int32)
    sc = np.asarray(L0_ffn_scales, np.float32)
    gi = np.asarray(L0_ffn_g_idx, np.int32)
    bias_bc = {}
    for n, b in (("bq", L0_attn_bq), ("bk", L0_attn_bk), ("bv", L0_attn_bv),
                 ("bo", L0_attn_bo), ("bf", L0_ffn_bias)):
        bias_bc[n] = cm(np.broadcast_to(np.asarray(b, np.float32)[None, :], (128, D)))
    r = np.arange(128)
    mask01 = (r[:, None] <= r[None, :]).astype(np.float32)
    ident = np.eye(128, dtype=np.float32)
    shifts = cm(np.broadcast_to((4 * (r % 8)).astype(np.int32)[:, None], (128, 2048)))
    iota = np.arange(32, dtype=np.float32).reshape(32, 1)

    nc = build()
    in_maps = []
    for c in range(NCORE):
        xs = np.concatenate([x[0, 512 * c:512 * (c + 1), :],
                             x[1, 512 * c:512 * (c + 1), :]], axis=0)
        m = dict(x_in=cm(xs), wqT=wq, wkT=wk, wvT=wv, woT=wo,
                 qw_s=cm(np.repeat(qw[64 * c:64 * (c + 1), :], 8, axis=0)),
                 qz_in=qz, sc_in=sc,
                 gi_in=cm(np.broadcast_to(gi[512 * c:512 * (c + 1)][None, :],
                                          (32, 512))),
                 mask01=mask01, ident_in=ident, shifts_in=shifts, iota_in=iota)
        for n in bias_bc:
            m[n + "_bc"] = bias_bc[n]
        in_maps.append(m)
    res = run_bass_kernel_spmd(nc, in_maps, core_ids=list(range(NCORE)))
    out = np.empty((2, 4096, D), dtype=np.float32)
    for c in range(NCORE):
        yc = res.results[c]["y_out"]
        out[0, 512 * c:512 * (c + 1), :] = yc[0:512, :]
        out[1, 512 * c:512 * (c + 1), :] = yc[512:1024, :]
    return out


if __name__ == "__main__":
    import reference as R
    ins = R.setup_inputs(0)
    out = kernel(**{k: np.asarray(v) for k, v in ins.items()})
    ref = np.asarray(R.reference(**ins))
    num = float(((out - ref) ** 2).mean())
    den = float((ref ** 2).mean())
    print("resid var ratio:", num / den)


# version 16 (final)
# speedup vs baseline: 1.0553x; 1.0028x over previous
import numpy as np
import ml_dtypes
import concourse.bass as bass
import concourse.mybir as mybir
from concourse.bass_utils import run_bass_kernel_spmd
from contextlib import ExitStack

F32 = mybir.dt.float32
BF16 = mybir.dt.bfloat16
I32 = mybir.dt.int32
AF = mybir.ActivationFunctionType
AL = mybir.AluOpType
AX = mybir.AxisListType
bfdt = ml_dtypes.bfloat16

D = 4096
NT = 8
KC = 32
NCORE = 8
SM_SCALE = 1.0 / float(np.sqrt(128.0))

A1_ROWS = 64 * 3 * 512
A2_ROWS = 8 * 128

QT_GROUPS = [list(range(4 * i, 4 * i + 4)) for i in range(8)]

import os as _os
CUT = float(_os.environ.get("MK_CUT", "9"))
STOP_AFTER = ""

ENGINES = ("gpsimd", "sync", "vector", "scalar", "tensor")


class Ev:
    def __init__(self, st, nc, name):
        self.h = st.enter_context(nc.semaphore(name))
        self.n = 0


class Prog:
    def __init__(self, nc, st):
        self.nc = nc
        self.st = st
        self.streams = {e: [] for e in ENGINES}
        self.evs = {}

    def ev(self, name):
        if name not in self.evs:
            self.evs[name] = Ev(self.st, self.nc, name)
        return self.evs[name]

    def wait(self, eng, ev, val):
        if val > 0:
            self.streams[eng].append(("w", ev, val))

    def op(self, eng, fn, sig=None, amt=1):
        if sig is not None:
            sig.n += amt
        self.streams[eng].append(("o", fn, sig, amt))

    def raw(self, eng, fn):
        self.streams[eng].append(("o", fn, None, 0))

    def opw(self, eng, fn, sig=None):
        ev = sig if sig is not None else self.ev("self_" + eng)
        self.op(eng, fn, ev)
        self.wait(eng, ev, ev.n)

    FENCE_EXCLUDE = {"Ag1", "AgT0", "AgT1", "AgT2", "Ag2", "Gp10", "Gp11",
                     "Go", "Rld"}

    def fence(self, engines):
        for eng in engines:
            for name, ev in list(self.evs.items()):
                if ev.n > 0 and name not in self.FENCE_EXCLUDE:
                    self.streams[eng].append(("w", ev, ev.n))

    def emit(self, block):
        for ename in ENGINES:
            stream = self.streams[ename]
            if not stream:
                continue
            deco = getattr(block, ename)

            def make(stream):
                def body(eng):
                    for item in stream:
                        if item[0] == "w":
                            eng.wait_ge(item[1].h, item[2])
                        else:
                            ins = item[1](eng)
                            if item[2] is not None:
                                ins.then_inc(item[2].h, item[3])
                return body
            deco(make(stream))


def build():
    nc = bass.Bass("TRN2", target_bir_lowering=False)

    x_in = nc.dram_tensor("x_in", [1024, D], F32, kind="ExternalInput")
    wqT = nc.dram_tensor("wqT", [D, D], F32, kind="ExternalInput")
    wkT = nc.dram_tensor("wkT", [D, D], F32, kind="ExternalInput")
    wvT = nc.dram_tensor("wvT", [D, D], F32, kind="ExternalInput")
    woT = nc.dram_tensor("woT", [D, D], F32, kind="ExternalInput")
    qw_s = nc.dram_tensor("qw_s", [512, D], I32, kind="ExternalInput")
    qz_in = nc.dram_tensor("qz_in", [32, 512], I32, kind="ExternalInput")
    sc_in = nc.dram_tensor("sc_in", [32, D], F32, kind="ExternalInput")
    gi_in = nc.dram_tensor("gi_in", [32, 512], I32, kind="ExternalInput")
    b_bc = {n: nc.dram_tensor(n + "_bc", [128, D], F32, kind="ExternalInput")
            for n in ("bq", "bk", "bv", "bo", "bf")}
    mask01 = nc.dram_tensor("mask01", [128, 128], F32, kind="ExternalInput")
    ident_in = nc.dram_tensor("ident_in", [128, 128], F32, kind="ExternalInput")
    shifts_in = nc.dram_tensor("shifts_in", [128, 2048], I32, kind="ExternalInput")
    iota_in = nc.dram_tensor("iota_in", [32, 1], F32, kind="ExternalInput")
    y_out = nc.dram_tensor("y_out", [1024, D], F32, kind="ExternalOutput")

    wdq_in = nc.dram_tensor("wdq_in", [512, D], BF16)
    wdq_all = nc.dram_tensor("wdq_all", [D, D], BF16)
    aT_in = [nc.dram_tensor(f"a1_in{t}", [64 * 512, 128], BF16) for t in range(3)]
    aT_all = [nc.dram_tensor(f"a1_all{t}", [8 * 64 * 512, 128], BF16) for t in range(3)]
    a2_inA = nc.dram_tensor("a2_inA", [4096, 512], BF16)
    a2_inB = nc.dram_tensor("a2_inB", [4096, 512], BF16)
    a2_allA = nc.dram_tensor("a2_allA", [8 * 4096, 512], BF16)
    a2_allB = nc.dram_tensor("a2_allB", [8 * 4096, 512], BF16)
    rs_st = nc.dram_tensor("rs_st", [8 * 512, 1024], BF16)
    p1buf = nc.dram_tensor("p1buf", [8, 8 * 1536, 128], BF16)

    st = ExitStack()
    en = st.enter_context

    x_sb = en(nc.sbuf_tensor("x_sb", [128, NT * D], F32))
    mask_sb = en(nc.sbuf_tensor("mask_sb", [128, 128], BF16))
    id_sb = en(nc.sbuf_tensor("id_sb", [128, 128], BF16))
    sh_sb = None
    io_sb = en(nc.sbuf_tensor("io_sb", [32, 1], F32))
    stt_sb = en(nc.sbuf_tensor("stt_sb", [128, 16], F32))
    cst_scope = ExitStack()
    cst_f = cst_scope.enter_context(nc.sbuf_tensor("cst_f", [128, 128], F32))

    P = Prog(nc, st)
    E = P.ev

    def x_ap(tt, c0, n):
        return x_sb[:, tt * D + c0: tt * D + c0 + n]

    xT_holder = {}

    def xT_ap(k, t0, n):
        return xT_holder["t"][:, k * 1024 + t0: k * 1024 + t0 + n]

    s_x, s_c, s_cc = E("s_x"), E("s_c"), E("s_cc")
    for tt in range(NT):
        P.op("gpsimd", lambda g, tt=tt: g.dma_start(
            out=x_sb[:, tt * D:(tt + 1) * D],
            in_=x_in[tt * 128:(tt + 1) * 128, :]), s_x, 16)
    P.op("gpsimd", lambda g: g.dma_start(out=io_sb[:, :], in_=iota_in[:, :]), s_c, 16)
    P.op("gpsimd", lambda g: g.dma_start(out=cst_f[:, :], in_=mask01[:, :]), s_c, 16)
    P.wait("vector", s_c, s_c.n)
    P.op("vector", lambda v: v.tensor_copy(mask_sb[:, :], cst_f[:, :]), s_cc)
    s_ci = E("s_ci")
    P.wait("gpsimd", s_cc, s_cc.n)
    P.op("gpsimd", lambda g: g.dma_start(out=cst_f[:, :], in_=ident_in[:, :]), s_ci, 16)
    P.wait("vector", s_ci, s_ci.n)
    P.op("vector", lambda v: v.tensor_copy(id_sb[:, :], cst_f[:, :]), s_cc)

    cst_scope.close()

    dq = ExitStack()
    sh_sb = dq.enter_context(nc.sbuf_tensor("sh_sb", [128, 2048], I32))
    qwb = dq.enter_context(nc.sbuf_tensor("qwb", [128, 2048], I32))
    nibf = dq.enter_context(nc.sbuf_tensor("nibf", [128, 2048], F32))
    gi_i = dq.enter_context(nc.sbuf_tensor("gi_i", [32, 512], I32))
    gi_f = dq.enter_context(nc.sbuf_tensor("gi_f", [32, 512], F32))
    gt_b = dq.enter_context(nc.sbuf_tensor("gt_b", [32, 512], BF16))
    qz_i = dq.enter_context(nc.sbuf_tensor("qz_i", [32, 512], I32))
    z_i = dq.enter_context(nc.sbuf_tensor("z_i", [32, 2048], I32))
    z_b = dq.enter_context(nc.sbuf_tensor("z_b", [32, D], BF16))
    sc_f = dq.enter_context(nc.sbuf_tensor("sc_f", [32, 2048], F32))
    sc_b = dq.enter_context(nc.sbuf_tensor("sc_b", [32, D], BF16))
    wsub = dq.enter_context(nc.sbuf_tensor("wsub", [128, 2 * 2048], BF16))
    ps_s = dq.enter_context(nc.psum_tensor("ps_s", [128, 2048], F32))
    ps_z = dq.enter_context(nc.psum_tensor("ps_z", [128, 2048], F32))

    Lqz, Lsc, Lqw, Lgi, Vdq, Mdq, Ag1 = (E(n) for n in
        ("Lqz", "Lsc", "Lqw", "Lgi", "Vdq", "Mdq", "Ag1"))
    Wst = [E("Wst0"), E("Wst1")]

    P.wait("gpsimd", s_cc, s_cc.n)
    P.op("gpsimd", lambda g: g.dma_start(out=qz_i[:, :], in_=qz_in[:, :]), Lqz, 16)
    P.op("gpsimd", lambda g: g.dma_start(out=sh_sb[:, :], in_=shifts_in[:, :]), s_c, 16)
    P.op("gpsimd", lambda g: g.dma_start(out=gi_i[:, :], in_=gi_in[:, :]), Lgi, 16)

    P.wait("vector", Lgi, Lgi.n)
    P.wait("vector", Lqz, Lqz.n)
    P.wait("vector", s_c, s_c.n)
    P.opw("vector", lambda v: v.tensor_scalar(gi_f[:, :], gi_i[:, :], 0.0, 31.0,
                                              AL.max, AL.min), Vdq)
    P.op("vector", lambda v: v.tensor_scalar(gt_b[:, :], gi_f[:, :], io_sb[:, 0:1],
                                             None, AL.is_equal), Vdq)
    for half in range(2):
        if half > 0:
            P.wait("vector", Vdq, Vdq.n)
        for j in range(8):
            P.op("vector", lambda v, j=j, half=half: v.tensor_scalar(
                z_i[:, j:j + 2041:8],
                qz_i[:, half * 256:(half + 1) * 256], 4 * j, 15,
                AL.logical_shift_right, AL.bitwise_and), Vdq)
        P.wait("vector", Vdq, Vdq.n)
        P.op("vector", lambda v, half=half: v.tensor_scalar(
            z_b[:, half * 2048:(half + 1) * 2048], z_i[:, :], 1, None, AL.add), Vdq)
        if half > 0:
            P.wait("gpsimd", Vdq, Vdq.n)
        P.op("gpsimd", lambda g, half=half: g.dma_start(
            out=sc_f[:, :], in_=sc_in[:, half * 2048:(half + 1) * 2048]), Lsc, 16)
        P.wait("vector", Lsc, Lsc.n)
        P.op("vector", lambda v, half=half: v.tensor_copy(
            sc_b[:, half * 2048:(half + 1) * 2048], sc_f[:, :]), Vdq)
    vprep = Vdq.n

    nib_done = {}
    comb_done = {}
    for t in range(4):
        for half in range(2):
            i = t * 2 + half
            par = i % 2
            if i > 0:
                P.wait("gpsimd", Vdq, nib_done[i - 1])
            P.op("gpsimd", lambda g, t=t, half=half: g.dma_start(
                out=qwb[:, :],
                in_=qw_s[128 * t:128 * (t + 1), half * 2048:(half + 1) * 2048]),
                Lqw, 16)
            P.wait("vector", Lqw, Lqw.n)
            if t == 0 and half == 0:
                P.wait("vector", s_c, s_c.n)
            P.opw("vector", lambda v: v.tensor_tensor(
                qwb[:, :], qwb[:, :], sh_sb[:, :], AL.logical_shift_right))
            P.opw("vector", lambda v: v.tensor_scalar(
                qwb[:, :], qwb[:, :], 15, None, AL.bitwise_and))
            P.opw("vector", lambda v: v.tensor_copy(nibf[:, :], qwb[:, :]), Vdq)
            nib_done[i] = Vdq.n
            P.wait("tensor", Vdq, vprep)
            if i > 0:
                P.wait("tensor", Vdq, comb_done[i - 1])
            P.wait("tensor", Vdq, nib_done[i])
            for oc in range(4):
                P.op("tensor", lambda tn, t=t, half=half, oc=oc: tn.matmul(
                    ps_s[:, oc * 512:(oc + 1) * 512], gt_b[:, t * 128:(t + 1) * 128],
                    sc_b[:, half * 2048 + oc * 512:half * 2048 + (oc + 1) * 512],
                    start=True, stop=True))
            for oc in range(4):
                P.op("tensor", lambda tn, t=t, half=half, oc=oc: tn.matmul(
                    ps_z[:, oc * 512:(oc + 1) * 512], gt_b[:, t * 128:(t + 1) * 128],
                    z_b[:, half * 2048 + oc * 512:half * 2048 + (oc + 1) * 512],
                    start=True, stop=True), Mdq if oc == 3 else None)
            P.wait("vector", Mdq, Mdq.n)
            if i >= 2:
                P.wait("vector", Wst[par], Wst[par].n)
            P.opw("vector", lambda v, par=par: v.tensor_sub(
                wsub[:, par * 2048:(par + 1) * 2048], nibf[:, :], ps_z[:, :]))
            P.op("vector", lambda v, par=par: v.tensor_mul(
                wsub[:, par * 2048:(par + 1) * 2048],
                wsub[:, par * 2048:(par + 1) * 2048], ps_s[:, :]), Vdq)
            comb_done[i] = Vdq.n
            P.wait("scalar", Vdq, comb_done[i])
            P.op("scalar", lambda s, t=t, half=half, par=par: s.dma_start(
                out=wdq_in[t * 128:(t + 1) * 128, half * 2048:(half + 1) * 2048],
                in_=wsub[:, par * 2048:(par + 1) * 2048]), Wst[par], 16)

    P.wait("gpsimd", Wst[0], Wst[0].n)
    P.wait("gpsimd", Wst[1], Wst[1].n)
    P.op("gpsimd", lambda g: g.collective_compute(
        "AllGather", AL.bypass, replica_groups=[list(range(NCORE))],
        ins=[wdq_in[:, :]], outs=[wdq_all[:, :]]), Ag1)
    dq.close()
    xT_holder["t"] = en(nc.sbuf_tensor("xT_sb", [128, KC * 1024], BF16))

    Vln, Sln, Tln, Eln = E("Vln"), E("Sln"), E("Tln"), E("Eln")

    ln_ctr = [0]

    def emit_ln():
        P.fence(["vector", "scalar", "tensor"])
        u = ln_ctr[0]
        ln_ctr[0] += 1
        ln = ExitStack()
        xlnb = ln.enter_context(nc.sbuf_tensor(f"xlnb{u}", [128, D], BF16))
        bnst = ln.enter_context(nc.sbuf_tensor(f"bnst{u}", [128, 2 * 64], F32))
        pt = [ln.enter_context(nc.psum_tensor(f"ptln{u}_{i}", [128, 128], BF16))
              for i in range(6)]
        t_base = Tln.n
        e_base = Eln.n
        for tt in range(NT):
            sp = tt % 2
            c0 = sp * 8
            for i in range(7):
                P.op("vector", lambda v, tt=tt, i=i, sp=sp: v.bn_stats(
                    bnst[:, sp * 64 + i * 6:sp * 64 + (i + 1) * 6],
                    x_ap(tt, i * 512, 512)))
            P.opw("vector", lambda v, tt=tt, sp=sp: v.bn_stats(
                bnst[:, sp * 64 + 42:sp * 64 + 48], x_ap(tt, 7 * 512, 512)))
            P.opw("vector", lambda v, sp=sp, c0=c0: v.bn_aggr(
                stt_sb[:, c0:c0 + 2], bnst[:, sp * 64:sp * 64 + 48]))
            P.op("vector", lambda v, c0=c0: v.tensor_scalar(
                stt_sb[:, c0 + 1:c0 + 2], stt_sb[:, c0 + 1:c0 + 2], 1e-5, None,
                AL.add), Vln)
            P.wait("scalar", Vln, Vln.n)
            P.op("scalar", lambda s, c0=c0: s.activation(
                stt_sb[:, c0 + 2:c0 + 3], stt_sb[:, c0 + 1:c0 + 2], AF.Sqrt,
                0.0, 1.0), Sln)
            P.wait("vector", Sln, Sln.n)
            P.opw("vector", lambda v, c0=c0: v.reciprocal(
                stt_sb[:, c0 + 3:c0 + 4], stt_sb[:, c0 + 2:c0 + 3]))
            P.op("vector", lambda v, c0=c0: v.scalar_tensor_tensor(
                stt_sb[:, c0 + 4:c0 + 5], stt_sb[:, c0:c0 + 1], -1.0,
                stt_sb[:, c0 + 3:c0 + 4], AL.mult, AL.mult), Vln)
            P.wait("scalar", Vln, Vln.n)
            if tt >= 1:
                P.wait("scalar", Tln, t_base + tt * KC)
            P.op("scalar", lambda s, tt=tt, c0=c0: s.activation(
                xlnb[:, :], x_ap(tt, 0, D), AF.Identity,
                stt_sb[:, c0 + 4:c0 + 5], stt_sb[:, c0 + 3:c0 + 4]), Sln)
            P.wait("tensor", Sln, Sln.n)
            for k in range(KC):
                li = tt * KC + k
                if li >= 6:
                    P.wait("tensor", Eln, e_base + li - 5)
                pp = li % 6
                P.op("tensor", lambda tn, k=k, pp=pp: tn.transpose(
                    pt[pp][:, :], xlnb[:, k * 128:(k + 1) * 128],
                    id_sb[:, :]), Tln)
                P.wait("vector", Tln, Tln.n)
                P.op("vector", lambda v, k=k, tt=tt, pp=pp: v.tensor_copy(
                    xT_ap(k, tt * 128, 128), pt[pp][:, :]), Eln)
        ln.close()

    Pd = [E(f"Pd{i}") for i in range(8)]
    Pc, Pmm, Pev = E("Pc"), E("Pmm"), E("Pev")
    Bld = [E("Bld0"), E("Bld1")]
    Sst = [E("Sst0"), E("Sst1")]

    prj_ctr = [0]

    def emit_proj(wsrc, w_is_f32, cast_eng, bias_src, sink, scale, stat_ap, sink_t=None,
                  extra_entry=()):
        P.fence(["sync", "gpsimd", "tensor", "vector", "scalar"])
        u = prj_ctr[0]
        prj_ctr[0] += 1
        a = ExitStack()
        WBD = 4 if w_is_f32 else 8
        if w_is_f32:
            wf = a.enter_context(nc.sbuf_tensor(f"wf{u}", [128, 2 * 512], F32))
        wb = a.enter_context(nc.sbuf_tensor(f"wb{u}", [128, WBD * 512], BF16))
        bias = a.enter_context(nc.sbuf_tensor(f"bias{u}", [128, 2 * 512], F32))
        vst = a.enter_context(nc.sbuf_tensor(f"vst{u}", [128, 2 * 512], BF16))
        ps = [a.enter_context(nc.psum_tensor(f"pp{u}_{i}", [128, 512], F32))
              for i in range(NT)]
        m_base = Pmm.n
        e_base = Pev.n
        b_base = [Bld[0].n, Bld[1].n]
        for (eng, ev, val) in extra_entry:
            P.wait(eng, ev, val)
        for og in range(8):
            bpar = og % 2
            if og >= 2:
                P.wait("gpsimd", Pev, e_base + (og - 1) * NT)
            P.op("gpsimd", lambda g, og=og, bpar=bpar: g.dma_start(
                out=bias[:, bpar * 512:(bpar + 1) * 512],
                in_=bias_src[:, og * 512:(og + 1) * 512]), Bld[bpar], 16)
            for k in range(KC):
                n = og * KC + k
                par = n % WBD
                wpar = n % 2
                if w_is_f32:
                    if n >= 2:
                        P.wait("sync", Pc, Pc.n - 1)
                    P.op("sync", lambda q, og=og, k=k, wpar=wpar, w=wsrc: q.dma_start(
                        out=wf[:, wpar * 512:(wpar + 1) * 512],
                        in_=w[k * 128:(k + 1) * 128, og * 512:(og + 1) * 512]),
                        Pd[wpar], 16)
                    P.wait(cast_eng, Pd[wpar], Pd[wpar].n)
                    if n >= 4:
                        P.wait(cast_eng, Pmm, m_base + n - 3)
                    if cast_eng == "scalar":
                        P.op(cast_eng, lambda e, par=par, wpar=wpar: e.copy(
                            wb[:, par * 512:(par + 1) * 512],
                            wf[:, wpar * 512:(wpar + 1) * 512]), Pc)
                    else:
                        P.op(cast_eng, lambda e, par=par, wpar=wpar: e.tensor_copy(
                            wb[:, par * 512:(par + 1) * 512],
                            wf[:, wpar * 512:(wpar + 1) * 512]), Pc)
                    P.wait("tensor", Pc, Pc.n)
                else:
                    if n >= WBD:
                        P.wait("sync", Pmm, m_base + n - (WBD - 1))
                    P.op("sync", lambda q, og=og, k=k, par=par, w=wsrc: q.dma_start(
                        out=wb[:, par * 512:(par + 1) * 512],
                        in_=w[k * 128:(k + 1) * 128, og * 512:(og + 1) * 512]),
                        Pd[par], 16)
                    P.wait("tensor", Pd[par], Pd[par].n)
                for tt in range(NT):
                    if og >= 1 and k == 0:
                        P.wait("tensor", Pev, e_base + (og - 1) * NT + tt + 1)
                    P.op("tensor", lambda tn, tt=tt, k=k, par=par: tn.matmul(
                        ps[tt][:, :], stat_ap(k, tt * 128, 128),
                        wb[:, par * 512:(par + 1) * 512],
                        start=(k == 0), stop=(k == KC - 1)),
                        Pmm if tt == NT - 1 else None)
            P.wait("vector", Bld[bpar], b_base[bpar] + (og // 2 + 1) * 16)
            if sink == "resid":
                for tt in range(NT):
                    P.opw("vector", lambda v, tt=tt, og=og, bpar=bpar, scale=scale:
                          v.scalar_tensor_tensor(
                              x_ap(tt, og * 512, 512),
                              bias[:, bpar * 512:(bpar + 1) * 512], scale,
                              x_ap(tt, og * 512, 512), AL.mult, AL.add))
            P.wait("vector", Pmm, m_base + (og + 1) * KC)
            for tt in range(NT):
                mtile = og * NT + tt
                if sink == "resid":
                    P.opw("vector", lambda v, tt=tt, og=og, scale=scale:
                          v.scalar_tensor_tensor(
                              x_ap(tt, og * 512, 512), ps[tt][:, :], scale,
                              x_ap(tt, og * 512, 512), AL.mult, AL.add), Pev)
                else:
                    par2 = mtile % 2
                    if mtile >= 2:
                        P.wait("vector", Sst[par2], Sst[par2].n)
                    P.op("vector", lambda v, tt=tt, bpar=bpar, par2=par2:
                         v.scalar_tensor_tensor(
                             vst[:, par2 * 512:(par2 + 1) * 512], ps[tt][:, :], 1.0,
                             bias[:, bpar * 512:(bpar + 1) * 512], AL.bypass, AL.add),
                         Pev)
                    P.wait("scalar", Pev, Pev.n)
                    b = tt // 4
                    for cb in range(4):
                        h = og * 4 + cb
                        gg = b * 32 + h
                        r0 = gg * 512 + (tt % 4) * 128
                        P.op("scalar", lambda s, r0=r0, par2=par2, cb=cb,
                             st_=sink_t: s.dma_start(
                            out=aT_in[st_][r0:r0 + 128, :],
                            in_=vst[:, par2 * 512 + cb * 128:par2 * 512 + (cb + 1) * 128]),
                            Sst[par2], 16)
        a.close()
        return dict(mm=Pmm.n, ev=Pev.n)

    P.wait("vector", s_x, s_x.n)
    emit_ln()
    ffn1 = emit_proj(wdq_all, False, None, b_bc["bf"], "resid", 0.5, xT_ap,
                     extra_entry=[("sync", Ag1, 1)])

    if STOP_AFTER == "ffn1":
        _finish(P, x_ap, y_out, ffn1["ev"])
        P.emit(en(nc.Block()))
        return nc

    emit_ln()
    AgT = [E(f"AgT{t}") for t in range(3)]

    def emit_a1_ag(t):
        P.wait("gpsimd", Sst[0], Sst[0].n)
        P.wait("gpsimd", Sst[1], Sst[1].n)
        P.op("gpsimd", lambda g, t=t: g.collective_compute(
            "AllGather", AL.bypass, replica_groups=[list(range(NCORE))],
            ins=[aT_in[t][:, :]], outs=[aT_all[t][:, :]]), AgT[t])

    emit_proj(wkT, True, "vector", b_bc["bk"], "a2a", 1.0, xT_ap, 0)
    emit_a1_ag(0)
    emit_proj(wqT, True, "vector", b_bc["bq"], "a2a", 1.0, xT_ap, 1)
    emit_a1_ag(1)
    prj_v = emit_proj(wvT, True, "scalar", b_bc["bv"], "a2a", 1.0, xT_ap, 2)
    emit_a1_ag(2)
    if STOP_AFTER == "qkv":
        _finish(P, x_ap, y_out, prj_v["ev"])
        P.emit(en(nc.Block()))
        return nc

    P.fence(ENGINES)
    at = ExitStack()
    xTt = xT_holder["t"]
    HB = 16384

    def K_ap(par, c0, n):
        return xTt[:, par * HB + c0: par * HB + c0 + n]

    def Q_ap(par, c0, n):
        return xTt[:, par * HB + 4096 + c0: par * HB + 4096 + c0 + n]

    def V_ap(par, c0, n):
        return xTt[:, par * HB + 8192 + c0: par * HB + 8192 + c0 + n]

    attT = at.enter_context(nc.sbuf_tensor("attT", [128, 3 * 512], BF16))
    rstb = at.enter_context(nc.sbuf_tensor("rstb", [128, 4096], BF16))
    ps_sc = [at.enter_context(nc.psum_tensor(f"ps_sc{i}", [128, 512], F32))
             for i in range(4)]
    ps_R = [at.enter_context(nc.psum_tensor(f"ps_R{i}", [128, 512], F32))
            for i in range(4)]

    Vini = E("Vini")
    Lks = [E("Lks0"), E("Lks1")]
    Lqs = [E("Lqs0"), E("Lqs1")]
    Rtd = [E("Rtd0"), E("Rtd1")]
    Gp1 = [E("Gp10"), E("Gp11")]
    Amm, Aex, Amk, Acv, Arv, Trv, Erv = (
        E(n) for n in ("Amm", "Aex", "Amk", "Acv", "Arv", "Trv", "Erv"))
    A2st = [E("A2st0"), E("A2st1")]
    Ag2, Rld = E("Ag2"), E("Rld")

    P.op("vector", lambda v: v.memset(V_ap(0, 0, 4128), 1.0), Vini)
    P.op("vector", lambda v: v.memset(V_ap(1, 0, 4128), 1.0), Vini)
    P.wait("gpsimd", AgT[0], 1)
    P.wait("gpsimd", AgT[1], 1)
    P.wait("gpsimd", AgT[2], 1)

    aTv = [aT_all[t].rearrange("(i r) c -> i r c", i=8) for t in range(3)]
    p1v = p1buf.rearrange("p (i r) c -> p i r c", i=8)

    def emit_pair_gather(g, p):
        for arm in nc.switch_core_id(engines=[g], n=128):
            c = arm.logical % NCORE
            gg = 8 * c + p
            for t in range(3):
                g.dma_start(out=p1v[p, :, t * 512:(t + 1) * 512, :],
                            in_=aTv[t][:, gg * 512:(gg + 1) * 512, :]
                            ).then_inc(Gp1[p % 2].h, 16)

    for p in range(8):
        if p >= 2:
            P.wait("gpsimd", Gp1[p % 2], Gp1[p % 2].n)
        P.raw("gpsimd", lambda g, p=p: emit_pair_gather(g, p))
        Gp1[p % 2].n += 3 * 16

    mm_end = {}
    lks_after = {}
    lqs_after = {}

    OLDLOAD = bool(_os.environ.get("MK_OLDLOAD"))

    def issue_pair_loads(p):
        par = p % 2
        for q in ("sync", "scalar"):
            P.wait(q, Gp1[par], (p // 2 + 1) * 48)
            P.wait(q, Vini, 2)
            if p >= 2:
                P.wait(q, Acv, mm_end[p - 2])
        for a_ in range(32):
            r0 = (a_ // 4) * 1536 + (a_ % 4) * 128
            P.op("sync", lambda s, r0=r0, a_=a_, par=par, p=p: s.dma_start(
                out=K_ap(par, a_ * 128, 128), in_=p1buf[p, r0:r0 + 128, :],
                transpose=True), Lks[par], 16)
        for a_ in range(32):
            r0 = (a_ // 4) * 1536 + 512 + (a_ % 4) * 128
            P.op("sync", lambda s, r0=r0, a_=a_, par=par, p=p: s.dma_start(
                out=Q_ap(par, a_ * 128, 128), in_=p1buf[p, r0:r0 + 128, :],
                transpose=True), Lks[par], 16)
        P.wait("gpsimd", Gp1[par], (p // 2 + 1) * 48)
        P.wait("gpsimd", Vini, 2)
        if p >= 2:
            P.wait("gpsimd", Acv, mm_end[p - 2])
        for a_ in range(32):
            r0 = (a_ // 4) * 1536 + 1024 + (a_ % 4) * 128
            P.op("gpsimd", lambda s, r0=r0, a_=a_, p=p, par=par: s.dma_start(
                out=V_ap(par, a_ * 129, 128), in_=p1buf[p, r0:r0 + 128, :]),
                Lqs[par], 16)
        lks_after[p] = Lks[par].n
        lqs_after[p] = Lqs[par].n

    NOPRE = bool(_os.environ.get("MK_NOPRE"))
    g_ch = [0]
    issue_pair_loads(0)
    if not NOPRE:
        issue_pair_loads(1)

    for p in range(8):
        par = p % 2
        if NOPRE and p > 0:
            issue_pair_loads(p)
        P.wait("tensor", Lks[par], lks_after[p])
        P.wait("tensor", Lqs[par], lqs_after[p])
        for G in QT_GROUPS:
            q0, q1 = G[0], G[-1]
            rv_pass = Arv.n
            chunks = []
            for kt in range(0, q1 + 1):
                vis0 = max(kt, q0)
                nch = (q1 - vis0 + 1 + 3) // 4
                for c in range(nch):
                    cq0 = vis0 + 4 * c
                    ncols = min(4, q1 - cq0 + 1) * 128
                    chunks.append((kt, c, cq0, ncols, q0 <= kt <= q1 and c == 0))

            def emit_scores(desc):
                kt, c, cq0, ncols, has_diag = desc
                ch = g_ch[0]
                g_ch[0] += 1
                spar = ch % 4
                apar_t = ch % 3
                if ch >= 4:
                    P.wait("tensor", Aex, ch - 3)
                if kt == 0 and c == 0:
                    P.wait("tensor", Arv, rv_pass)
                P.op("tensor", lambda tn, kt=kt, cq0=cq0, ncols=ncols, spar=spar,
                     par=par: tn.matmul(
                         ps_sc[spar][:, 0:ncols], K_ap(par, kt * 128, 128),
                         Q_ap(par, cq0 * 128, ncols),
                         start=True, stop=True), Amm)
                P.wait("scalar", Amm, Amm.n)
                if ch >= 3:
                    P.wait("scalar", Acv, ch - 2)
                P.op("scalar", lambda s, spar=spar, apar_t=apar_t, ncols=ncols:
                     s.activation(attT[:, apar_t * 512:apar_t * 512 + ncols],
                                  ps_sc[spar][:, 0:ncols], AF.Exp, 0.0, SM_SCALE),
                     Aex)
                if has_diag:
                    P.wait("vector", Aex, Aex.n)
                    P.op("vector", lambda v, apar_t=apar_t: v.tensor_mul(
                        attT[:, apar_t * 512:apar_t * 512 + 128],
                        attT[:, apar_t * 512:apar_t * 512 + 128], mask_sb[:, :]), Amk)
                return (ch, Aex.n, Amk.n)

            def emit_attv(desc, toks):
                kt, c, cq0, ncols, has_diag = desc
                ch, aex_n, amk_n = toks
                apar_t = ch % 3
                if has_diag:
                    P.wait("tensor", Amk, amk_n)
                else:
                    P.wait("tensor", Aex, aex_n)
                nq = ncols // 128
                for qi in range(nq):
                    qt = cq0 + qi
                    slot = qt - q0
                    P.op("tensor", lambda tn, qt=qt, qi=qi, kt=kt, apar_t=apar_t,
                         slot=slot, par=par: tn.matmul(
                             ps_R[slot][:, 0:129],
                             attT[:, apar_t * 512 + qi * 128:apar_t * 512 + (qi + 1) * 128],
                             V_ap(par, kt * 129, 129),
                             start=(kt == 0), stop=(kt == qt)),
                         Acv if qi == nq - 1 else None)
                if has_diag:
                    qt = kt
                    slot = qt - q0
                    P.wait("vector", Acv, Acv.n)
                    P.wait("vector", Arv, Arv.n)
                    if qt == 0 and p > 0:
                        P.wait("vector", Rtd[(p - 1) % 2], Rtd[(p - 1) % 2].n)
                    P.opw("vector", lambda v, slot=slot: v.reciprocal(
                        stt_sb[:, 8:9], ps_R[slot][:, 128:129]))
                    P.op("vector", lambda v, slot=slot, qt=qt: v.tensor_scalar(
                        rstb[:, qt * 128:(qt + 1) * 128], ps_R[slot][:, 0:128],
                        stt_sb[:, 8:9], None, AL.mult), Arv)

            SHIFT = 2
            toks = {}
            for i, desc in enumerate(chunks):
                toks[i] = emit_scores(desc)
                if i >= SHIFT:
                    emit_attv(chunks[i - SHIFT], toks[i - SHIFT])
            for i in range(max(0, len(chunks) - SHIFT), len(chunks)):
                emit_attv(chunks[i], toks[i])
        mm_end[p] = Acv.n
        if not NOPRE and p + 2 < 8:
            issue_pair_loads(p + 2)
        apar = p % 2
        P.wait("scalar", Arv, Arv.n)
        a2t = a2_inA if p < 4 else a2_inB
        pc = (p % 4) * 128
        for qt in range(32):
            P.op("scalar", lambda s, qt=qt, a2t=a2t, pc=pc: s.dma_start(
                out=a2t[qt * 128:(qt + 1) * 128, pc:pc + 128],
                in_=rstb[:, qt * 128:(qt + 1) * 128]), Rtd[apar], 16)
        if p == 3:
            P.wait("gpsimd", Rtd[0], Rtd[0].n)
            P.wait("gpsimd", Rtd[1], Rtd[1].n)
            P.op("gpsimd", lambda g: g.collective_compute(
                "AllGather", AL.bypass, replica_groups=[list(range(NCORE))],
                ins=[a2_inA[:, :]], outs=[a2_allA[:, :]]), Ag2)
        if STOP_AFTER == "pair0":
            break
    at.close()
    if STOP_AFTER in ("attn", "pair0"):
        _finish(P, x_ap, y_out, ffn1["ev"])
        P.emit(en(nc.Block()))
        return nc

    P.wait("gpsimd", Rtd[0], Rtd[0].n)
    P.wait("gpsimd", Rtd[1], Rtd[1].n)
    P.op("gpsimd", lambda g: g.collective_compute(
        "AllGather", AL.bypass, replica_groups=[list(range(NCORE))],
        ins=[a2_inB[:, :]], outs=[a2_allB[:, :]]), Ag2)
    P.wait("gpsimd", Ag2, Ag2.n)
    P.wait("gpsimd", Pmm, prj_v["mm"])
    Go = E("Go")

    def emit_o_gather(g):
        for arm in nc.switch_core_id(engines=[g], n=128):
            c = arm.logical % NCORE
            for i in range(8):
                g.dma_start(out=rs_st[i * 512:(i + 1) * 512, 0:512],
                            in_=a2_allA[i * 4096 + 512 * c:i * 4096 + 512 * (c + 1), :]
                            ).then_inc(Go.h, 16)
                g.dma_start(out=rs_st[i * 512:(i + 1) * 512, 512:1024],
                            in_=a2_allB[i * 4096 + 512 * c:i * 4096 + 512 * (c + 1), :]
                            ).then_inc(Go.h, 16)
    P.raw("gpsimd", emit_o_gather)
    Go.n += 16 * 16
    rb = ExitStack()
    rb_st = rb.enter_context(nc.sbuf_tensor("rb_st", [128, 4 * 128], BF16))
    rb_pt = [rb.enter_context(nc.psum_tensor(f"rb_pt{i}", [128, 128], BF16))
             for i in range(4)]
    Rli = [E(f"Rli{i}") for i in range(4)]
    Rtt, Rle = E("Rtt"), E("Rle")
    P.wait("sync", Go, Go.n)
    P.fence(["sync", "tensor", "vector"])
    n2 = [0]
    for b in range(2):
        for h in range(32):
            gg = b * 32 + h
            i_src, pp_ = gg // 8, gg % 8
            c0 = (pp_ % 4) * 128 + (0 if pp_ < 4 else 512)
            for j in range(4):
                n = n2[0]
                n2[0] += 1
                sl = n % 4
                if n >= 4:
                    P.wait("sync", Rtt, n - 3)
                P.op("sync", lambda s, i_src=i_src, j=j, c0=c0, sl=sl: s.dma_start(
                    out=rb_st[:, sl * 128:(sl + 1) * 128],
                    in_=rs_st[i_src * 512 + j * 128:i_src * 512 + (j + 1) * 128,
                              c0:c0 + 128]), Rli[sl], 16)
                P.wait("tensor", Rli[sl], Rli[sl].n)
                if n >= 4:
                    P.wait("tensor", Rle, n - 3)
                P.op("tensor", lambda tn, sl=sl: tn.transpose(
                    rb_pt[sl][:, :], rb_st[:, sl * 128:(sl + 1) * 128],
                    id_sb[:, :]), Rtt)
                P.wait("vector", Rtt, Rtt.n)
                P.op("vector", lambda v, h=h, b=b, j=j, sl=sl: v.tensor_copy(
                    xT_holder["t"][:, h * 1024 + b * 512 + j * 128:
                                   h * 1024 + b * 512 + (j + 1) * 128],
                    rb_pt[sl][:, :]), Rle)
    rb.close()

    oprj = emit_proj(woT, True, "vector", b_bc["bo"], "resid", 1.0, xT_ap,
                     extra_entry=[("tensor", Rle, Rle.n), ("vector", Rle, Rle.n),
                                  ("sync", Rle, Rle.n)])

    if STOP_AFTER == "oproj":
        _finish(P, x_ap, y_out, oprj["ev"])
        P.emit(en(nc.Block()))
        return nc

    emit_ln()
    ffn2 = emit_proj(wdq_all, False, None, b_bc["bf"], "resid", 0.5, xT_ap)
    _finish(P, x_ap, y_out, ffn2["ev"])
    P.emit(en(nc.Block()))
    return nc


def _finish(P, x_ap, y_out, ev_val):
    Sout = P.ev("Sout")
    Pev = P.ev("Pev")
    P.wait("scalar", Pev, ev_val)
    for tt in range(NT):
        P.op("scalar", lambda s, tt=tt: s.dma_start(
            out=y_out[tt * 128:(tt + 1) * 128, :], in_=x_ap(tt, 0, D)), Sout, 16)
    P.wait("scalar", Sout, Sout.n)


def kernel(L0_attn_x, L0_attn_Wq, L0_attn_bq, L0_attn_Wk, L0_attn_bk, L0_attn_Wv,
           L0_attn_bv, L0_attn_Wo, L0_attn_bo, L0_ffn_qweight, L0_ffn_qzeros,
           L0_ffn_scales, L0_ffn_g_idx, L0_ffn_bias):
    x = np.asarray(L0_attn_x, dtype=np.float32)
    cm = np.ascontiguousarray
    wq = cm(np.asarray(L0_attn_Wq, np.float32).T)
    wk = cm(np.asarray(L0_attn_Wk, np.float32).T)
    wv = cm(np.asarray(L0_attn_Wv, np.float32).T)
    wo = cm(np.asarray(L0_attn_Wo, np.float32).T)
    qw = np.asarray(L0_ffn_qweight, np.int32)
    qz = np.asarray(L0_ffn_qzeros, np.int32)
    sc = np.asarray(L0_ffn_scales, np.float32)
    gi = np.asarray(L0_ffn_g_idx, np.int32)
    bias_bc = {}
    for n, b in (("bq", L0_attn_bq), ("bk", L0_attn_bk), ("bv", L0_attn_bv),
                 ("bo", L0_attn_bo), ("bf", L0_ffn_bias)):
        bias_bc[n] = cm(np.broadcast_to(np.asarray(b, np.float32)[None, :], (128, D)))
    r = np.arange(128)
    mask01 = (r[:, None] <= r[None, :]).astype(np.float32)
    ident = np.eye(128, dtype=np.float32)
    shifts = cm(np.broadcast_to((4 * (r % 8)).astype(np.int32)[:, None], (128, 2048)))
    iota = np.arange(32, dtype=np.float32).reshape(32, 1)

    nc = build()
    in_maps = []
    for c in range(NCORE):
        xs = np.concatenate([x[0, 512 * c:512 * (c + 1), :],
                             x[1, 512 * c:512 * (c + 1), :]], axis=0)
        m = dict(x_in=cm(xs), wqT=wq, wkT=wk, wvT=wv, woT=wo,
                 qw_s=cm(np.repeat(qw[64 * c:64 * (c + 1), :], 8, axis=0)),
                 qz_in=qz, sc_in=sc,
                 gi_in=cm(np.broadcast_to(gi[512 * c:512 * (c + 1)][None, :],
                                          (32, 512))),
                 mask01=mask01, ident_in=ident, shifts_in=shifts, iota_in=iota)
        for n in bias_bc:
            m[n + "_bc"] = bias_bc[n]
        in_maps.append(m)
    res = run_bass_kernel_spmd(nc, in_maps, core_ids=list(range(NCORE)))
    out = np.empty((2, 4096, D), dtype=np.float32)
    for c in range(NCORE):
        yc = res.results[c]["y_out"]
        out[0, 512 * c:512 * (c + 1), :] = yc[0:512, :]
        out[1, 512 * c:512 * (c + 1), :] = yc[512:1024, :]
    return out


if __name__ == "__main__":
    import reference as R
    ins = R.setup_inputs(0)
    out = kernel(**{k: np.asarray(v) for k, v in ins.items()})
    ref = np.asarray(R.reference(**ins))
    num = float(((out - ref) ** 2).mean())
    den = float((ref ** 2).mean())
    print("resid var ratio:", num / den)
